# version 16
# baseline (speedup 1.0000x reference)
import numpy as np
import ml_dtypes
import contextlib

import concourse.bass as bass
import concourse.mybir as mybir
from concourse.bass_utils import run_bass_kernel_spmd
from concourse.alu_op_type import AluOpType as OP

F32 = mybir.dt.float32
BF16 = mybir.dt.bfloat16
AFT = mybir.ActivationFunctionType
BF = ml_dtypes.bfloat16

NCORES = 8
ENGINES = ("sync", "gpsimd", "vector", "scalar", "tensor")
DMA_SEMS = {"sdc", "sdw", "sdw1", "sdx", "sdx1", "sdf0", "sdf1", "sdo0", "sdo1", "gdr", "gdf"}
SIG = np.array([(s % 8) * 2 + s // 8 for s in range(16)])


class CFG:
    def __init__(self, T=2048):
        self.T = T
        self.H = 1024
        self.NH = 16
        self.DH = 64
        self.B = 2
        self.TILE = 128
        self.NT = T // self.TILE
        self.NCC = 8
        self.FW = self.B * self.TILE
        self.WIDE = self.NH * self.DH


class Ev:
    __slots__ = ("sem", "val", "eng")
    def __init__(self, sem, val, eng):
        self.sem, self.val, self.eng = sem, val, eng


class Prog:
    def __init__(self):
        self.ops = {e: [] for e in ENGINES}
        self.seq = 0
        self.stream = ("P",)
        self.last_ev = {}
        self._lastfam = None
        self.final_cnt = {}

    def unit(self, sem):
        return 16 if sem in DMA_SEMS else 1

    def op(self, eng, fn, waits=(), inc=None, tag=""):
        ev = None
        if inc is not None:
            ev = Ev(inc, None, eng)
            self.last_ev[eng] = ev
        flat = []
        def _fl(w):
            if w is None:
                return
            if isinstance(w, (list, tuple)):
                for x in w:
                    _fl(x)
            else:
                flat.append(w)
        for w in waits:
            _fl(w)
        self.ops[eng].append((fn, tuple(flat), inc, ev, tag, self.seq, self.stream))
        self.seq += 1
        return ev

    _COST = (
        ("pj", 160), ("wo", 160), ("vt", 130), ("xnt", 130), ("k2t", 130),
        ("kk", 105), ("qk", 105), ("pr", 105), ("ap", 105), ("sqN", 105),
        ("sqM", 105), ("oi", 105), ("ox", 105), ("st", 105),
        ("dg", 380), ("G", 100), ("Gf", 120), ("ba", 120),
        ("bc", 120), ("bq", 120), ("bk", 120), ("ob", 120),
        ("ss", 150), ("os", 150), ("dw", 60),
        ("cv", 520), ("ye", 1150), ("B", 1150), ("BT", 1150), ("Mm", 1150),
        ("u1", 1200), ("u2", 1200), ("dv", 1150), ("oe", 1200),
        ("sd", 1150), ("sa", 1150), ("sc", 560), ("qt", 560), ("k2", 560),
        ("on", 560), ("og", 560), ("gn", 2300), ("lns", 450), ("lnq", 450),
        ("lna", 450), ("r1", 600), ("r2", 600),
        ("pe", 1250), ("sv", 1250), ("sl", 1800), ("sq", 1800), ("dx", 1250),
        ("k2e", 650), ("vte", 650), ("xne", 650), ("oxe", 1150),
        ("sb", 650), ("Ne", 1250), ("Me", 1250),
        ("xtm", 2500), ("xf", 2500), ("st", 2500), ("c_", 2000),
        ("wd", 2500), ("fA", 1200), ("f3", 1200), ("asm", 800), ("rb", 800),
    )

    _FAM = (
        ("sig", 1), ("sl", 1), ("sv", 1), ("lnq", 2),
        ("lnl", 0), ("lne", 0), ("re", 0), ("spe", 0), ("spl", 0),
        ("Ge", 0), ("egc", 0), ("gce2", 0), ("egf", 0), ("ol", 0),
        ("oe2", 0),
    )

    def _family(self, tag):
        for pfx, f in self._FAM:
            if tag.startswith(pfx):
                return f
        return None

    def _cost(self, tag, eng):
        if eng == "gpsimd" and tag[:2] in ("sq", "o2", "gn"):
            return 3500
        if eng in ("sync", "gpsimd"):
            return 900
        for pfx, c in self._COST:
            if tag.startswith(pfx):
                return c
        return 300

    def _final_order(self, groups):
        if groups is None:
            return {e: list(self.ops[e]) for e in ENGINES}
        sub = {e: {} for e in ENGINES}
        for e in ENGINES:
            for op in self.ops[e]:
                sub[e].setdefault(op[6], []).append(op)
        seen = set()
        for g in groups:
            seen.update(g)
        for e in ENGINES:
            for k in sub[e]:
                assert k in seen, f"stream {k} not in emission groups"
        final = {e: [] for e in ENGINES}
        evt = {}
        clock = {e: 0.0 for e in ENGINES}
        for g in groups:
            lists = {}
            for e in ENGINES:
                for k in g:
                    lst = sub[e].get(k)
                    if lst:
                        lists[(e, k)] = lst
            idxs = {key: 0 for key in lists}
            remaining = sum(len(v) for v in lists.values())
            while remaining:
                best, beststart = None, None
                for key, lst in lists.items():
                    i = idxs[key]
                    if i >= len(lst):
                        continue
                    op = lst[i]
                    ok = True
                    start = clock[key[0]]
                    for w in op[1]:
                        t = evt.get(id(w))
                        if t is None:
                            ok = False
                            break
                        if t > start:
                            start = t
                    if not ok:
                        continue
                    if key[0] == "scalar":
                        fam = self._family(op[4])
                        if fam is not None and fam != self._lastfam:
                            start += 1300
                    if beststart is None or start < beststart:
                        best, beststart = key, start
                assert best is not None, (
                    f"stream merge stalled in group {g}")
                e, k = best
                op = lists[best][idxs[best]]
                idxs[best] += 1
                remaining -= 1
                final[e].append(op)
                dur = self._cost(op[4], e)
                if e == "scalar":
                    fam = self._family(op[4])
                    if fam is not None:
                        self._lastfam = fam
                end = beststart + dur
                clock[e] = end
                if op[3] is not None:
                    evt[id(op[3])] = end
        return final

    def emit(self, block, sems, groups=None):
        import os as _os
        import re as _re
        stubk = int(_os.environ.get("STUBK", "-1"))
        stubtag = _os.environ.get("STUBTAG", "")
        stubwin = _os.environ.get("STUBWIN", "")
        wlo, whi = (int(v) for v in stubwin.split(":")) if stubwin else (0, 0)
        final = self._final_order(groups)
        cnt = {}
        for e in ENGINES:
            for op in final[e]:
                if op[2] is not None:
                    cnt[op[2]] = cnt.get(op[2], 0) + 1
                    op[3].val = cnt[op[2]]
        self.final_cnt = cnt
        def mk(ename):
            oplist = final[ename]
            def body(eng):
                waited = {}
                for fn, waits, inc, ev, tag, seq, stream in oplist:
                    stub = (stubk >= 0 and seq >= stubk) or \
                           (wlo <= seq < whi) or \
                           (stubtag and _re.match(stubtag, tag))
                    for w in waits:
                        assert w.val is not None, f"unnumbered wait in {tag}"
                        tgt = w.val * self.unit(w.sem)
                        if waited.get(w.sem, -1) >= tgt:
                            continue
                        eng.wait_ge(sems[w.sem], tgt)
                        waited[w.sem] = tgt
                    if stub:
                        if inc is not None:
                            eng.sem_inc(sems[inc], self.unit(inc))
                        continue
                    instr = fn(eng)
                    if inc is not None:
                        assert instr is not None, f"inc-op returned None: {tag}"
                        instr.then_inc(sems[inc], self.unit(inc))
            return body
        for e in ENGINES:
            getattr(block, e)(mk(e))


def host_constants(cfg):
    NH = cfg.NH
    c = {}
    c["ident128"] = np.eye(128, dtype=BF)
    L = np.zeros((128, 128), np.float32)
    for blk in range(2):
        for s in range(64):
            L[blk * 64 + s, blk * 64 + s:blk * 64 + 64] = 1.0
    c["lones"] = L
    bo = np.zeros((3 * NH, NH * 64), np.float32)
    for l in range(3):
        for h in range(NH):
            bo[l * NH + h, h * 64:h * 64 + 64] = 1.0
    c["bones48"] = bo.astype(BF)
    c["bones32"] = bo[:2 * NH].astype(BF)
    r = np.arange(64)
    mA = np.where(r[:, None] > r[None, :], 0.0, -30000.0)
    m2 = np.where(r[None, :] >= r[:, None], 0.0, -30000.0)
    m3 = np.where(r[None, :] > r[:, None], 0.0, -30000.0)
    c["maskA"] = np.tile(np.concatenate([mA, mA], 0), (1, 8)).astype(BF)
    c["mask2"] = np.tile(np.concatenate([m2, m2], 0), (1, 8)).astype(BF)
    c["mask3"] = np.tile(np.concatenate([m3, m3], 0), (1, 8)).astype(BF)
    hm = np.zeros((cfg.NCC, 128, NH), np.float32)
    hmT = np.zeros((cfg.NCC, NH, 128), np.float32)
    for cc in range(cfg.NCC):
        hm[cc, 0:64, cc] = 1.0
        hm[cc, 64:128, 8 + cc] = 1.0
        hmT[cc, cc, 0:64] = 1.0
        hmT[cc, 8 + cc, 64:128] = 1.0
    c["hmask"] = hm.astype(BF)
    c["hmaskT"] = hmT.astype(BF)
    ho = np.zeros((2, 2 * 2 * 64), np.float32)
    ho[0, 0:64] = 1.0; ho[0, 128:192] = 1.0
    ho[1, 64:128] = 1.0; ho[1, 192:256] = 1.0
    c["halfones"] = ho.astype(BF)
    c["onesrow"] = np.ones((1, 64), dtype=BF)
    c["identf32"] = np.eye(128, dtype=np.float32)
    return c


def build_program(cfg):
    nc = bass.Bass("TRN2", target_bir_lowering=False)
    H, NH, DH, B = cfg.H, cfg.NH, cfg.DH, cfg.B
    TILE, NT, NCC, FW, WIDE = cfg.TILE, cfg.NT, cfg.NCC, cfg.FW, cfg.WIDE
    T = cfg.T
    consts = host_constants(cfg)

    def din(name, shape, dt=F32):
        return nc.dram_tensor(name, shape, dt, kind="ExternalInput").ap()

    x_tm_d = din("x_tm", [B, T, H])
    x_fm_d = din("x_fm", [B, H, T])
    w_d = {w: din(f"w_{w}t", [H, H]) for w in ("q", "k", "v", "g", "o", "f")}
    wbat_d = din("wbat", [H, 2 * NH])
    convw_d = din("convw", [128, 3, NCC, 4])
    alog_d = din("alog", [1, NH])
    dtrep_d = din("dtrep", [128, NH])
    normw2_d = din("normw2", [128, 1])
    cst_d = {nm: din("c_" + nm, list(a.shape), BF16 if a.dtype == BF else F32)
             for nm, a in consts.items()}
    y_fm_d = nc.dram_tensor("y_fm", [B, H, T], F32, kind="ExternalOutput").ap()

    es = contextlib.ExitStack()
    sb = lambda nm, sh, dt=F32: es.enter_context(nc.sbuf_tensor(nm, sh, dt))
    ps = lambda nm, sh, dt=F32: es.enter_context(nc.psum_tensor(nm, sh, dt))

    W = {w: sb(f"W{w}", [128, NCC, H], BF16) for w in ("q", "k", "v", "g", "o", "f")}
    Wba = sb("Wba", [128, NCC, 2 * NH], BF16)
    wstage = sb("wstage", [128, 1024])
    convw = sb("convw_t", [128, 3, NCC, 4])
    ident = sb("identt", [128, 128], BF16)
    identf = sb("identft", [128, 128])
    lones = sb("lonest", [128, 128])
    maskA = sb("maskAt", [128, 512], BF16)
    mask2 = sb("mask2t", [128, 512], BF16)
    mask3 = sb("mask3t", [128, 512], BF16)
    hmask = sb("hmaskt", [128, NCC, NH], BF16)
    hmaskT = sb("hmaskTt", [NH, NCC, 128], BF16)
    onesrow = sb("onesrowt", [1, 64], BF16)
    dtrep = sb("dtrept", [128, NH])
    normw2 = sb("normw2t", [128, 1])
    alog = sb("alogt", [1, NH])
    negexpA = sb("negexpAt", [128, NH])
    epsln = sb("epsln", [128, 1])
    epsl2 = sb("epsl2", [128, 1])
    epso = sb("epso", [128, 1])
    lnq8 = sb("lnq8", [128, 1])
    lnsq = sb("lnsq", [128, 1])
    lnsq2 = sb("lnsq2", [128, 1])
    rhsA = sb("rhsA", [58, 2, WIDE], BF16)
    rhs3 = sb("rhs3", [86, 2, WIDE], BF16)
    Gpos = sb("Gpos", [58, 2, B, 64], BF16)
    Gneg = sb("Gneg", [86, 2, B, 64], BF16)
    xtm = sb("xtm", [128, H])
    xfm = sb("xfm", [128, 2, FW])
    xnf = sb("xnf", [128, NCC, B, TILE], BF16)
    lnsc = sb("lnsc", [128, B, 2])
    prebuf = sb("prebuf", [128, NCC, B, TILE + 3], BF16)
    xnbv = prebuf[:].rearrange("p a b t -> p (a b t)")[:, 0:2 * H].rearrange(
        "p (b h) -> p b h", b=2)
    halo = sb("halo", [128, 3, NCC, B, 3], BF16)
    gatebuf = sb("gatebuf", [128, NCC, B, TILE], BF16)
    ffng = sb("ffng", [128, NCC, B, TILE], BF16)
    convacc = sb("convacc", [128, NCC, B, TILE], BF16)
    silbuf = sb("silbuf", [128, NCC, B, TILE], BF16)
    xtm2 = silbuf[:].rearrange("p a b t -> p (a b t)").bitcast(F32)[:, 0:H]
    khat = sb("khat", [128, NCC, B, TILE], BF16)
    qtil = sb("qtil", [128, NCC, B, TILE], BF16)
    qhat = sb("qhat", [128, NCC, B, TILE], BF16)
    vtm = sb("vtm", [128, 2, WIDE], BF16)
    k2tm = sb("k2tm", [128, 2, WIDE], BF16)
    uh = sb("uh", [128, 2, WIDE], BF16)
    dvb = sb("dvb", [128, 2, WIDE], BF16)
    yv = sb("yv", [128, WIDE], BF16)
    yv2 = sb("yv2", [128, WIDE], BF16)
    Dexp1 = yv
    Dexp3 = yv2
    Mc = sb("Mcc2", [128, WIDE], BF16) if False else None
    tscr = wstage
    S32b = wstage
    Dexp2 = sb("Dexp2", [128, WIDE], BF16)
    McT = Dexp2
    Bw = sb("Bww", [128, WIDE], BF16)
    BTw = sb("BTww", [128, WIDE], BF16)
    Mmw = sb("Mmww", [128, WIDE], BF16)

    Ncr = sb("Ncc", [128, WIDE], BF16)
    Mx = sb("Mxx", [128, WIDE], BF16)
    Nx = sb("Nxx", [128, WIDE], BF16)
    S32 = sb("S32", [128, B, NCC, DH])
    Sb16 = sb("Sb16", [128, B, NCC, DH], BF16)
    qscq = sb("qscq", [16, FW], BF16)
    qsck = sb("qsck", [16, FW], BF16)
    osc = sb("osc", [16, FW], BF16)
    beta_tm = sb("beta_tm", [128, 2, NH])
    lnb_tm = sb("lnb_tm", [128, 2, NH])
    g_tm = sb("g_tm", [128, 2, NH])
    G_tm = sb("G_tm", [128, 2, NH])
    expG_tm = sb("expG_tm", [128, 2, NH])
    tmsc1 = sb("tmsc1", [128, 2, NH])
    Gfm = sb("Gfm", [NH, B, 2, 64])
    Gl1s = sb("Gl1s", [NH, B, TILE], BF16)
    Gn1s = sb("Gn1s", [NH, B, TILE], BF16)
    Lb = [sb(f"Lb{i}", [NH, B, TILE], BF16) for i in range(2)]
    Glv1 = sb("Glv1", [NH, B, TILE])
    Glv2 = sb("Glv2", [NH, B, TILE])
    lnbfm = Glv2
    eGceFM = sb("eGceFM", [NH, B, TILE], BF16)
    expGfm = sb("expGfm", [NH, B, TILE], BF16)
    rowb = sb("rowb", [1, B, NH])
    rowbb = sb("rowbb", [1, 2, B * NCC], BF16)
    resid = wstage[:].rearrange("p (a b) -> p a b", a=2)[:, :, 0:FW]
    osq = wstage[:].bitcast(BF16).rearrange("p (a b t) -> p a b t", a=NCC, b=B)
    oxsv = wstage[:].bitcast(BF16)[:, 0:1024]

    PA = ps("PA", [128, 512])
    PB = ps("PBb", [128, 512])
    PW1 = ps("PW1", [128, 1024])
    PW2 = ps("PW2", [128, 1024])
    PO = ps("PO", [128, 1024])
    PAb = PA[:].bitcast(BF16)
    PW2b = PW2[:].bitcast(BF16)

    sems = {nm: es.enter_context(nc.semaphore(nm))
            for nm in ("sdc", "sdw", "sdw1", "sdx", "sdx1", "sdf0", "sdf1",
                       "sdo0", "sdo1", "gdr", "gdf", "tp", "vp", "sp", "gp")}
    P = Prog()

    def dma(q, sem, out, in_, waits=(), tag="", slow=False):
        return P.op(q, lambda e, o=out, i=in_, s=slow: e.dma_start(
            out=o, in_=i, allow_slow_non_contiguous=s),
            waits=waits, inc=sem, tag=tag)

    def T_(fn, waits=(), tag="", inc="tp"):
        return P.op("tensor", fn, waits=waits, inc=inc, tag=tag)

    def V_(fn, waits=(), tag="", inc="vp"):
        return P.op("vector", fn, waits=waits, inc=inc, tag=tag)

    def S_(fn, waits=(), tag="", inc="sp"):
        return P.op("scalar", fn, waits=waits, inc=inc, tag=tag)

    def G_(fn, waits=(), tag="", inc="gp"):
        return P.op("gpsimd", fn, waits=waits, inc=inc, tag=tag)

    bkr = {k: [] for k in ("PA", "PB", "PW1", "PW2", "PO")}

    def bank_open(bank):
        ws = bkr[bank]
        bkr[bank] = []
        return ws

    def bank_read(bank, ev):
        bkr[bank].append(ev)
        return ev

    evc = []
    for dst, nm in [(ident[:], "ident128"), (identf[:], "identf32"),
                    (lones[:], "lones"),
                    (maskA[:], "maskA"), (mask2[:], "mask2"), (mask3[:], "mask3"),
                    (onesrow[:], "onesrow")]:
        evc.append(dma("sync", "sdc", dst, cst_d[nm], tag=f"c_{nm}"))
    evc.append(dma("sync", "sdc", hmask[:], cst_d["hmask"].rearrange("a p b -> p a b"),
                   tag="c_hm"))
    evc.append(dma("sync", "sdc", hmaskT[:], cst_d["hmaskT"].rearrange("a p b -> p a b"),
                   tag="c_hmT"))
    evc.append(dma("sync", "sdc", convw[:].rearrange("p a b c -> p (a b c)"),
                   convw_d.rearrange("p a b c -> p (a b c)"), tag="c_cw"))
    evc.append(dma("sync", "sdc", dtrep[:], dtrep_d, tag="c_dt"))
    evc.append(dma("sync", "sdc", normw2[:], normw2_d, tag="c_nw"))
    evc.append(dma("sync", "sdc", alog[:], alog_d, tag="c_al"))
    for ck in range(2):
        evc.append(dma("sync", "sdc", rhsA[0:48, ck, :], cst_d["bones48"], tag="c_bA"))
        evc.append(dma("sync", "sdc", rhs3[0:48, ck, :], cst_d["bones48"], tag="c_b3a"))
        evc.append(dma("sync", "sdc", rhs3[54:86, ck, :], cst_d["bones32"], tag="c_b3b"))
    sel0 = cst_d["halfones"][0:1, :]
    sel1 = cst_d["halfones"][1:2, :]
    row = lambda ap: ap.rearrange("p ck b r -> p (ck b r)")
    for i in range(5):
        evc.append(dma("sync", "sdc", row(Gpos[48 + i:49 + i]), sel0, tag="hoA"))
        evc.append(dma("sync", "sdc", row(Gpos[53 + i:54 + i]), sel1, tag="hoB"))
    for i in range(3):
        evc.append(dma("sync", "sdc", row(Gneg[48 + i:49 + i]), sel0, tag="ho3A"))
        evc.append(dma("sync", "sdc", row(Gneg[51 + i:52 + i]), sel1, tag="ho3B"))
    ev_const = evc[-1]

    evm = []
    evm.append(V_(lambda e: e.memset(epsln[:], 1e-5), tag="m1"))
    evm.append(V_(lambda e: e.memset(epsl2[:], 1e-6), tag="m2"))
    evm.append(V_(lambda e: e.memset(epso[:], 1e-5), tag="m3"))
    evm.append(V_(lambda e: e.memset(lnq8[:], float(np.log(1.0 / 8.0))), tag="m4"))
    ev_S_w = V_(lambda e: e.memset(S32[:].rearrange("p a b c -> p (a b c)"), 0.0), tag="m5")
    ev_Sb_w = V_(lambda e: e.memset(Sb16[:].rearrange("p a b c -> p (a b c)"), 0.0), tag="m6")
    ev_halo0 = V_(lambda e: e.memset(halo[:].rearrange("p a b c d -> p (a b c d)"), 0.0), tag="m7")
    ev_minit = ev_halo0

    ev_e1 = S_(lambda e: e.activation(out=alog[:], in_=alog[:], func=AFT.Exp),
               waits=[ev_const], tag="expA")
    wA0 = [ev_e1] + bank_open("PB")
    ev_e2a = T_(lambda e: e.matmul(PB[0:64, 496:496 + NH], lones[0:1, 0:64], alog[:],
                                   start=True, stop=True, tile_position=(0, 0)),
                waits=wA0, tag="bcA0")
    ev_e2 = T_(lambda e: e.matmul(PB[64:128, 496:496 + NH], lones[0:1, 0:64], alog[:],
                                  start=True, stop=True, tile_position=(0, 64)),
               waits=[ev_e2a], tag="bcA1")
    ev_negA = V_(lambda e: e.tensor_scalar(out=negexpA[:], in0=PB[:, 496:496 + NH],
                                           scalar1=-1.0, scalar2=None, op0=OP.mult),
                 waits=[ev_e2], tag="negA")
    bank_read("PB", ev_negA)

    ev_wcast = {}
    half_cast = [None, None]
    nslot = 0
    for ic in range(NCC):
        hh = nslot % 2
        nslot += 1
        evd = dma("sync" if hh == 0 else "scalar", "sdw" if hh == 0 else "sdw1",
                  wstage[:, hh * 512:hh * 512 + 2 * NH],
                  wbat_d[ic * 128:(ic + 1) * 128, :],
                  waits=[half_cast[hh]], tag=f"wdba{ic}")
        half_cast[hh] = V_(lambda e, ic=ic, hh=hh: e.tensor_scalar(
            out=Wba[:, ic, :], in0=wstage[:, hh * 512:hh * 512 + 2 * NH],
            scalar1=1.0, scalar2=None, op0=OP.mult),
            waits=[evd], tag=f"wcba{ic}")
        ev_wcast[("ba", ic)] = half_cast[hh]
    for wname in ("q", "k", "v", "g", "o", "f"):
        for ic in range(NCC):
            last = None
            for ch in range(2):
                hh = nslot % 2
                nslot += 1
                evd = dma("sync" if hh == 0 else "scalar",
                          "sdw" if hh == 0 else "sdw1",
                          wstage[:, hh * 512:hh * 512 + 512],
                          w_d[wname][ic * 128:(ic + 1) * 128, ch * 512:ch * 512 + 512],
                          waits=[half_cast[hh]], tag=f"wd{wname}{ic}{ch}")
                if wname == "f":
                    last = S_(lambda e, ic=ic, hh=hh, ch=ch: e.sign(
                        out=W["f"][:, ic, ch * 512:ch * 512 + 512],
                        in_=wstage[:, hh * 512:hh * 512 + 512]),
                        waits=[evd], tag=f"wsg{ic}{ch}")
                else:
                    last = V_(lambda e, wn=wname, ic=ic, hh=hh, ch=ch: e.tensor_scalar(
                        out=W[wn][:, ic, ch * 512:ch * 512 + 512],
                        in0=wstage[:, hh * 512:hh * 512 + 512],
                        scalar1=1.0, scalar2=None, op0=OP.mult),
                        waits=[evd], tag=f"wc{wname}{ic}{ch}")
                half_cast[hh] = last
            ev_wcast[(wname, ic)] = last


    ev_xtm_use = ev_minit
    ev_xtm2_use = ev_minit
    ev_lnsq = None
    ev_uh_use = None
    ev_uh_w = None
    ev_state_S = ev_S_w
    ev_state_Sb = ev_Sb_w
    ev_halo = [ev_halo0, ev_halo0, ev_halo0]
    ev_prebuf_free = [None, None, None]
    bankfree = {0: None, 1: None}
    ev_wfree = {}
    ev_laststore = {}

    for it in range(NT):
        P.stream = ("L", it)
        t0 = it * TILE
        ev_lnap = []
        for b in range(B):
            xb = xtm if b == 0 else xtm2
            ev_xt = dma("sync", "sdx" if b == 0 else "sdx1", xb[:, :],
                        x_tm_d[b, t0:t0 + TILE, :],
                        waits=[ev_xtm_use if b == 0 else
                               [ev_xtm2_use, ev_wfree.get("silbuf")]],
                        tag=f"xtm{it}_{b}")
            e1 = V_(lambda e, b=b, xb=xb: e.tensor_scalar(
                out=xb[:, :], in0=xb[:, :], scalar1=1.0, scalar2=0.0,
                op0=OP.mult, op1=OP.add, accum_out=lnsc[:, b, 0:1]),
                waits=[ev_xt], tag=f"lns{it}_{b}")
            e2w = bank_open("PA")
            e2a = S_(lambda e, b=b, xb=xb: e.activation(
                out=PA[:, 0:512], in_=xb[:, 0:512], func=AFT.Square,
                accum_out=lnsc[:, b, 1:2]), waits=[ev_xt, e1, ev_lnsq] + e2w,
                tag=f"lnq{it}_{b}")
            e2b = S_(lambda e, b=b, xb=xb: e.activation(
                out=PA[:, 0:512], in_=xb[:, 512:1024], func=AFT.Square,
                accum_out=lnsq[:, 0:1] if b == 0 else lnsq2[:, 0:1]),
                waits=[e2a], tag=f"lnq2{it}_{b}")
            bank_read("PA", e2b)
            e2 = V_(lambda e, b=b: e.tensor_tensor(
                out=lnsc[:, b, 1:2], in0=lnsc[:, b, 1:2],
                in1=lnsq[:, 0:1] if b == 0 else lnsq2[:, 0:1],
                op=OP.add), waits=[e2b], tag=f"lnq3{it}_{b}")
            ev_lnsq = e2
            e3 = V_(lambda e, b=b: e.tensor_scalar(
                out=lnsc[:, b, :], in0=lnsc[:, b, :], scalar1=1.0 / H,
                scalar2=None, op0=OP.mult), waits=[e1, e2], tag=f"lnm{it}_{b}")
            e5 = V_(lambda e, b=b: e.scalar_tensor_tensor(
                out=lnsc[:, b, 1:2], in0=lnsc[:, b, 0:1], scalar=lnsc[:, b, 0:1],
                in1=lnsc[:, b, 1:2], op0=OP.mult, op1=OP.subtract),
                waits=[e3], tag=f"lnv{it}_{b}")
            e6 = V_(lambda e, b=b: e.tensor_scalar(
                out=lnsc[:, b, 1:2], in0=lnsc[:, b, 1:2], scalar1=-1.0,
                scalar2=None, op0=OP.mult), waits=[e5], tag=f"lnn{it}_{b}")
            e7 = S_(lambda e, b=b: e.activation(
                out=lnsc[:, b, 1:2], in_=lnsc[:, b, 1:2], func=AFT.Ln,
                bias=epsln[:, 0:1]), waits=[e6], tag=f"lnl{it}_{b}")
            e8 = S_(lambda e, b=b: e.activation(
                out=lnsc[:, b, 1:2], in_=lnsc[:, b, 1:2], func=AFT.Exp,
                scale=-0.5), waits=[e7], tag=f"lne{it}_{b}")
            e9 = V_(lambda e, b=b, xb=xb: e.tensor_scalar(
                out=xnbv[:, b, :], in0=xb[:, :],
                scalar1=lnsc[:, b, 0:1], scalar2=lnsc[:, b, 1:2],
                op0=OP.subtract, op1=OP.mult),
                waits=[e8, ev_prebuf_free[0], ev_prebuf_free[1],
                       ev_prebuf_free[2], ev_halo[0], ev_halo[1], ev_halo[2]],
                tag=f"lna{it}_{b}")
            ev_lnap.append(e9)
            if b == 0:
                ev_xtm_use = [e9, e1, e2b]
            else:
                ev_xtm2_use = [e9, e1, e2b]
                ev_wfree["silbuf"] = [e9, e2b, ev_wfree.get("silbuf")]

        ev_xnf = []
        for b in range(B):
            for g4 in range(2):
                evt = None
                for q4 in range(4):
                    ic = g4 * 4 + q4
                    evt = T_(lambda e, b=b, ic=ic, q4=q4: e.transpose(
                        PAb[:, q4 * 128:q4 * 128 + 128],
                        xnbv[:, b, ic * 128:ic * 128 + 128],
                        ident[:]),
                        waits=[ev_lnap[b]] + (bank_open("PA") if q4 == 0 else []),
                        tag=f"xnt{it}_{b}_{ic}")
                enf = S_(lambda e, b=b, g4=g4: e.activation(
                    out=xnf[:, g4 * 4:g4 * 4 + 4, b, :],
                    in_=PAb[:, 0:512].rearrange("p (a t) -> p a t", a=4), func=AFT.Copy),
                    waits=[evt, ev_wfree.get("xnf")], tag=f"xne{it}_{b}_{g4}")
                ev_xnf.append(enf)
                bank_read("PA", enf)
        ev_xnf_done = ev_xnf[-1]
        ev_uh_use = ev_xnf_done

        ev_pre = {}
        ev_sc_done = {}
        banks = [PA, PB]
        for wname in ("v", "k", "q"):
            wi = {"q": 0, "k": 1, "v": 2}[wname]
            ehc = V_(lambda e, wi=wi: e.tensor_copy(
                out=prebuf[:, :, :, 0:3],
                in_=halo[:, wi, :, :, :]),
                waits=[ev_halo[wi], ev_prebuf_free[wi], ev_xnf_done],
                tag=f"hi{it}_{wi}")
            evac = {}
            for oc in range(NCC):
                bank = banks[(oc // 2) % 2]
                bnm = "PA" if (oc // 2) % 2 == 0 else "PB"
                half = (oc % 2) * FW
                evmm = None
                for kc in range(NCC):
                    evmm = T_(lambda e, wn=wname, oc=oc, kc=kc, bank=bank, half=half: e.matmul(
                        bank[:, half:half + FW], W[wn][:, kc, oc * 128:(oc + 1) * 128],
                        xnf[:, kc, :, :].rearrange("p b t -> p (b t)"),
                        start=(kc == 0), stop=(kc == NCC - 1)),
                        waits=[ev_xnf_done, ev_wcast[(wname, NCC - 1)]]
                        + (bank_open(bnm) if kc == 0 else []),
                        tag=f"pj{it}{wname}{oc}_{kc}",
                        inc="tp" if kc == NCC - 1 else None)
                if oc % 2 == 1:
                    eev = S_(lambda e, oc=oc, bank=bank: e.activation(
                        out=prebuf[:, oc - 1:oc + 1, :, 3:3 + TILE],
                        in_=bank[:, 0:2 * FW].rearrange("p (o b t) -> p o b t", o=2, b=B),
                        func=AFT.Copy),
                        waits=[evmm, ehc], tag=f"pe{it}{wname}{oc}")
                    evac[oc - 1] = evac[oc] = eev
                    bank_read(bnm, eev)
            ev_pre[wname] = evac[NCC - 1]
            ev_halo[wi] = V_(lambda e, wi=wi: e.tensor_copy(
                out=halo[:, wi, :, :, :],
                in_=prebuf[:, :, :, TILE:TILE + 3]),
                waits=[ev_pre[wname]], tag=f"ho{it}_{wi}")

            if wname == "v":
                cwb = lambda j: convw[:, 2, :, j].unsqueeze(2).unsqueeze(3).broadcast_to(
                    [128, NCC, B, TILE])
                ecv = V_(lambda e, cw0=cwb(0): e.tensor_tensor(
                    out=convacc[:, :, :, :], in0=prebuf[:, :, :, 0:TILE],
                    in1=cw0, op=OP.mult),
                    waits=[evac[NCC - 1], ev_const, ev_wfree.get("convacc")],
                    tag=f"cv{it}v0")
                for j in range(1, 4):
                    ez2 = V_(lambda e, j=j, cwj=cwb(j): e.tensor_tensor(
                        out=silbuf[:, :, :, :], in0=prebuf[:, :, :, j:j + TILE],
                        in1=cwj, op=OP.mult),
                        waits=[ecv,
                               ev_wfree.get("silbuf") if j == 1 else None],
                        tag=f"cw{it}v{j}")
                    ecv = V_(lambda e: e.tensor_tensor(
                        out=convacc[:, :, :, :], in0=convacc[:, :, :, :],
                        in1=silbuf[:, :, :, :], op=OP.add),
                        waits=[ez2], tag=f"cv{it}v{j}")
                ev_prebuf_free[wi] = ecv
                ev_wfree["convacc"] = None
                evsil = S_(lambda e: e.activation(
                    out=silbuf[:].rearrange("p a b t -> p (a b t)"),
                    in_=convacc[:].rearrange("p a b t -> p (a b t)"), func=AFT.Silu),
                    waits=[ecv, ev_wfree.get("silbuf")], tag=f"sl{it}v")
                evvt = []
                for b in range(B):
                    for ck in range(2):
                        for g2 in range(2):
                            evt = None
                            for q4 in range(4):
                                ic = g2 * 4 + q4
                                evt = T_(lambda e, b=b, ck=ck, ic=ic, q4=q4: e.transpose(
                                    PAb[0:64, q4 * 128:q4 * 128 + 128],
                                    silbuf[:, ic, b, ck * 64:ck * 64 + 64],
                                    ident[:]),
                                    waits=[evsil] + (bank_open("PA") if q4 == 0 else []),
                                    tag=f"vt{it}_{b}{ck}{ic}")
                            evte = S_(lambda e, b=b, ck=ck, g2=g2: e.activation(
                                out=vtm[64 * b:64 * b + 64, ck, g2 * 512:g2 * 512 + 512],
                                in_=PAb[0:64, 0:512], func=AFT.Copy),
                                waits=[evt, ev_wfree.get("vtm")], tag=f"vte{it}_{b}{ck}{g2}")
                            evvt.append(evte)
                            bank_read("PA", evte)
                ev_pre["vtm"] = evvt[-1]
                ev_wfree["silbuf"] = evvt[-1]
                ev_wfree["convacc"] = evsil
                continue

            cwb = lambda j: convw[:, wi, :, j].unsqueeze(2).unsqueeze(3).broadcast_to(
                [128, NCC, B, TILE])
            ecv = V_(lambda e, cw0=cwb(0): e.tensor_tensor(
                out=convacc[:, :, :, :], in0=prebuf[:, :, :, 0:TILE],
                in1=cw0, op=OP.mult),
                waits=[evac[NCC - 1], ev_const, ev_wfree.get("convacc")],
                tag=f"cv{it}{wname}0")
            for j in range(1, 4):
                ez2 = V_(lambda e, j=j, cwj=cwb(j): e.tensor_tensor(
                    out=silbuf[:, :, :, :], in0=prebuf[:, :, :, j:j + TILE],
                    in1=cwj, op=OP.mult),
                    waits=[ecv,
                           ev_wfree.get("silbuf") if j == 1 else None],
                    tag=f"cw{it}{wname}{j}")
                ecv = V_(lambda e: e.tensor_tensor(
                    out=convacc[:, :, :, :], in0=convacc[:, :, :, :],
                    in1=silbuf[:, :, :, :], op=OP.add),
                    waits=[ez2], tag=f"cv{it}{wname}{j}")
            ev_prebuf_free[wi] = ecv
            esil = S_(lambda e: e.activation(
                out=silbuf[:].rearrange("p a b t -> p (a b t)"),
                in_=convacc[:].rearrange("p a b t -> p (a b t)"), func=AFT.Silu),
                waits=[ecv, ev_wfree.get("silbuf")], tag=f"sl{it}{wname}")
            esq = V_(lambda e: e.tensor_tensor(
                out=convacc[:].rearrange("p a b t -> p (a b t)"),
                in0=silbuf[:].rearrange("p a b t -> p (a b t)"),
                in1=silbuf[:].rearrange("p a b t -> p (a b t)"), op=OP.mult),
                waits=[esil], tag=f"sq{it}{wname}")
            ev_wfree["convacc"] = esq
            evmm = None
            for ic in range(NCC):
                evmm = T_(lambda e, ic=ic: e.matmul(
                    PB[0:16, 0:FW], hmask[:, ic, 0:16],
                    convacc[:, ic, :, :].rearrange("p b t -> p (b t)"),
                    start=(ic == 0), stop=(ic == NCC - 1)),
                    waits=[esq] + (bank_open("PB") if ic == 0 else []),
                    tag=f"ss{it}{wname}{ic}",
                    inc="tp" if ic == NCC - 1 else None)
            qst = qscq if wname == "q" else qsck
            el1 = S_(lambda e, qst=qst: e.activation(
                out=qst[:], in_=PB[0:16, 0:FW], func=AFT.Ln,
                bias=epsl2[0:16, 0:1]), waits=[evmm, ev_wfree.get("qsc")],
                tag=f"rl{it}{wname}")
            bank_read("PB", el1)
            if wname == "q":
                el2 = S_(lambda e, qst=qst: e.activation(
                    out=qst[:], in_=qst[:],
                    func=AFT.Exp, scale=-0.5, bias=lnq8[0:16, 0:1]),
                    waits=[el1], tag=f"re{it}{wname}")
            else:
                el2 = S_(lambda e, qst=qst: e.activation(
                    out=qst[:], in_=qst[:],
                    func=AFT.Exp, scale=-0.5), waits=[el1], tag=f"re{it}{wname}")
            dst = qhat if wname == "q" else khat
            evsc = []
            for ic in range(NCC):
                bbank = banks[ic % 2]
                bbnm = "PA" if ic % 2 == 0 else "PB"
                evb = T_(lambda e, ic=ic, qst=qst, bbank=bbank: e.matmul(
                    bbank[:, 0:FW], hmaskT[:, ic, :], qst[:],
                    start=True, stop=True),
                    waits=[el2] + bank_open(bbnm),
                    tag=f"bc{it}{wname}{ic}")
                evsc.append(V_(lambda e, ic=ic, dst=dst, bbank=bbank: e.tensor_tensor(
                    out=dst[:, ic, :, :].rearrange("p b t -> p (b t)"),
                    in0=silbuf[:, ic, :, :].rearrange("p b t -> p (b t)"),
                    in1=bbank[:, 0:FW], op=OP.mult),
                    waits=[evb, ev_wfree.get("qhat") if wname == "q"
                           else ev_wfree.get("khat")],
                    tag=f"sc{it}{wname}{ic}"))
                bank_read(bbnm, evsc[-1])
            ev_pre[wname + "hat"] = evsc[-1]
            ev_sc_done[wname] = [evsc[-1], evb]
            ev_wfree["silbuf"] = evsc[-1]


        for wname, dstb in (("g", gatebuf), ("f", ffng)):
            evacs = []
            for oc in range(NCC):
                bank = banks[(oc // 2) % 2]
                bnm = "PA" if (oc // 2) % 2 == 0 else "PB"
                half = (oc % 2) * FW
                evmm = None
                for kc in range(NCC):
                    evmm = T_(lambda e, wn=wname, oc=oc, kc=kc, bank=bank, half=half: e.matmul(
                        bank[:, half:half + FW], W[wn][:, kc, oc * 128:(oc + 1) * 128],
                        xnf[:, kc, :, :].rearrange("p b t -> p (b t)"),
                        start=(kc == 0), stop=(kc == NCC - 1)),
                        waits=[ev_xnf_done, ev_wcast[(wname, NCC - 1)]]
                        + (bank_open(bnm) if kc == 0 else []),
                        tag=f"pj{it}{wname}{oc}_{kc}",
                        inc="tp" if kc == NCC - 1 else None)
                if oc % 2 == 1:
                    eev = S_(lambda e, oc=oc, bank=bank, dstb=dstb: e.activation(
                        out=dstb[:, oc - 1:oc + 1, :, :],
                        in_=bank[:, 0:2 * FW].rearrange("p (o b t) -> p o b t", o=2, b=B),
                        func=AFT.Silu),
                        waits=[evmm, ev_wfree.get("gate" if dstb is gatebuf else "ffn")],
                        tag=f"sv{it}{wname}{oc}")
                    evacs.append(eev)
                    bank_read(bnm, eev)
            if wname == "g":
                ev_gate_raw = evacs[-1]
            else:
                ev_ffn_done = evacs[-1]
                ev_wfree["xnf"] = evmm
        eg1 = V_(lambda e: e.tensor_copy(
            out=convacc[:].rearrange("p a b t -> p (a b t)"),
            in_=gatebuf[:].rearrange("p a b t -> p (a b t)")),
            waits=[ev_gate_raw, ev_wfree.get("convacc")], tag=f"gn1{it}")
        ev_gate_done = V_(lambda e: e.tensor_scalar(
            out=gatebuf[:].rearrange("p a b t -> p (a b t)"),
            in0=convacc[:].rearrange("p a b t -> p (a b t)"),
            scalar1=normw2[:, 0:1], scalar2=None, op0=OP.mult),
            waits=[eg1], tag=f"gn2{it}")
        ev_wfree["convacc"] = ev_gate_done

        P.stream = ("S", it)

        evba = {}
        evmm = None
        for ck in range(2):
            for b in range(B):
                for kc in range(NCC):
                    evmm = T_(lambda e, ck=ck, b=b, kc=kc: e.matmul(
                        PW2[64 * b:64 * b + 64, 256 + 32 * ck:288 + 32 * ck],
                        xnf[:, kc, b, ck * 64:ck * 64 + 64],
                        Wba[:, kc, :], start=(kc == 0), stop=(kc == NCC - 1),
                        tile_position=(0, 64 * b)),
                        waits=[ev_xnf_done, ev_wcast[("ba", NCC - 1)]]
                        + (bank_open("PW2") if (ck == 0 and b == 0 and kc == 0) else []),
                        tag=f"ba{it}_{ck}_{b}_{kc}",
                        inc="tp" if kc == NCC - 1 else None)
        bav = PW2[:, 256:320].rearrange("p (c w) -> p c w", c=2)
        eb = S_(lambda e: e.activation(
            out=beta_tm[:, :, :], in_=bav[:, :, 0:NH], func=AFT.Sigmoid),
            waits=[evmm, ev_wfree.get("tmsmalls")], tag=f"sig{it}")
        bank_read("PW2", eb)
        elb = S_(lambda e: e.activation(
            out=lnb_tm[:, :, :], in_=beta_tm[:, :, :], func=AFT.Ln),
            waits=[eb], tag=f"lnb{it}")
        evba[("lnb", 0)] = evba[("lnb", 1)] = elb
        ez = V_(lambda e: e.tensor_tensor(
            out=g_tm[:, :, :], in0=bav[:, :, NH:2 * NH],
            in1=dtrep[:].unsqueeze(1).broadcast_to([128, 2, NH]),
            op=OP.add), waits=[evmm, ev_const], tag=f"z{it}")
        bank_read("PW2", ez)
        ea = V_(lambda e: e.scalar_tensor_tensor(
            out=tmsc1[:, :, :], in0=g_tm[:, :, :], scalar=-1.0,
            op0=OP.mult, in1=g_tm[:, :, :], op1=OP.max),
            waits=[ez], tag=f"abs{it}")
        ee = S_(lambda e: e.activation(
            out=tmsc1[:, :, :], in_=tmsc1[:, :, :], func=AFT.Exp,
            scale=-1.0), waits=[ea, elb], tag=f"spe{it}")
        el = S_(lambda e: e.activation(
            out=tmsc1[:, :, :], in_=tmsc1[:, :, :], func=AFT.Ln,
            bias=1.0), waits=[ee], tag=f"spl{it}")
        er = V_(lambda e: e.tensor_scalar(
            out=g_tm[:, :, :], in0=g_tm[:, :, :], scalar1=0.0,
            scalar2=None, op0=OP.max), waits=[el], tag=f"spr{it}")
        es2 = V_(lambda e: e.tensor_tensor(
            out=g_tm[:, :, :], in0=g_tm[:, :, :], in1=tmsc1[:, :, :],
            op=OP.add), waits=[er], tag=f"spa{it}")
        eg = V_(lambda e: e.tensor_tensor(
            out=g_tm[:, :, :], in0=g_tm[:, :, :],
            in1=negexpA[:].unsqueeze(1).broadcast_to([128, 2, NH]),
            op=OP.mult), waits=[es2, ev_negA], tag=f"gg{it}")
        evg = []
        for ck in range(2):
            for b in range(B):
                evg.append(T_(lambda e, ck=ck, b=b: e.matmul(
                    PW2[64 * b:64 * b + 64, 320 + 16 * ck:336 + 16 * ck],
                    lones[64 * b:64 * b + 64, 64 * b:64 * b + 64],
                    g_tm[64 * b:64 * b + 64, ck, :], start=True, stop=True,
                    tile_position=(64 * b, 64 * b)),
                    waits=[eg] + (bank_open("PW2") if (ck == 0 and b == 0) else []),
                    tag=f"G{it}_{ck}_{b}"))
        ecp = V_(lambda e: e.tensor_copy(
            out=G_tm[:, :, :],
            in_=PW2[:, 320:352].rearrange("p (c h) -> p c h", c=2)),
            waits=[evg[-1]], tag=f"Gc{it}")
        bank_read("PW2", ecp)
        eex = S_(lambda e: e.activation(
            out=expG_tm[:, :, :], in_=G_tm[:, :, :], func=AFT.Exp),
            waits=[ecp], tag=f"Ge{it}")
        evba[("expG", 0)] = evba[("expG", 1)] = eex
        for ck in range(2):
            gfc = 352 + 160 * ck
            eF = T_(lambda e, ck=ck, gfc=gfc: e.matmul(
                PW2[0:NH, gfc:gfc + 128],
                g_tm[:, ck, :],
                lones[:, 0:128],
                start=True, stop=True),
                waits=[eg] + (bank_open("PW2") if ck == 0 else []),
                tag=f"Gf{it}_{ck}")
            egfc = V_(lambda e, ck=ck, gfc=gfc: e.tensor_copy(
                out=Gfm[:, :, ck, :],
                in_=PW2[0:NH, gfc:gfc + 128].rearrange("p (b t) -> p b t", b=B)),
                waits=[eF], tag=f"Gfc{it}_{ck}")
            bank_read("PW2", egfc)
            evba[("gfm", ck)] = egfc
            evba[("gfmfree", ck + 1)] = egfc
        Gl = [Gpos[0:16].rearrange("p ck b r -> p b ck r"), Gl1s[:],
              Gpos[32:48].rearrange("p ck b r -> p b ck r")]
        Gn = [Gneg[0:16].rearrange("p ck b r -> p b ck r"), Gn1s[:],
              Gneg[32:48].rearrange("p ck b r -> p b ck r")]
        gsrc = Gfm[:].rearrange("p b c t -> p (b c t)")
        gl = ev_wfree.get("Gs")
        l1 = V_(lambda e: e.tensor_copy(
            out=Gl[0], in_=Gfm[:]),
            waits=[evba[("gfm", 1)], gl], tag=f"lv1{it}")
        r1 = V_(lambda e: e.tensor_tensor(
            out=Glv1[:].rearrange("p b (ck r) -> p b ck r", ck=2), in0=Gfm[:],
            in1=Gl[0], op=OP.subtract),
            waits=[l1], tag=f"rs1{it}")
        l2 = V_(lambda e: e.tensor_copy(
            out=Gl[1].rearrange("p b t -> p (b t)"),
            in_=Glv1[:].rearrange("p b t -> p (b t)")), waits=[r1], tag=f"lv2{it}")
        r2 = V_(lambda e: e.tensor_tensor(
            out=Glv2[:].rearrange("p b t -> p (b t)"),
            in0=Glv1[:].rearrange("p b t -> p (b t)"),
            in1=Gl[1].rearrange("p b t -> p (b t)"), op=OP.subtract),
            waits=[l2, ev_wfree.get("Glv2")], tag=f"rs2{it}")
        l3 = V_(lambda e: e.tensor_copy(
            out=Gl[2],
            in_=Glv2[:].rearrange("p b (ck r) -> p b ck r", ck=2)), waits=[r2], tag=f"lv3{it}")
        evng = []
        for i in range(3):
            evng.append(V_(lambda e, i=i: e.tensor_scalar(
                out=(Gn[i] if i != 1 else Gn[i].rearrange("p b (ck r) -> p b ck r", ck=2)),
                in0=(Gl[i] if i != 1 else Gl[i].rearrange("p b (ck r) -> p b ck r", ck=2)),
                scalar1=-1.0, scalar2=None, op0=OP.mult),
                waits=[l3], tag=f"ngl{it}_{i}"))
        ng = evng[-1]
        ef1 = S_(lambda e: e.activation(
            out=expGfm[:].rearrange("p b t -> p (b t)"), in_=gsrc, func=AFT.Exp),
            waits=[evba[("gfm", 1)], ev_wfree.get("expGfm")], tag=f"egf{it}")
        ec1 = V_(lambda e: e.tensor_tensor(
            out=Glv1[:].rearrange("p b (c t) -> p b c t", c=2),
            in0=Gfm[:, :, :, 63:64].broadcast_to([NH, B, 2, 64]),
            in1=Gfm[:], op=OP.subtract), waits=[evba[("gfm", 1)], l3], tag=f"gcf{it}")
        ec2 = S_(lambda e: e.activation(
            out=eGceFM[:].rearrange("p b t -> p (b t)"),
            in_=Glv1[:].rearrange("p b t -> p (b t)"), func=AFT.Exp),
            waits=[ec1, ev_wfree.get("eGceFM")], tag=f"gce2{it}")
        evlb = []
        for ck in range(2):
            evlb.append(T_(lambda e, ck=ck: e.transpose(
                PW2[0:NH, ck * 128:ck * 128 + 128],
                lnb_tm[:, ck, :],
                identf[:]),
                waits=[evba[("lnb", 0)], evba[("lnb", 1)],
                       ev_wfree.get("pw2")] + (bank_open("PW2") if ck == 0 else []),
                tag=f"lbt{it}_{ck}"))
        elnb = V_(lambda e: e.tensor_copy(
            out=lnbfm[:].rearrange("p b (c r) -> p b c r", c=2),
            in_=PW2[0:NH, 0:256].rearrange("p (c b r) -> p b c r", c=2, b=B)),
            waits=[evlb[-1], ec2], tag=f"lbc{it}")
        bank_read("PW2", elnb)
        lb1 = V_(lambda e: e.tensor_copy(
            out=Lb[0][:].rearrange("p b t -> p (b t)"),
            in_=lnbfm[:].rearrange("p b t -> p (b t)")), waits=[elnb], tag=f"lb1{it}")
        lb2 = V_(lambda e: e.tensor_tensor(
            out=Lb[1][:].rearrange("p b t -> p (b t)"),
            in0=lnbfm[:].rearrange("p b t -> p (b t)"),
            in1=Lb[0][:].rearrange("p b t -> p (b t)"), op=OP.subtract),
            waits=[lb1], tag=f"lb2{it}")
        ev_wfree["Glv2"] = elnb
        ev_Gs = lb2
        ev_asm = []
        for bb_ in range(B):
            ev_asm.append(dma("gpsimd", "gdf", Gpos[16:32, :, bb_, :],
                              Gl1s[:, bb_, :], waits=[l3, ev_wfree.get("rhs")],
                              tag=f"asmP{it}{bb_}"))
            ev_asm.append(dma("gpsimd", "gdf", Gneg[16:32, :, bb_, :],
                              Gn1s[:, bb_, :], waits=[ng, ev_wfree.get("rhs")],
                              tag=f"asmN{it}{bb_}"))
            for i in range(2):
                ev_asm.append(dma("gpsimd", "gdf",
                                  Gneg[54 + 16 * i:70 + 16 * i, :, bb_, :],
                                  Lb[i][:, bb_, :], waits=[lb2, ev_wfree.get("rhs")],
                                  tag=f"asmL{it}_{i}{bb_}"))
        ev_flats = list(ev_asm)
        for ck in range(2):
            for b in range(B):
                base = 48 if b == 0 else 53
                ev_flats.append(dma(
                    "gpsimd", "gdf", rhsA[base:base + 3, ck, :],
                    Gneg[0:48, ck, b, :],
                    waits=[ev_asm[-1]], tag=f"fA{it}_{ck}_{b}"))
                ev_flats.append(dma(
                    "gpsimd", "gdf", rhsA[base + 3:base + 5, ck, :],
                    Gneg[54:86, ck, b, :],
                    waits=[ev_asm[-1]], tag=f"fAl{it}_{ck}_{b}"))
                b3 = 48 if b == 0 else 51
                ev_flats.append(dma(
                    "gpsimd", "gdf", rhs3[b3:b3 + 3, ck, :],
                    Gpos[0:48, ck, b, :],
                    waits=[ev_asm[-1]], tag=f"f3{it}_{ck}_{b}"))
        ev_flats_done = ev_flats[-1]

        evq = []
        for ic in range(NCC):
            half = (ic % 2) * 512
            evb = T_(lambda e, ic=ic, half=half: e.matmul(
                PW2[:, half:half + FW], hmaskT[:, ic, :],
                expGfm[:].rearrange("p b t -> p (b t)"),
                start=True, stop=True),
                waits=[ef1, ev_sc_done["q"]] + bank_open("PW2"),
                tag=f"bq{it}{ic}")
            evq.append(V_(lambda e, ic=ic, half=half: e.tensor_tensor(
                out=qtil[:, ic, :, :].rearrange("p b t -> p (b t)"),
                in0=qhat[:, ic, :, :].rearrange("p b t -> p (b t)"),
                in1=PW2[:, half:half + FW], op=OP.mult),
                waits=[evb, ev_wfree.get("qtil")], tag=f"qt{it}{ic}"))
            bank_read("PW2", evq[-1])
        ev_pre["qtil"] = evq[-1]
        k2prod = dvb[:].rearrange("p c w -> p (c w)").rearrange(
            "p (a b t) -> p a b t", a=NCC, b=B)
        evk = []
        for ic in range(NCC):
            half = (ic % 2) * 512
            evb = T_(lambda e, ic=ic, half=half: e.matmul(
                PW2[:, half:half + FW], hmaskT[:, ic, :],
                eGceFM[:].rearrange("p b t -> p (b t)"),
                start=True, stop=True),
                waits=[ec2, ev_sc_done["k"]] + bank_open("PW2"),
                tag=f"bk{it}{ic}")
            evk.append(V_(lambda e, ic=ic, half=half: e.tensor_tensor(
                out=k2prod[:, ic, :, :],
                in0=khat[:, ic, :, :],
                in1=PW2[:, half:half + FW].rearrange("p (b t) -> p b t", b=B),
                op=OP.mult),
                waits=[evb, ev_wfree.get("k2prod")], tag=f"k2{it}{ic}"))
            bank_read("PW2", evk[-1])
        evkt = []
        for b in range(B):
            for ck in range(2):
                for g2 in range(2):
                    evt = None
                    for q4 in range(4):
                        ic = g2 * 4 + q4
                        evt = T_(lambda e, b=b, ck=ck, ic=ic, q4=q4: e.transpose(
                            PW2b[0:64, q4 * 128:q4 * 128 + 128],
                            k2prod[:, ic, b, ck * 64:ck * 64 + 64],
                            ident[:]),
                            waits=[evk[-1]] + (bank_open("PW2") if q4 == 0 else []),
                            tag=f"k2t{it}_{b}{ck}{ic}")
                    ekt = S_(lambda e, b=b, ck=ck, g2=g2: e.activation(
                        out=k2tm[64 * b:64 * b + 64, ck, g2 * 512:g2 * 512 + 512],
                        in_=PW2b[0:64, 0:512], func=AFT.Copy),
                        waits=[evt, ev_wfree.get("k2tm")], tag=f"k2e{it}_{b}{ck}{g2}")
                    evkt.append(ekt)
                    bank_read("PW2", ekt)
        ev_pre["k2tm"] = evkt[-1]

        evD = {}
        ev_po_evac = {}
        prev_serial = []
        for ck in range(2):
            wA = [ev_Gs, ev_flats_done, ev_const] + bank_open("PW1")
            for hw_ in range(2):
                mA1 = T_(lambda e, ck=ck, hw_=hw_: e.matmul(
                    PW1[:, hw_ * 512:hw_ * 512 + 512],
                    Gpos[:, ck, :, :],
                    rhsA[:, ck, hw_ * 512:hw_ * 512 + 512], start=True, stop=False),
                    waits=wA, tag=f"dgA{it}_{ck}_{hw_}")
                mA2 = T_(lambda e, hw_=hw_: e.matmul(
                    PW1[:, hw_ * 512:hw_ * 512 + 512], ident[:],
                    maskA[:, 0:512],
                    start=False, stop=True), waits=[mA1], tag=f"dgAm{it}_{ck}_{hw_}")
            x1 = S_(lambda e: e.activation(out=Dexp1[:], in_=PW1[:, 0:WIDE],
                                           func=AFT.Exp),
                    waits=[mA2, evD.get(("d1f", ck))] + prev_serial,
                    tag=f"dxA{it}_{ck}")
            bank_read("PW1", x1)
            w2 = [x1] + bank_open("PW1")
            for hw_ in range(2):
                m21 = T_(lambda e, ck=ck, hw_=hw_: e.matmul(
                    PW1[:, hw_ * 512:hw_ * 512 + 512],
                    Gneg[0:54, ck, :, :],
                    rhs3[0:54, ck, hw_ * 512:hw_ * 512 + 512], start=True, stop=False),
                    waits=w2, tag=f"dg2{it}_{ck}_{hw_}")
                m22 = T_(lambda e, hw_=hw_: e.matmul(
                    PW1[:, hw_ * 512:hw_ * 512 + 512], ident[:],
                    mask2[:, 0:512],
                    start=False, stop=True), waits=[m21], tag=f"dg2m{it}_{ck}_{hw_}")
            x2 = S_(lambda e: e.activation(out=Dexp2[:], in_=PW1[:, 0:WIDE],
                                           func=AFT.Exp),
                    waits=[m22, evD.get(("d2f", ck))] + prev_serial,
                    tag=f"dx2{it}_{ck}")
            bank_read("PW1", x2)
            w3 = [x2] + bank_open("PW1")
            for hw_ in range(2):
                m31 = T_(lambda e, ck=ck, hw_=hw_: e.matmul(
                    PW1[:, hw_ * 512:hw_ * 512 + 512],
                    Gneg[:, ck, :, :],
                    rhs3[:, ck, hw_ * 512:hw_ * 512 + 512], start=True, stop=False),
                    waits=w3, tag=f"dg3{it}_{ck}_{hw_}")
                m32 = T_(lambda e, hw_=hw_: e.matmul(
                    PW1[:, hw_ * 512:hw_ * 512 + 512], ident[:],
                    mask3[:, 0:512],
                    start=False, stop=True), waits=[m31], tag=f"dg3m{it}_{ck}_{hw_}")
            x3 = S_(lambda e: e.activation(out=Dexp3[:], in_=PW1[:, 0:WIDE],
                                           func=AFT.Exp),
                    waits=[m32, evD.get(("d3f", ck))] + prev_serial,
                    tag=f"dx3{it}_{ck}")
            bank_read("PW1", x3)
            evkk = []
            for s in range(NH):
                par, j = s // 8, s % 8
                for b in range(B):
                    evkk.append(T_(lambda e, ck=ck, s=s, par=par, j=j, b=b: e.matmul(
                        PW2[64 * b:64 * b + 64, s * 64:s * 64 + 64],
                        khat[64 * par:64 * par + 64, j, b, ck * 64:ck * 64 + 64],
                        khat[64 * par:64 * par + 64, j, b, ck * 64:ck * 64 + 64],
                        start=True, stop=True, tile_position=(64 * par, 64 * b)),
                        waits=[ev_pre["khat"], ev_pre["vtm"]]
                        + (bank_open("PW2") if (s == 0 and b == 0) else []),
                        tag=f"kk{it}_{ck}_{s}_{b}"))
            eB = V_(lambda e, ck=ck: e.scalar_tensor_tensor(
                out=Bw[:], in0=PW2[:, 0:WIDE], scalar=-1.0, in1=Dexp1[:],
                op0=OP.mult, op1=OP.mult),
                waits=[evkk[-1], x1, ev_wfree.get("Bw")], tag=f"B{it}_{ck}")
            bank_read("PW2", eB)
            eBT = V_(lambda e, ck=ck: e.scalar_tensor_tensor(
                out=BTw[:], in0=PW2[:, 0:WIDE], scalar=-1.0, in1=Dexp3[:],
                op0=OP.mult, op1=OP.mult),
                waits=[x3, ev_wfree.get("BTw")], tag=f"BT{it}_{ck}")
            bank_read("PW2", eBT)
            evD[("d1f", ck + 1)] = eB
            evD[("d3f", ck + 1)] = eBT
            evqk = []
            for s in range(NH):
                par, j = s // 8, s % 8
                for b in range(B):
                    evqk.append(T_(lambda e, ck=ck, s=s, par=par, j=j, b=b: e.matmul(
                        PW1[64 * b:64 * b + 64, s * 64:s * 64 + 64],
                        khat[64 * par:64 * par + 64, j, b, ck * 64:ck * 64 + 64],
                        qhat[64 * par:64 * par + 64, j, b, ck * 64:ck * 64 + 64],
                        start=True, stop=True, tile_position=(64 * par, 64 * b)),
                        waits=[ev_pre["qhat"], x3]
                        + (bank_open("PW1") if (s == 0 and b == 0) else []),
                        tag=f"qk{it}_{ck}_{s}_{b}"))
            eM = V_(lambda e, ck=ck: e.tensor_tensor(
                out=Mmw[:], in0=PW1[:, 0:WIDE], in1=Dexp2[:], op=OP.mult),
                waits=[evqk[-1], x2, ev_wfree.get("Mmw")], tag=f"Mm{it}_{ck}")
            bank_read("PW1", eM)
            evD[("d2f", ck + 1)] = eM
            evD[("B", ck)] = eB
            evD[("BT", ck)] = eBT
            evD[("Mm", ck)] = eM
            evD[("pw1f", ck + 1)] = eM
            evD[("pw2f", ck + 1)] = eBT

            evp = []
            for s in range(NH):
                par, j = s // 8, s % 8
                for b in range(B):
                    evp.append(T_(lambda e, s=s, par=par, j=j, b=b, ck=ck: e.matmul(
                        PW1[64 * b:64 * b + 64, s * 64:s * 64 + 64],
                        khat[64 * par:64 * par + 64, j, b, ck * 64:ck * 64 + 64],
                        Sb16[64 * par:64 * par + 64, b, j, :],
                        start=True, stop=True, tile_position=(64 * par, 64 * b)),
                        waits=[ev_state_Sb, eM, ev_pre["khat"]]
                        + (bank_open("PW1") if (s == 0 and b == 0) else []),
                        tag=f"pr{it}_{ck}_{s}_{b}"))
            ev_last_pr = evp[-1]
            eu1 = V_(lambda e, ck=ck: e.tensor_tensor(
                out=tscr[:].rearrange("p (h v) -> p h v", h=NH),
                in0=PW1[:, 0:WIDE].rearrange("p (h v) -> p h v", h=NH),
                in1=expG_tm[:, ck, :].unsqueeze(2).broadcast_to([128, NH, 64]),
                op=OP.mult), waits=[evp[-1], evba[("expG", ck)],
                                    ev_wfree.get("oxs")], tag=f"u1{it}_{ck}")
            bank_read("PW1", eu1)
            eu2 = V_(lambda e, ck=ck: e.tensor_tensor(
                out=uh[:, ck, :].rearrange("p (par j d) -> p par j d", par=2, j=8),
                in0=vtm[:, ck, :].rearrange("p (j par d) -> p par j d", j=8, par=2),
                in1=tscr[:].rearrange("p (par j d) -> p par j d", par=2, j=8),
                op=OP.subtract),
                waits=[eu1, ev_pre["vtm"], ev_uh_use], tag=f"u2{it}_{ck}")
            ev_uh_w = eu2

            Msrc = [Bw[:], McT[:], Mx[:], McT[:], Mx[:]]
            Nsrc = [BTw[:], Ncr[:], Nx[:], Ncr[:], Nx[:], Ncr[:]]
            evMw = {0: evD[("B", ck)]}
            evNw = {0: evD[("BT", ck)]}
            ycur, ev_y = uh[:, ck, :], eu2
            ev_pw1_reader = eu1
            ev_pw2_reader = eBT
            for l in range(6):
                evs = []
                for b in range(B):
                    for h in range(NH):
                        evs.append(T_(lambda e, b=b, h=h, l=l, N=Nsrc[l], yc=ycur: e.matmul(
                            PW1[64 * b:64 * b + 64, h * 64:h * 64 + 64],
                            N[64 * b:64 * b + 64, h * 64:h * 64 + 64],
                            yc[64 * b:64 * b + 64, h * 64:h * 64 + 64],
                            start=True, stop=True, tile_position=(64 * b, 64 * b)),
                            waits=[evNw[l], ev_y]
                            + (bank_open("PW1") if (b == 0 and h == 0) else []),
                            tag=f"ap{it}_{ck}_{l}_{b}_{h}"))
                ynext = (yv if l % 2 == 0 else yv2)
                ev_y2 = V_(lambda e, yn=ynext, yc=ycur: e.tensor_tensor(
                    out=yn[:], in0=PW1[:, 0:WIDE], in1=yc, op=OP.add),
                    waits=[evs[-1], ev_wfree.get("yv")], tag=f"ye{it}_{ck}_{l}")
                bank_read("PW1", ev_y2)
                ycur, ev_y = ynext[:], ev_y2
                if l < 5:
                    evsqN = []
                    for b in range(B):
                        for h in range(NH):
                            evsqN.append(T_(lambda e, b=b, h=h, M=Msrc[l], N=Nsrc[l]: e.matmul(
                                PW2[64 * b:64 * b + 64, h * 64:h * 64 + 64],
                                M[64 * b:64 * b + 64, h * 64:h * 64 + 64],
                                N[64 * b:64 * b + 64, h * 64:h * 64 + 64],
                                start=True, stop=True, tile_position=(64 * b, 64 * b)),
                                waits=[evMw[l], evNw[l]]
                                + (bank_open("PW2") if (b == 0 and h == 0) else []),
                                tag=f"sqN{it}_{ck}_{l}_{b}_{h}"))
                    eNev = S_(lambda e, dstN=Nsrc[l + 1]: e.activation(
                        out=dstN, in_=PW2[:, 0:WIDE], func=AFT.Copy),
                        waits=[evsqN[-1], evNw.get(l - 1)], tag=f"Ne{it}_{ck}_{l}")
                    evNw[l + 1] = eNev
                    bank_read("PW2", eNev)
                    if l < 4:
                        evsqM = []
                        for b in range(B):
                            for h in range(NH):
                                evsqM.append(T_(lambda e, b=b, h=h, M=Msrc[l], N=Nsrc[l]: e.matmul(
                                    PW2[64 * b:64 * b + 64, h * 64:h * 64 + 64],
                                    N[64 * b:64 * b + 64, h * 64:h * 64 + 64],
                                    M[64 * b:64 * b + 64, h * 64:h * 64 + 64],
                                    start=True, stop=True, tile_position=(64 * b, 64 * b)),
                                    waits=[eNev]
                                    + (bank_open("PW2") if (b == 0 and h == 0) else []),
                                    tag=f"sqM{it}_{ck}_{l}_{b}_{h}"))
                        eMev = S_(lambda e, dstM=Msrc[l + 1]: e.activation(
                            out=dstM, in_=PW2[:, 0:WIDE], func=AFT.Copy),
                            waits=[evsqM[-1], evMw.get(l - 1)], tag=f"Me{it}_{ck}_{l}")
                        evMw[l + 1] = eMev
                        bank_read("PW2", eMev)
            ev_wfree["yv"] = None
            edv = V_(lambda e, ck=ck, yc=ycur: e.tensor_tensor(
                out=dvb[:, ck, :].rearrange("p (h v) -> p h v", h=NH),
                in0=yc.rearrange("p (h v) -> p h v", h=NH),
                in1=beta_tm[:, ck, :].unsqueeze(2).broadcast_to([128, NH, 64]),
                op=OP.mult), waits=[ev_y, ev_pre["k2tm"], ev_wfree.get("dvb")],
                tag=f"dv{it}_{ck}")

            evo = []
            for s in range(NH):
                par, j = s // 8, s % 8
                for b in range(B):
                    evo.append(T_(lambda e, s=s, par=par, j=j, b=b, ck=ck: e.matmul(
                        PO[64 * par:64 * par + 64, b * 512 + j * 64:b * 512 + j * 64 + 64],
                        dvb[64 * b:64 * b + 64, ck, s * 64:s * 64 + 64],
                        Mmw[64 * b:64 * b + 64, s * 64:s * 64 + 64],
                        start=True, stop=True, tile_position=(64 * b, 64 * par)),
                        waits=[edv, eM]
                        + (bank_open("PO") if (s == 0 and b == 0) else []),
                        tag=f"oi{it}_{ck}_{s}_{b}"))
            evox = []
            for s in range(NH):
                par, j = s // 8, s % 8
                for b in range(B):
                    evox.append(T_(lambda e, par=par, j=j, b=b, ck=ck: e.matmul(
                        PW1[64 * par:64 * par + 64, j * 128 + b * 64:j * 128 + b * 64 + 64],
                        Sb16[64 * par:64 * par + 64, b, j, :],
                        qtil[64 * par:64 * par + 64, j, b, ck * 64:ck * 64 + 64],
                        start=True, stop=True, tile_position=(64 * par, 64 * par)),
                        waits=[ev_state_Sb, ev_pre["qtil"]]
                        + (bank_open("PW1") if (s == 0 and b == 0) else []),
                        tag=f"ox{it}_{ck}_{s}_{b}"))
            oxs = oxsv
            ev_oxe = S_(lambda e, oxs=oxs: e.activation(
                out=oxs, in_=PW1[:, 0:1024], func=AFT.Copy),
                waits=[evox[-1], eu2, ev_wfree.get("oxs")],
                tag=f"oxe{it}_{ck}")
            bank_read("PW1", ev_oxe)
            ev_po_evac[ck + 1] = V_(lambda e, ck=ck, oxs=oxs: e.tensor_tensor(
                out=qhat[:, :, :, ck * 64:ck * 64 + 64],
                in0=PO[:, 0:1024].rearrange("p (b j t) -> p j b t", b=B, j=NCC),
                in1=oxs.rearrange("p (j b t) -> p j b t", j=NCC, b=B),
                op=OP.add),
                waits=[evo[-1], ev_oxe, ev_wfree.get("obuf")], tag=f"oe{it}_{ck}")
            bank_read("PO", ev_po_evac[ck + 1])
            ev_wfree["oxs"] = ev_po_evac[ck + 1]

            egr = []
            for b in range(B):
                egr.append(dma("gpsimd", "gdr", rowb[0:1, b, :],
                               G_tm[64 * b + 63:64 * b + 64, ck, :],
                               waits=[ecp, ev_wfree.get("rowbfree")],
                               tag=f"rb{it}_{ck}_{b}"))
            eeg = S_(lambda e: e.activation(
                out=rowbb[0:1].rearrange("p par (b j) -> p par b j", b=B),
                in_=rowb[0:1, :, :].rearrange("p b (par j) -> p par b j", par=2),
                func=AFT.Exp),
                waits=[egr[-1], ev_wfree.get("dwfree")], tag=f"egc{it}_{ck}")
            ev_wfree["rowbfree"] = eeg
            evba[("eGCp", ck)] = eeg
            edw = None
            for par in range(2):
                edw = T_(lambda e, par=par: e.matmul(
                    PW1[64 * par:64 * par + 64, 0:B * NCC],
                    onesrow[0:1, 0:64],
                    rowbb[0:1, par, :],
                    start=True, stop=True, tile_position=(0, 64 * par)),
                    waits=[evba[("eGCp", ck)]]
                    + (bank_open("PW1") if par == 0 else []),
                    tag=f"dw{it}_{ck}_{par}")
            ev_wfree["dwfree"] = edw
            es1 = V_(lambda e, ck=ck: e.tensor_tensor(
                out=S32b[:].rearrange("p (a b) -> p a b", a=B * NCC),
                in0=S32[:].rearrange("p a b c -> p (a b) c"),
                in1=PW1[:, 0:B * NCC].unsqueeze(2).broadcast_to([128, B * NCC, DH]),
                op=OP.mult), waits=[edw, ev_state_S, ev_po_evac[ck + 1],
                                    ev_wfree.get("oxs")], tag=f"sd{it}_{ck}")
            bank_read("PW1", es1)
            evst = []
            for s in range(NH):
                par, j = s // 8, s % 8
                hn = 2 * j + par
                for b in range(B):
                    evst.append(T_(lambda e, s=s, par=par, j=j, hn=hn, b=b, ck=ck: e.matmul(
                        PW2[64 * par:64 * par + 64, b * 512 + j * 64:b * 512 + j * 64 + 64],
                        k2tm[64 * b:64 * b + 64, ck, hn * 64:hn * 64 + 64],
                        dvb[64 * b:64 * b + 64, ck, s * 64:s * 64 + 64],
                        start=True, stop=True, tile_position=(64 * b, 64 * par)),
                        waits=[edv, ev_pre["k2tm"]]
                        + (bank_open("PW2") if (s == 0 and b == 0) else []),
                        tag=f"st{it}_{ck}_{s}_{b}"))
            es2b = V_(lambda e: e.tensor_tensor(
                out=S32[:].rearrange("p a b c -> p (a b c)"),
                in0=S32b[:, 0:1024],
                in1=PW2[:, 0:1024], op=OP.add),
                waits=[es1, evst[-1]], tag=f"sa{it}_{ck}")
            bank_read("PW2", es2b)
            ev_state_S = es2b
            ev_state_Sb = S_(lambda e: e.activation(
                out=Sb16[:].rearrange("p a b c -> p (a b c)"),
                in_=S32[:].rearrange("p a b c -> p (a b c)"), func=AFT.Copy),
                waits=[es2b], tag=f"sb{it}_{ck}")
            ev_pw2_reader = es2b
            evD[("pw2f", ck + 1)] = es2b
            evD[("pw1f", ck + 1)] = ev_pw1_reader
            prev_serial = [ev_y, edv]
        ev_wfree["dvb"] = None
        ev_wfree["Bw"] = ev_y
        ev_wfree["BTw"] = ev_y
        ev_wfree["Mmw"] = ev_po_evac[2]
        ev_wfree["vtm"] = ev_uh_w
        ev_wfree["k2tm"] = ev_state_Sb
        ev_obuf_done = ev_po_evac[2]

        eo2 = V_(lambda e: e.tensor_tensor(
            out=osq[:].rearrange("p a b t -> p (a b t)"),
            in0=qhat[:].rearrange("p a b t -> p (a b t)"),
            in1=qhat[:].rearrange("p a b t -> p (a b t)"), op=OP.mult),
            waits=[ev_obuf_done, ev_state_S], tag=f"o2{it}")
        evmm = None
        for ic in range(NCC):
            evmm = T_(lambda e, ic=ic: e.matmul(
                PW2[0:16, 0:FW], hmask[:, ic, 0:16],
                osq[:, ic, :, :].rearrange("p b t -> p (b t)"),
                start=(ic == 0), stop=(ic == NCC - 1)),
                waits=[eo2] + (bank_open("PW2") if ic == 0 else []),
                tag=f"os{it}{ic}",
                inc="tp" if ic == NCC - 1 else None)
        eol = S_(lambda e: e.activation(
            out=osc[0:16, :], in_=PW2[0:16, 0:FW], func=AFT.Ln,
            bias=epso[0:16, 0:1], scale=1.0 / 64.0), waits=[evmm], tag=f"ol{it}")
        bank_read("PW2", eol)
        eoe = S_(lambda e: e.activation(
            out=osc[0:16, :], in_=osc[0:16, :], func=AFT.Exp, scale=-0.5),
            waits=[eol], tag=f"oe2{it}")
        evon = []
        for ic in range(NCC):
            half = (ic % 2) * 512
            evb = T_(lambda e, ic=ic, half=half: e.matmul(
                PW1[:, half:half + FW], hmaskT[:, ic, :], osc[0:16, :],
                start=True, stop=True),
                waits=[eoe] + bank_open("PW1"),
                tag=f"ob{it}{ic}")
            ev1 = V_(lambda e, ic=ic, half=half: e.tensor_tensor(
                out=osq[:, ic, :, :].rearrange("p b t -> p (b t)"),
                in0=qhat[:, ic, :, :].rearrange("p b t -> p (b t)"),
                in1=PW1[:, half:half + FW], op=OP.mult),
                waits=[evb], tag=f"on{it}{ic}")
            bank_read("PW1", ev1)
            evon.append(V_(lambda e, ic=ic: e.tensor_tensor(
                out=qhat[:, ic, :, :].rearrange("p b t -> p (b t)"),
                in0=osq[:, ic, :, :].rearrange("p b t -> p (b t)"),
                in1=gatebuf[:, ic, :, :].rearrange("p b t -> p (b t)"), op=OP.mult),
                waits=[ev1, ev_gate_done], tag=f"og{it}{ic}"))
        ev_ofinal = evon[-1]

        for oc in range(NCC):
            pocol = ((oc // 2) % 2) * 512 + (oc % 2) * FW
            ex = dma("sync", "sdf0" if oc % 2 == 0 else "sdf1", xfm[:, oc % 2, :].rearrange("p (b t) -> p b t", b=B),
                     x_fm_d[:, oc * 128:(oc + 1) * 128, t0:t0 + TILE].rearrange("b p t -> p b t"),
                     waits=[ev_laststore.get((it - 1, oc)), ev_laststore.get((it, oc - 2))],
                     tag=f"xf{it}{oc}")
            evmm = None
            for kc in range(NCC):
                evmm = T_(lambda e, oc=oc, kc=kc, pocol=pocol: e.matmul(
                    PO[:, pocol:pocol + FW], W["o"][:, kc, oc * 128:(oc + 1) * 128],
                    qhat[:, kc, :, :].rearrange("p b t -> p (b t)"),
                    start=(kc == 0), stop=(kc == NCC - 1)),
                    waits=[ev_ofinal, ev_wcast[("o", NCC - 1)]]
                    + (bank_open("PO") if kc == 0 else []),
                    tag=f"wo{it}{oc}_{kc}", inc="tp" if kc == NCC - 1 else None)
            er1 = V_(lambda e, oc=oc, pocol=pocol: e.scalar_tensor_tensor(
                out=resid[:, oc % 2, :], in0=PO[:, pocol:pocol + FW], scalar=1.0,
                in1=ffng[:, oc, :, :].rearrange("p b t -> p (b t)"),
                op0=OP.mult, op1=OP.mult),
                waits=[evmm, ev_ffn_done, ev_state_Sb, ev_ofinal,
                       ev_laststore.get((it, oc - 2)) or ev_laststore.get((it - 1, oc + 6))],
                tag=f"r1{it}{oc}")
            bank_read("PO", er1)
            er2 = V_(lambda e, oc=oc: e.tensor_tensor(
                out=resid[:, oc % 2, :], in0=resid[:, oc % 2, :],
                in1=xfm[:, oc % 2, :], op=OP.add),
                waits=[er1, ex], tag=f"r2{it}{oc}")
            ev_laststore[(it, oc)] = dma(
                "sync", "sdo0" if oc % 2 == 0 else "sdo1", y_fm_d[:, oc * 128:(oc + 1) * 128, t0:t0 + TILE].rearrange("b p t -> p b t"),
                resid[:, oc % 2, :].rearrange("p (b t) -> p b t", b=B),
                waits=[er2], tag=f"st{it}{oc}")
        ev_wfree["gate"] = er1
        ev_wfree["ffn"] = er1
        ev_wfree["oxs"] = [ev_laststore[(it, NCC - 2)], ev_laststore[(it, NCC - 1)]]
        ev_wfree["k2prod"] = [evo[-1], evst[-1]]
        ev_wfree["qhat"] = evmm
        ev_wfree["obuf"] = evmm
        ev_wfree["qtil"] = ev_po_evac[2]
        ev_Tend = P.last_ev["tensor"]
        ev_wfree["Tend"] = ev_Tend
        ev_wfree["Gs"] = ev_Tend
        ev_wfree["rhs"] = ev_Tend
        ev_wfree["expGfm"] = ev_Tend
        ev_wfree["eGceFM"] = ev_Tend
        ev_wfree["qsc"] = [ev_sc_done["q"][1], ev_sc_done["k"][1]]
        ev_wfree["tmsmalls"] = ev_Tend
        ev_wfree["khat"] = ev_last_pr
        ev_wfree["PO"] = ev_po_evac[2]
        ev_wfree["pw2"] = ev_state_Sb

    def _finalwait(e):
        last = None
        for snm in DMA_SEMS:
            n = P.final_cnt.get(snm, 0)
            if n:
                last = e.wait_ge(sems[snm], n * 16)
        return last
    P.stream = ("F",)
    P.op("sync", _finalwait, tag="final_wait")

    emit_groups = [[("P",), ("L", 0)]]
    for t in range(NT):
        g = [("S", t)]
        if t + 1 < NT:
            g.append(("L", t + 1))
        emit_groups.append(g)
    emit_groups.append([("F",)])

    blk = es.enter_context(nc.Block())
    P.emit(blk, sems, groups=emit_groups)
    es.close()
    return nc



_PROG_CACHE = {}


def _get_prog(T):
    if T not in _PROG_CACHE:
        cfg = CFG(T)
        _PROG_CACHE[T] = (cfg, build_program(cfg))
    return _PROG_CACHE[T]


def make_in_maps(cfg, inputs):
    x = np.ascontiguousarray(inputs["L0_attn_hidden"], dtype=np.float32)
    NBAT = x.shape[0]
    T = x.shape[1]
    consts = host_constants(cfg)

    def t(a):
        return np.ascontiguousarray(np.asarray(a, dtype=np.float32).T)

    shared = {
        "w_qt": t(inputs["L0_attn_Wq"]), "w_kt": t(inputs["L0_attn_Wk"]),
        "w_vt": t(inputs["L0_attn_Wv"]), "w_gt": t(inputs["L0_attn_Wg"]),
        "w_ot": t(inputs["L0_attn_Wo"]), "w_ft": t(inputs["L0_ffn_weight"]),
        "wbat": np.ascontiguousarray(
            np.concatenate([t(inputs["L0_attn_Wb"])[:, SIG],
                            t(inputs["L0_attn_Wa"])[:, SIG]], axis=1)),
        "alog": np.asarray(inputs["L0_attn_A_log"], np.float32)[SIG].reshape(1, -1),
        "dtrep": np.tile(np.asarray(inputs["L0_attn_dt_bias"], np.float32)[SIG][None, :],
                         (128, 1)),
        "normw2": np.tile(np.asarray(inputs["L0_attn_norm_w"], np.float32), 2)[:, None].copy(),
    }
    cw = np.stack([np.asarray(inputs["L0_attn_conv_q"], np.float32),
                   np.asarray(inputs["L0_attn_conv_k"], np.float32),
                   np.asarray(inputs["L0_attn_conv_v"], np.float32)])
    shared["convw"] = np.ascontiguousarray(
        cw.reshape(3, cfg.NCC, 128, 4).transpose(2, 0, 1, 3))
    for nm, a in consts.items():
        shared["c_" + nm] = np.ascontiguousarray(a)

    in_maps = []
    for ci in range(NCORES):
        m = dict(shared)
        xs = x[2 * ci:2 * ci + 2]
        m["x_tm"] = np.ascontiguousarray(xs)
        m["x_fm"] = np.ascontiguousarray(xs.transpose(0, 2, 1))
        in_maps.append(m)
    return in_maps


def kernel(**inputs):
    x = inputs["L0_attn_hidden"]
    NBAT, T, H = x.shape
    cfg, nc = _get_prog(T)
    in_maps = make_in_maps(cfg, inputs)
    res = run_bass_kernel_spmd(nc, in_maps, core_ids=list(range(NCORES)))
    outs = []
    for ci in range(NCORES):
        y = np.asarray(res.results[ci]["y_fm"])
        outs.append(y.transpose(0, 2, 1))
    return np.ascontiguousarray(np.concatenate(outs, axis=0), dtype=np.float32)


# version 17
# speedup vs baseline: 1.0528x; 1.0528x over previous
import numpy as np
import ml_dtypes
import contextlib

import concourse.bass as bass
import concourse.mybir as mybir
from concourse.bass_utils import run_bass_kernel_spmd
from concourse.alu_op_type import AluOpType as OP

F32 = mybir.dt.float32
BF16 = mybir.dt.bfloat16
AFT = mybir.ActivationFunctionType
BF = ml_dtypes.bfloat16

NCORES = 8
ENGINES = ("sync", "gpsimd", "vector", "scalar", "tensor")
DMA_SEMS = {"sdc", "sdw", "sdw1", "sdx", "sdx1", "sdf0", "sdf1", "sdo0", "sdo1", "gdr", "gdf"}
SIG = np.array([(s % 8) * 2 + s // 8 for s in range(16)])


class CFG:
    def __init__(self, T=2048):
        self.T = T
        self.H = 1024
        self.NH = 16
        self.DH = 64
        self.B = 2
        self.TILE = 128
        self.NT = T // self.TILE
        self.NCC = 8
        self.FW = self.B * self.TILE
        self.WIDE = self.NH * self.DH


class Ev:
    __slots__ = ("sem", "val", "eng")
    def __init__(self, sem, val, eng):
        self.sem, self.val, self.eng = sem, val, eng


class Prog:
    def __init__(self):
        self.ops = {e: [] for e in ENGINES}
        self.seq = 0
        self.stream = ("P",)
        self.last_ev = {}
        self._lastfam = None
        self.final_cnt = {}

    def unit(self, sem):
        return 16 if sem in DMA_SEMS else 1

    def op(self, eng, fn, waits=(), inc=None, tag=""):
        ev = None
        if inc is not None:
            ev = Ev(inc, None, eng)
            self.last_ev[eng] = ev
        flat = []
        def _fl(w):
            if w is None:
                return
            if isinstance(w, (list, tuple)):
                for x in w:
                    _fl(x)
            else:
                flat.append(w)
        for w in waits:
            _fl(w)
        self.ops[eng].append((fn, tuple(flat), inc, ev, tag, self.seq, self.stream))
        self.seq += 1
        return ev

    _COST = (
        ("pj", 160), ("wo", 160), ("vt", 130), ("xnt", 130), ("k2t", 130),
        ("kk", 105), ("qk", 105), ("pr", 105), ("ap", 105), ("sqN", 105),
        ("sqM", 105), ("oi", 105), ("ox", 105), ("st", 105),
        ("dg", 380), ("G", 100), ("Gf", 120), ("ba", 120),
        ("bc", 120), ("bq", 120), ("bk", 120), ("ob", 120),
        ("ss", 150), ("os", 150), ("dw", 60),
        ("cv", 520), ("ye", 1150), ("B", 1150), ("BT", 1150), ("Mm", 1150),
        ("u1", 1200), ("u2", 1200), ("dv", 1150), ("oe", 1200),
        ("sd", 1150), ("sa", 1150), ("sc", 560), ("qt", 560), ("k2", 560),
        ("on", 560), ("og", 560), ("gn", 2300), ("lns", 450), ("lnq", 450),
        ("lna", 450), ("r1", 600), ("r2", 600),
        ("pe", 1250), ("sv", 1250), ("sl", 1800), ("sq", 1800), ("dx", 1250),
        ("k2e", 650), ("vte", 650), ("xne", 650), ("oxe", 1150),
        ("sb", 650), ("Ne", 1250), ("Me", 1250),
        ("xtm", 2500), ("xf", 2500), ("st", 2500), ("c_", 2000),
        ("wd", 2500), ("fA", 1200), ("f3", 1200), ("asm", 800), ("rb", 800),
    )

    _FAM = (
        ("sig", 1), ("sl", 1), ("sv", 1), ("lnq", 2),
        ("lnl", 0), ("lne", 0), ("re", 0), ("spe", 0), ("spl", 0),
        ("Ge", 0), ("egc", 0), ("gce2", 0), ("egf", 0), ("ol", 0),
        ("oe2", 0),
    )

    def _family(self, tag):
        for pfx, f in self._FAM:
            if tag.startswith(pfx):
                return f
        return None

    def _cost(self, tag, eng):
        if eng == "gpsimd" and tag[:2] in ("sq", "o2", "gn"):
            return 3500
        if eng in ("sync", "gpsimd"):
            return 900
        for pfx, c in self._COST:
            if tag.startswith(pfx):
                return c
        return 300

    def _final_order(self, groups):
        if groups is None:
            return {e: list(self.ops[e]) for e in ENGINES}
        sub = {e: {} for e in ENGINES}
        for e in ENGINES:
            for op in self.ops[e]:
                sub[e].setdefault(op[6], []).append(op)
        seen = set()
        for g in groups:
            seen.update(g)
        for e in ENGINES:
            for k in sub[e]:
                assert k in seen, f"stream {k} not in emission groups"
        final = {e: [] for e in ENGINES}
        evt = {}
        clock = {e: 0.0 for e in ENGINES}
        for g in groups:
            lists = {}
            for e in ENGINES:
                for k in g:
                    lst = sub[e].get(k)
                    if lst:
                        lists[(e, k)] = lst
            idxs = {key: 0 for key in lists}
            remaining = sum(len(v) for v in lists.values())
            while remaining:
                best, beststart = None, None
                for key, lst in lists.items():
                    i = idxs[key]
                    if i >= len(lst):
                        continue
                    op = lst[i]
                    ok = True
                    start = clock[key[0]]
                    for w in op[1]:
                        t = evt.get(id(w))
                        if t is None:
                            ok = False
                            break
                        if t > start:
                            start = t
                    if not ok:
                        continue
                    if key[0] == "scalar":
                        fam = self._family(op[4])
                        if fam is not None and fam != self._lastfam:
                            start += 1300
                    if beststart is None or start < beststart:
                        best, beststart = key, start
                assert best is not None, (
                    f"stream merge stalled in group {g}")
                e, k = best
                op = lists[best][idxs[best]]
                idxs[best] += 1
                remaining -= 1
                final[e].append(op)
                dur = self._cost(op[4], e)
                if e == "scalar":
                    fam = self._family(op[4])
                    if fam is not None:
                        self._lastfam = fam
                end = beststart + dur
                clock[e] = end
                if op[3] is not None:
                    evt[id(op[3])] = end
        return final

    def emit(self, block, sems, groups=None):
        import os as _os
        import re as _re
        stubk = int(_os.environ.get("STUBK", "-1"))
        stubtag = _os.environ.get("STUBTAG", "")
        stubwin = _os.environ.get("STUBWIN", "")
        wlo, whi = (int(v) for v in stubwin.split(":")) if stubwin else (0, 0)
        final = self._final_order(groups)
        cnt = {}
        for e in ENGINES:
            for op in final[e]:
                if op[2] is not None:
                    cnt[op[2]] = cnt.get(op[2], 0) + 1
                    op[3].val = cnt[op[2]]
        self.final_cnt = cnt
        def mk(ename):
            oplist = final[ename]
            def body(eng):
                waited = {}
                for fn, waits, inc, ev, tag, seq, stream in oplist:
                    stub = (stubk >= 0 and seq >= stubk) or \
                           (wlo <= seq < whi) or \
                           (stubtag and _re.match(stubtag, tag))
                    for w in waits:
                        assert w.val is not None, f"unnumbered wait in {tag}"
                        tgt = w.val * self.unit(w.sem)
                        if waited.get(w.sem, -1) >= tgt:
                            continue
                        eng.wait_ge(sems[w.sem], tgt)
                        waited[w.sem] = tgt
                    if stub:
                        if inc is not None:
                            eng.sem_inc(sems[inc], self.unit(inc))
                        continue
                    instr = fn(eng)
                    if inc is not None:
                        assert instr is not None, f"inc-op returned None: {tag}"
                        instr.then_inc(sems[inc], self.unit(inc))
            return body
        for e in ENGINES:
            getattr(block, e)(mk(e))


def host_constants(cfg):
    NH = cfg.NH
    c = {}
    c["ident128"] = np.eye(128, dtype=BF)
    L = np.zeros((128, 128), np.float32)
    for blk in range(2):
        for s in range(64):
            L[blk * 64 + s, blk * 64 + s:blk * 64 + 64] = 1.0
    c["lones"] = L
    bo = np.zeros((3 * NH, NH * 64), np.float32)
    for l in range(3):
        for h in range(NH):
            bo[l * NH + h, h * 64:h * 64 + 64] = 1.0
    c["bones48"] = bo.astype(BF)
    c["bones32"] = bo[:2 * NH].astype(BF)
    r = np.arange(64)
    mA = np.where(r[:, None] > r[None, :], 0.0, -30000.0)
    m2 = np.where(r[None, :] >= r[:, None], 0.0, -30000.0)
    m3 = np.where(r[None, :] > r[:, None], 0.0, -30000.0)
    c["maskA"] = np.tile(np.concatenate([mA, mA], 0), (1, 8)).astype(BF)
    c["mask2"] = np.tile(np.concatenate([m2, m2], 0), (1, 8)).astype(BF)
    c["mask3"] = np.tile(np.concatenate([m3, m3], 0), (1, 8)).astype(BF)
    hm = np.zeros((cfg.NCC, 128, NH), np.float32)
    hmT = np.zeros((cfg.NCC, NH, 128), np.float32)
    for cc in range(cfg.NCC):
        hm[cc, 0:64, cc] = 1.0
        hm[cc, 64:128, 8 + cc] = 1.0
        hmT[cc, cc, 0:64] = 1.0
        hmT[cc, 8 + cc, 64:128] = 1.0
    c["hmask"] = hm.astype(BF)
    c["hmaskT"] = hmT.astype(BF)
    ho = np.zeros((2, 2 * 2 * 64), np.float32)
    ho[0, 0:64] = 1.0; ho[0, 128:192] = 1.0
    ho[1, 64:128] = 1.0; ho[1, 192:256] = 1.0
    c["halfones"] = ho.astype(BF)
    c["onesrow"] = np.ones((1, 64), dtype=BF)
    c["identf32"] = np.eye(128, dtype=np.float32)
    return c


def build_program(cfg):
    nc = bass.Bass("TRN2", target_bir_lowering=False)
    H, NH, DH, B = cfg.H, cfg.NH, cfg.DH, cfg.B
    TILE, NT, NCC, FW, WIDE = cfg.TILE, cfg.NT, cfg.NCC, cfg.FW, cfg.WIDE
    T = cfg.T
    consts = host_constants(cfg)

    def din(name, shape, dt=F32):
        return nc.dram_tensor(name, shape, dt, kind="ExternalInput").ap()

    x_tm_d = din("x_tm", [B, T, H])
    x_fm_d = din("x_fm", [B, H, T])
    w_d = {w: din(f"w_{w}t", [H, H]) for w in ("q", "k", "v", "g", "o", "f")}
    wbat_d = din("wbat", [H, 2 * NH])
    convw_d = din("convw", [128, 3, NCC, 4])
    alog_d = din("alog", [1, NH])
    dtrep_d = din("dtrep", [128, NH])
    normw2_d = din("normw2", [128, 1])
    cst_d = {nm: din("c_" + nm, list(a.shape), BF16 if a.dtype == BF else F32)
             for nm, a in consts.items()}
    y_fm_d = nc.dram_tensor("y_fm", [B, H, T], F32, kind="ExternalOutput").ap()

    es = contextlib.ExitStack()
    sb = lambda nm, sh, dt=F32: es.enter_context(nc.sbuf_tensor(nm, sh, dt))
    ps = lambda nm, sh, dt=F32: es.enter_context(nc.psum_tensor(nm, sh, dt))

    W = {w: sb(f"W{w}", [128, NCC, H], BF16) for w in ("q", "k", "v", "g", "o", "f")}
    Wba = sb("Wba", [128, NCC, 2 * NH], BF16)
    wstage = sb("wstage", [128, 1024])
    convw = sb("convw_t", [128, 3, NCC, 4])
    ident = sb("identt", [128, 128], BF16)
    identf = sb("identft", [128, 128])
    lones = sb("lonest", [128, 128])
    maskA = sb("maskAt", [128, 512], BF16)
    mask2 = sb("mask2t", [128, 512], BF16)
    mask3 = sb("mask3t", [128, 512], BF16)
    hmask = sb("hmaskt", [128, NCC, NH], BF16)
    hmaskT = sb("hmaskTt", [NH, NCC, 128], BF16)
    onesrow = sb("onesrowt", [1, 64], BF16)
    dtrep = sb("dtrept", [128, NH])
    normw2 = sb("normw2t", [128, 1])
    alog = sb("alogt", [1, NH])
    negexpA = sb("negexpAt", [128, NH])
    epsln = sb("epsln", [128, 1])
    epsl2 = sb("epsl2", [128, 1])
    epso = sb("epso", [128, 1])
    lnq8 = sb("lnq8", [128, 1])
    lnsq = sb("lnsq", [128, 1])
    lnsq2 = sb("lnsq2", [128, 1])
    rhsA = sb("rhsA", [58, 2, WIDE], BF16)
    rhs3 = sb("rhs3", [86, 2, WIDE], BF16)
    Gpos = sb("Gpos", [58, 2, B, 64], BF16)
    Gneg = sb("Gneg", [86, 2, B, 64], BF16)
    xtm = sb("xtm", [128, H])
    xfm = sb("xfm", [128, 2, FW])
    xnf = sb("xnf", [128, NCC, B, TILE], BF16)
    lnsc = sb("lnsc", [128, B, 2])
    prebuf = sb("prebuf", [128, NCC, B, TILE + 3], BF16)
    xnbv = prebuf[:].rearrange("p a b t -> p (a b t)")[:, 0:2 * H].rearrange(
        "p (b h) -> p b h", b=2)
    halo = sb("halo", [128, 3, NCC, B, 3], BF16)
    gatebuf = sb("gatebuf", [128, NCC, B, TILE], BF16)
    ffng = sb("ffng", [128, NCC, B, TILE], BF16)
    convacc = sb("convacc", [128, NCC, B, TILE], BF16)
    silbuf = sb("silbuf", [128, NCC, B, TILE], BF16)
    xtm2 = silbuf[:].rearrange("p a b t -> p (a b t)").bitcast(F32)[:, 0:H]
    khat = sb("khat", [128, NCC, B, TILE], BF16)
    qtil = sb("qtil", [128, NCC, B, TILE], BF16)
    qhat = sb("qhat", [128, NCC, B, TILE], BF16)
    vtm = sb("vtm", [128, 2, WIDE], BF16)
    k2tm = sb("k2tm", [128, 2, WIDE], BF16)
    uh = sb("uh", [128, 2, WIDE], BF16)
    dvb = sb("dvb", [128, 2, WIDE], BF16)
    yv = sb("yv", [128, WIDE], BF16)
    yv2 = sb("yv2", [128, WIDE], BF16)
    Dexp1 = yv
    Dexp3 = yv2
    Mc = sb("Mcc2", [128, WIDE], BF16) if False else None
    tscr = wstage
    S32b = wstage
    Dexp2 = sb("Dexp2", [128, WIDE], BF16)
    McT = Dexp2
    Bw = sb("Bww", [128, WIDE], BF16)
    BTw = sb("BTww", [128, WIDE], BF16)
    Mmw = sb("Mmww", [128, WIDE], BF16)

    Ncr = sb("Ncc", [128, WIDE], BF16)
    Mx = sb("Mxx", [128, WIDE], BF16)
    Nx = sb("Nxx", [128, WIDE], BF16)
    S32 = sb("S32", [128, B, NCC, DH])
    Sb16 = sb("Sb16", [128, B, NCC, DH], BF16)
    qscq = sb("qscq", [16, FW], BF16)
    qsck = sb("qsck", [16, FW], BF16)
    osc = sb("osc", [16, FW], BF16)
    beta_tm = sb("beta_tm", [128, 2, NH])
    lnb_tm = sb("lnb_tm", [128, 2, NH])
    g_tm = sb("g_tm", [128, 2, NH])
    G_tm = sb("G_tm", [128, 2, NH])
    expG_tm = sb("expG_tm", [128, 2, NH])
    tmsc1 = sb("tmsc1", [128, 2, NH])
    Gfm = sb("Gfm", [NH, B, 2, 64])
    Gl1s = sb("Gl1s", [NH, B, TILE], BF16)
    Gn1s = sb("Gn1s", [NH, B, TILE], BF16)
    Lb = [sb(f"Lb{i}", [NH, B, TILE], BF16) for i in range(2)]
    Glv1 = sb("Glv1", [NH, B, TILE])
    Glv2 = sb("Glv2", [NH, B, TILE])
    lnbfm = Glv2
    eGceFM = sb("eGceFM", [NH, B, TILE], BF16)
    expGfm = sb("expGfm", [NH, B, TILE], BF16)
    rowb = sb("rowb", [1, B, NH])
    rowbb = sb("rowbb", [1, 2, B * NCC], BF16)
    resid = wstage[:].rearrange("p (a b) -> p a b", a=2)[:, :, 0:FW]
    osq = wstage[:].bitcast(BF16).rearrange("p (a b t) -> p a b t", a=NCC, b=B)
    oxsv = wstage[:].bitcast(BF16)[:, 0:1024]

    PA = ps("PA", [128, 512])
    PB = ps("PBb", [128, 512])
    PW1 = ps("PW1", [128, 1024])
    PW2 = ps("PW2", [128, 1024])
    PO = ps("PO", [128, 1024])
    PAb = PA[:].bitcast(BF16)
    PW2b = PW2[:].bitcast(BF16)

    sems = {nm: es.enter_context(nc.semaphore(nm))
            for nm in ("sdc", "sdw", "sdw1", "sdx", "sdx1", "sdf0", "sdf1",
                       "sdo0", "sdo1", "gdr", "gdf", "tp", "vp", "sp", "gp")}
    P = Prog()

    def dma(q, sem, out, in_, waits=(), tag="", slow=False):
        return P.op(q, lambda e, o=out, i=in_, s=slow: e.dma_start(
            out=o, in_=i, allow_slow_non_contiguous=s),
            waits=waits, inc=sem, tag=tag)

    def T_(fn, waits=(), tag="", inc="tp"):
        return P.op("tensor", fn, waits=waits, inc=inc, tag=tag)

    def V_(fn, waits=(), tag="", inc="vp"):
        return P.op("vector", fn, waits=waits, inc=inc, tag=tag)

    def S_(fn, waits=(), tag="", inc="sp"):
        return P.op("scalar", fn, waits=waits, inc=inc, tag=tag)

    def G_(fn, waits=(), tag="", inc="gp"):
        return P.op("gpsimd", fn, waits=waits, inc=inc, tag=tag)

    bkr = {k: [] for k in ("PA", "PB", "PW1", "PW2", "PO")}

    def bank_open(bank):
        ws = bkr[bank]
        bkr[bank] = []
        return ws

    def bank_read(bank, ev):
        bkr[bank].append(ev)
        return ev

    evc = []
    for dst, nm in [(ident[:], "ident128"), (identf[:], "identf32"),
                    (lones[:], "lones"),
                    (maskA[:], "maskA"), (mask2[:], "mask2"), (mask3[:], "mask3"),
                    (onesrow[:], "onesrow")]:
        evc.append(dma("sync", "sdc", dst, cst_d[nm], tag=f"c_{nm}"))
    evc.append(dma("sync", "sdc", hmask[:], cst_d["hmask"].rearrange("a p b -> p a b"),
                   tag="c_hm"))
    evc.append(dma("sync", "sdc", hmaskT[:], cst_d["hmaskT"].rearrange("a p b -> p a b"),
                   tag="c_hmT"))
    evc.append(dma("sync", "sdc", convw[:].rearrange("p a b c -> p (a b c)"),
                   convw_d.rearrange("p a b c -> p (a b c)"), tag="c_cw"))
    evc.append(dma("sync", "sdc", dtrep[:], dtrep_d, tag="c_dt"))
    evc.append(dma("sync", "sdc", normw2[:], normw2_d, tag="c_nw"))
    evc.append(dma("sync", "sdc", alog[:], alog_d, tag="c_al"))
    for ck in range(2):
        evc.append(dma("sync", "sdc", rhsA[0:48, ck, :], cst_d["bones48"], tag="c_bA"))
        evc.append(dma("sync", "sdc", rhs3[0:48, ck, :], cst_d["bones48"], tag="c_b3a"))
        evc.append(dma("sync", "sdc", rhs3[54:86, ck, :], cst_d["bones32"], tag="c_b3b"))
    sel0 = cst_d["halfones"][0:1, :]
    sel1 = cst_d["halfones"][1:2, :]
    row = lambda ap: ap.rearrange("p ck b r -> p (ck b r)")
    for i in range(5):
        evc.append(dma("sync", "sdc", row(Gpos[48 + i:49 + i]), sel0, tag="hoA"))
        evc.append(dma("sync", "sdc", row(Gpos[53 + i:54 + i]), sel1, tag="hoB"))
    for i in range(3):
        evc.append(dma("sync", "sdc", row(Gneg[48 + i:49 + i]), sel0, tag="ho3A"))
        evc.append(dma("sync", "sdc", row(Gneg[51 + i:52 + i]), sel1, tag="ho3B"))
    ev_const = evc[-1]

    evm = []
    evm.append(V_(lambda e: e.memset(epsln[:], 1e-5), tag="m1"))
    evm.append(V_(lambda e: e.memset(epsl2[:], 1e-6), tag="m2"))
    evm.append(V_(lambda e: e.memset(epso[:], 1e-5), tag="m3"))
    evm.append(V_(lambda e: e.memset(lnq8[:], float(np.log(1.0 / 8.0))), tag="m4"))
    ev_S_w = V_(lambda e: e.memset(S32[:].rearrange("p a b c -> p (a b c)"), 0.0), tag="m5")
    ev_Sb_w = V_(lambda e: e.memset(Sb16[:].rearrange("p a b c -> p (a b c)"), 0.0), tag="m6")
    ev_halo0 = V_(lambda e: e.memset(halo[:].rearrange("p a b c d -> p (a b c d)"), 0.0), tag="m7")
    ev_minit = ev_halo0

    ev_e1 = S_(lambda e: e.activation(out=alog[:], in_=alog[:], func=AFT.Exp),
               waits=[ev_const], tag="expA")
    wA0 = [ev_e1] + bank_open("PB")
    ev_e2a = T_(lambda e: e.matmul(PB[0:64, 496:496 + NH], lones[0:1, 0:64], alog[:],
                                   start=True, stop=True, tile_position=(0, 0)),
                waits=wA0, tag="bcA0")
    ev_e2 = T_(lambda e: e.matmul(PB[64:128, 496:496 + NH], lones[0:1, 0:64], alog[:],
                                  start=True, stop=True, tile_position=(0, 64)),
               waits=[ev_e2a], tag="bcA1")
    ev_negA = V_(lambda e: e.tensor_scalar(out=negexpA[:], in0=PB[:, 496:496 + NH],
                                           scalar1=-1.0, scalar2=None, op0=OP.mult),
                 waits=[ev_e2], tag="negA")
    bank_read("PB", ev_negA)

    ev_wcast = {}
    half_cast = [None, None]
    nslot = 0
    for ic in range(NCC):
        hh = nslot % 2
        nslot += 1
        evd = dma("sync" if hh == 0 else "scalar", "sdw" if hh == 0 else "sdw1",
                  wstage[:, hh * 512:hh * 512 + 2 * NH],
                  wbat_d[ic * 128:(ic + 1) * 128, :],
                  waits=[half_cast[hh]], tag=f"wdba{ic}")
        half_cast[hh] = V_(lambda e, ic=ic, hh=hh: e.tensor_scalar(
            out=Wba[:, ic, :], in0=wstage[:, hh * 512:hh * 512 + 2 * NH],
            scalar1=1.0, scalar2=None, op0=OP.mult),
            waits=[evd], tag=f"wcba{ic}")
        ev_wcast[("ba", ic)] = half_cast[hh]
    for wname in ("q", "k", "v", "g", "o", "f"):
        for ic in range(NCC):
            last = None
            for ch in range(2):
                hh = nslot % 2
                nslot += 1
                evd = dma("sync" if hh == 0 else "scalar",
                          "sdw" if hh == 0 else "sdw1",
                          wstage[:, hh * 512:hh * 512 + 512],
                          w_d[wname][ic * 128:(ic + 1) * 128, ch * 512:ch * 512 + 512],
                          waits=[half_cast[hh]], tag=f"wd{wname}{ic}{ch}")
                if wname == "f":
                    last = S_(lambda e, ic=ic, hh=hh, ch=ch: e.sign(
                        out=W["f"][:, ic, ch * 512:ch * 512 + 512],
                        in_=wstage[:, hh * 512:hh * 512 + 512]),
                        waits=[evd], tag=f"wsg{ic}{ch}")
                else:
                    last = V_(lambda e, wn=wname, ic=ic, hh=hh, ch=ch: e.tensor_scalar(
                        out=W[wn][:, ic, ch * 512:ch * 512 + 512],
                        in0=wstage[:, hh * 512:hh * 512 + 512],
                        scalar1=1.0, scalar2=None, op0=OP.mult),
                        waits=[evd], tag=f"wc{wname}{ic}{ch}")
                half_cast[hh] = last
            ev_wcast[(wname, ic)] = last


    ev_xtm_use = ev_minit
    ev_xtm2_use = ev_minit
    ev_lnsq = None
    ev_uh_use = None
    ev_uh_w = None
    ev_state_S = ev_S_w
    ev_state_Sb = ev_Sb_w
    ev_halo = [ev_halo0, ev_halo0, ev_halo0]
    ev_prebuf_free = [None, None, None]
    bankfree = {0: None, 1: None}
    ev_wfree = {}
    ev_laststore = {}

    for it in range(NT):
        P.stream = ("L", it)
        t0 = it * TILE
        ev_lnap = []
        for b in range(B):
            xb = xtm if b == 0 else xtm2
            ev_xt = dma("sync", "sdx" if b == 0 else "sdx1", xb[:, :],
                        x_tm_d[b, t0:t0 + TILE, :],
                        waits=[ev_xtm_use if b == 0 else
                               [ev_xtm2_use, ev_wfree.get("silbuf")]],
                        tag=f"xtm{it}_{b}")
            e1 = V_(lambda e, b=b, xb=xb: e.tensor_scalar(
                out=xb[:, :], in0=xb[:, :], scalar1=1.0, scalar2=0.0,
                op0=OP.mult, op1=OP.add, accum_out=lnsc[:, b, 0:1]),
                waits=[ev_xt], tag=f"lns{it}_{b}")
            e2w = bank_open("PA")
            e2a = S_(lambda e, b=b, xb=xb: e.activation(
                out=PA[:, 0:512], in_=xb[:, 0:512], func=AFT.Square,
                accum_out=lnsc[:, b, 1:2]), waits=[ev_xt, e1, ev_lnsq] + e2w,
                tag=f"lnq{it}_{b}")
            e2b = S_(lambda e, b=b, xb=xb: e.activation(
                out=PA[:, 0:512], in_=xb[:, 512:1024], func=AFT.Square,
                accum_out=lnsq[:, 0:1] if b == 0 else lnsq2[:, 0:1]),
                waits=[e2a], tag=f"lnq2{it}_{b}")
            bank_read("PA", e2b)
            e2 = V_(lambda e, b=b: e.tensor_tensor(
                out=lnsc[:, b, 1:2], in0=lnsc[:, b, 1:2],
                in1=lnsq[:, 0:1] if b == 0 else lnsq2[:, 0:1],
                op=OP.add), waits=[e2b], tag=f"lnq3{it}_{b}")
            ev_lnsq = e2
            e3 = V_(lambda e, b=b: e.tensor_scalar(
                out=lnsc[:, b, :], in0=lnsc[:, b, :], scalar1=1.0 / H,
                scalar2=None, op0=OP.mult), waits=[e1, e2], tag=f"lnm{it}_{b}")
            e5 = V_(lambda e, b=b: e.scalar_tensor_tensor(
                out=lnsc[:, b, 1:2], in0=lnsc[:, b, 0:1], scalar=lnsc[:, b, 0:1],
                in1=lnsc[:, b, 1:2], op0=OP.mult, op1=OP.subtract),
                waits=[e3], tag=f"lnv{it}_{b}")
            e6 = V_(lambda e, b=b: e.tensor_scalar(
                out=lnsc[:, b, 1:2], in0=lnsc[:, b, 1:2], scalar1=-1.0,
                scalar2=None, op0=OP.mult), waits=[e5], tag=f"lnn{it}_{b}")
            e7 = S_(lambda e, b=b: e.activation(
                out=lnsc[:, b, 1:2], in_=lnsc[:, b, 1:2], func=AFT.Ln,
                bias=epsln[:, 0:1]), waits=[e6], tag=f"lnl{it}_{b}")
            e8 = S_(lambda e, b=b: e.activation(
                out=lnsc[:, b, 1:2], in_=lnsc[:, b, 1:2], func=AFT.Exp,
                scale=-0.5), waits=[e7], tag=f"lne{it}_{b}")
            e9 = V_(lambda e, b=b, xb=xb: e.tensor_scalar(
                out=xnbv[:, b, :], in0=xb[:, :],
                scalar1=lnsc[:, b, 0:1], scalar2=lnsc[:, b, 1:2],
                op0=OP.subtract, op1=OP.mult),
                waits=[e8, ev_prebuf_free[0], ev_prebuf_free[1],
                       ev_prebuf_free[2], ev_halo[0], ev_halo[1], ev_halo[2]],
                tag=f"lna{it}_{b}")
            ev_lnap.append(e9)
            if b == 0:
                ev_xtm_use = [e9, e1, e2b]
            else:
                ev_xtm2_use = [e9, e1, e2b]
                ev_wfree["silbuf"] = [e9, e2b, ev_wfree.get("silbuf")]

        ev_xnf = []
        for b in range(B):
            for g4 in range(2):
                evt = None
                for q4 in range(4):
                    ic = g4 * 4 + q4
                    evt = T_(lambda e, b=b, ic=ic, q4=q4: e.transpose(
                        PAb[:, q4 * 128:q4 * 128 + 128],
                        xnbv[:, b, ic * 128:ic * 128 + 128],
                        ident[:]),
                        waits=[ev_lnap[b]] + (bank_open("PA") if q4 == 0 else []),
                        tag=f"xnt{it}_{b}_{ic}")
                enf = S_(lambda e, b=b, g4=g4: e.activation(
                    out=xnf[:, g4 * 4:g4 * 4 + 4, b, :],
                    in_=PAb[:, 0:512].rearrange("p (a t) -> p a t", a=4), func=AFT.Copy),
                    waits=[evt, ev_wfree.get("xnf")], tag=f"xne{it}_{b}_{g4}")
                ev_xnf.append(enf)
                bank_read("PA", enf)
        ev_xnf_done = ev_xnf[-1]
        ev_uh_use = ev_xnf_done

        ev_pre = {}
        ev_sc_done = {}
        banks = [PA, PB]
        for wname in ("v", "k", "q"):
            wi = {"q": 0, "k": 1, "v": 2}[wname]
            ehc = V_(lambda e, wi=wi: e.tensor_copy(
                out=prebuf[:, :, :, 0:3],
                in_=halo[:, wi, :, :, :]),
                waits=[ev_halo[wi], ev_prebuf_free[wi], ev_xnf_done],
                tag=f"hi{it}_{wi}")
            evac = {}
            for oc in range(NCC):
                bank = banks[(oc // 2) % 2]
                bnm = "PA" if (oc // 2) % 2 == 0 else "PB"
                half = (oc % 2) * FW
                evmm = None
                for kc in range(NCC):
                    evmm = T_(lambda e, wn=wname, oc=oc, kc=kc, bank=bank, half=half: e.matmul(
                        bank[:, half:half + FW], W[wn][:, kc, oc * 128:(oc + 1) * 128],
                        xnf[:, kc, :, :].rearrange("p b t -> p (b t)"),
                        start=(kc == 0), stop=(kc == NCC - 1)),
                        waits=[ev_xnf_done, ev_wcast[(wname, NCC - 1)]]
                        + (bank_open(bnm) if kc == 0 else []),
                        tag=f"pj{it}{wname}{oc}_{kc}",
                        inc="tp" if kc == NCC - 1 else None)
                if oc % 2 == 1:
                    eev = S_(lambda e, oc=oc, bank=bank: e.activation(
                        out=prebuf[:, oc - 1:oc + 1, :, 3:3 + TILE],
                        in_=bank[:, 0:2 * FW].rearrange("p (o b t) -> p o b t", o=2, b=B),
                        func=AFT.Copy),
                        waits=[evmm, ehc], tag=f"pe{it}{wname}{oc}")
                    evac[oc - 1] = evac[oc] = eev
                    bank_read(bnm, eev)
            ev_pre[wname] = evac[NCC - 1]
            ev_halo[wi] = V_(lambda e, wi=wi: e.tensor_copy(
                out=halo[:, wi, :, :, :],
                in_=prebuf[:, :, :, TILE:TILE + 3]),
                waits=[ev_pre[wname]], tag=f"ho{it}_{wi}")

            if wname == "v":
                cwb = lambda j: convw[:, 2, :, j].unsqueeze(2).unsqueeze(3).broadcast_to(
                    [128, NCC, B, TILE])
                ecv = V_(lambda e, cw0=cwb(0): e.tensor_tensor(
                    out=convacc[:, :, :, :], in0=prebuf[:, :, :, 0:TILE],
                    in1=cw0, op=OP.mult),
                    waits=[evac[NCC - 1], ev_const, ev_wfree.get("convacc")],
                    tag=f"cv{it}v0")
                for j in range(1, 4):
                    ez2 = V_(lambda e, j=j, cwj=cwb(j): e.tensor_tensor(
                        out=silbuf[:, :, :, :], in0=prebuf[:, :, :, j:j + TILE],
                        in1=cwj, op=OP.mult),
                        waits=[ecv,
                               ev_wfree.get("silbuf") if j == 1 else None],
                        tag=f"cw{it}v{j}")
                    ecv = V_(lambda e: e.tensor_tensor(
                        out=convacc[:, :, :, :], in0=convacc[:, :, :, :],
                        in1=silbuf[:, :, :, :], op=OP.add),
                        waits=[ez2], tag=f"cv{it}v{j}")
                ev_prebuf_free[wi] = ecv
                ev_wfree["convacc"] = None
                evsil = S_(lambda e: e.activation(
                    out=silbuf[:].rearrange("p a b t -> p (a b t)"),
                    in_=convacc[:].rearrange("p a b t -> p (a b t)"), func=AFT.Silu),
                    waits=[ecv, ev_wfree.get("silbuf")], tag=f"sl{it}v")
                evvt = []
                for b in range(B):
                    for ck in range(2):
                        for g2 in range(2):
                            evt = None
                            for q4 in range(4):
                                ic = g2 * 4 + q4
                                evt = T_(lambda e, b=b, ck=ck, ic=ic, q4=q4: e.transpose(
                                    PAb[0:64, q4 * 128:q4 * 128 + 128],
                                    silbuf[:, ic, b, ck * 64:ck * 64 + 64],
                                    ident[:]),
                                    waits=[evsil] + (bank_open("PA") if q4 == 0 else []),
                                    tag=f"vt{it}_{b}{ck}{ic}")
                            evte = S_(lambda e, b=b, ck=ck, g2=g2: e.activation(
                                out=vtm[64 * b:64 * b + 64, ck, g2 * 512:g2 * 512 + 512],
                                in_=PAb[0:64, 0:512], func=AFT.Copy),
                                waits=[evt, ev_wfree.get("vtm")], tag=f"vte{it}_{b}{ck}{g2}")
                            evvt.append(evte)
                            bank_read("PA", evte)
                ev_pre["vtm"] = evvt[-1]
                ev_wfree["silbuf"] = evvt[-1]
                ev_wfree["convacc"] = evsil
                continue

            cwb = lambda j: convw[:, wi, :, j].unsqueeze(2).unsqueeze(3).broadcast_to(
                [128, NCC, B, TILE])
            ecv = V_(lambda e, cw0=cwb(0): e.tensor_tensor(
                out=convacc[:, :, :, :], in0=prebuf[:, :, :, 0:TILE],
                in1=cw0, op=OP.mult),
                waits=[evac[NCC - 1], ev_const, ev_wfree.get("convacc")],
                tag=f"cv{it}{wname}0")
            for j in range(1, 4):
                ez2 = V_(lambda e, j=j, cwj=cwb(j): e.tensor_tensor(
                    out=silbuf[:, :, :, :], in0=prebuf[:, :, :, j:j + TILE],
                    in1=cwj, op=OP.mult),
                    waits=[ecv,
                           ev_wfree.get("silbuf") if j == 1 else None],
                    tag=f"cw{it}{wname}{j}")
                ecv = V_(lambda e: e.tensor_tensor(
                    out=convacc[:, :, :, :], in0=convacc[:, :, :, :],
                    in1=silbuf[:, :, :, :], op=OP.add),
                    waits=[ez2], tag=f"cv{it}{wname}{j}")
            ev_prebuf_free[wi] = ecv
            esil = S_(lambda e: e.activation(
                out=silbuf[:].rearrange("p a b t -> p (a b t)"),
                in_=convacc[:].rearrange("p a b t -> p (a b t)"), func=AFT.Silu),
                waits=[ecv, ev_wfree.get("silbuf")], tag=f"sl{it}{wname}")
            esq = V_(lambda e: e.tensor_tensor(
                out=convacc[:].rearrange("p a b t -> p (a b t)"),
                in0=silbuf[:].rearrange("p a b t -> p (a b t)"),
                in1=silbuf[:].rearrange("p a b t -> p (a b t)"), op=OP.mult),
                waits=[esil], tag=f"sq{it}{wname}")
            ev_wfree["convacc"] = esq
            evmm = None
            for ic in range(NCC):
                evmm = T_(lambda e, ic=ic: e.matmul(
                    PB[0:16, 0:FW], hmask[:, ic, 0:16],
                    convacc[:, ic, :, :].rearrange("p b t -> p (b t)"),
                    start=(ic == 0), stop=(ic == NCC - 1)),
                    waits=[esq] + (bank_open("PB") if ic == 0 else []),
                    tag=f"ss{it}{wname}{ic}",
                    inc="tp" if ic == NCC - 1 else None)
            qst = qscq if wname == "q" else qsck
            el1 = S_(lambda e, qst=qst: e.activation(
                out=qst[:], in_=PB[0:16, 0:FW], func=AFT.Ln,
                bias=epsl2[0:16, 0:1]), waits=[evmm, ev_wfree.get("qsc")],
                tag=f"rl{it}{wname}")
            bank_read("PB", el1)
            if wname == "q":
                el2 = S_(lambda e, qst=qst: e.activation(
                    out=qst[:], in_=qst[:],
                    func=AFT.Exp, scale=-0.5, bias=lnq8[0:16, 0:1]),
                    waits=[el1], tag=f"re{it}{wname}")
            else:
                el2 = S_(lambda e, qst=qst: e.activation(
                    out=qst[:], in_=qst[:],
                    func=AFT.Exp, scale=-0.5), waits=[el1], tag=f"re{it}{wname}")
            dst = qhat if wname == "q" else khat
            evsc = []
            for ic in range(NCC):
                bbank = banks[ic % 2]
                bbnm = "PA" if ic % 2 == 0 else "PB"
                evb = T_(lambda e, ic=ic, qst=qst, bbank=bbank: e.matmul(
                    bbank[:, 0:FW], hmaskT[:, ic, :], qst[:],
                    start=True, stop=True),
                    waits=[el2] + bank_open(bbnm),
                    tag=f"bc{it}{wname}{ic}")
                evsc.append(V_(lambda e, ic=ic, dst=dst, bbank=bbank: e.tensor_tensor(
                    out=dst[:, ic, :, :].rearrange("p b t -> p (b t)"),
                    in0=silbuf[:, ic, :, :].rearrange("p b t -> p (b t)"),
                    in1=bbank[:, 0:FW], op=OP.mult),
                    waits=[evb, ev_wfree.get("qhat") if wname == "q"
                           else ev_wfree.get("khat")],
                    tag=f"sc{it}{wname}{ic}"))
                bank_read(bbnm, evsc[-1])
            ev_pre[wname + "hat"] = evsc[-1]
            ev_sc_done[wname] = [evsc[-1], evb]
            ev_wfree["silbuf"] = evsc[-1]


        for wname, dstb in (("g", gatebuf), ("f", ffng)):
            evacs = []
            for oc in range(NCC):
                bank = banks[(oc // 2) % 2]
                bnm = "PA" if (oc // 2) % 2 == 0 else "PB"
                half = (oc % 2) * FW
                evmm = None
                for kc in range(NCC):
                    evmm = T_(lambda e, wn=wname, oc=oc, kc=kc, bank=bank, half=half: e.matmul(
                        bank[:, half:half + FW], W[wn][:, kc, oc * 128:(oc + 1) * 128],
                        xnf[:, kc, :, :].rearrange("p b t -> p (b t)"),
                        start=(kc == 0), stop=(kc == NCC - 1)),
                        waits=[ev_xnf_done, ev_wcast[(wname, NCC - 1)]]
                        + (bank_open(bnm) if kc == 0 else []),
                        tag=f"pj{it}{wname}{oc}_{kc}",
                        inc="tp" if kc == NCC - 1 else None)
                if oc % 2 == 1:
                    eev = S_(lambda e, oc=oc, bank=bank, dstb=dstb: e.activation(
                        out=dstb[:, oc - 1:oc + 1, :, :],
                        in_=bank[:, 0:2 * FW].rearrange("p (o b t) -> p o b t", o=2, b=B),
                        func=AFT.Silu),
                        waits=[evmm, ev_wfree.get("gate" if dstb is gatebuf else "ffn")],
                        tag=f"sv{it}{wname}{oc}")
                    evacs.append(eev)
                    bank_read(bnm, eev)
            if wname == "g":
                ev_gate_raw = evacs[-1]
            else:
                ev_ffn_done = evacs[-1]
                ev_wfree["xnf"] = evmm
        eg1 = V_(lambda e: e.tensor_copy(
            out=convacc[:].rearrange("p a b t -> p (a b t)"),
            in_=gatebuf[:].rearrange("p a b t -> p (a b t)")),
            waits=[ev_gate_raw, ev_wfree.get("convacc")], tag=f"gn1{it}")
        ev_gate_done = V_(lambda e: e.tensor_scalar(
            out=gatebuf[:].rearrange("p a b t -> p (a b t)"),
            in0=convacc[:].rearrange("p a b t -> p (a b t)"),
            scalar1=normw2[:, 0:1], scalar2=None, op0=OP.mult),
            waits=[eg1], tag=f"gn2{it}")
        ev_wfree["convacc"] = ev_gate_done

        P.stream = ("S", it)

        evba = {}
        evmm = None
        for ck in range(2):
            for b in range(B):
                for kc in range(NCC):
                    evmm = T_(lambda e, ck=ck, b=b, kc=kc: e.matmul(
                        PW2[64 * b:64 * b + 64, 256 + 32 * ck:288 + 32 * ck],
                        xnf[:, kc, b, ck * 64:ck * 64 + 64],
                        Wba[:, kc, :], start=(kc == 0), stop=(kc == NCC - 1),
                        tile_position=(0, 64 * b)),
                        waits=[ev_xnf_done, ev_wcast[("ba", NCC - 1)]]
                        + (bank_open("PW2") if (ck == 0 and b == 0 and kc == 0) else []),
                        tag=f"ba{it}_{ck}_{b}_{kc}",
                        inc="tp" if kc == NCC - 1 else None)
        bav = PW2[:, 256:320].rearrange("p (c w) -> p c w", c=2)
        eb = S_(lambda e: e.activation(
            out=beta_tm[:, :, :], in_=bav[:, :, 0:NH], func=AFT.Sigmoid),
            waits=[evmm, ev_wfree.get("tmsmalls")], tag=f"sig{it}")
        bank_read("PW2", eb)
        elb = S_(lambda e: e.activation(
            out=lnb_tm[:, :, :], in_=beta_tm[:, :, :], func=AFT.Ln),
            waits=[eb], tag=f"lnb{it}")
        evba[("lnb", 0)] = evba[("lnb", 1)] = elb
        ez = V_(lambda e: e.tensor_tensor(
            out=g_tm[:, :, :], in0=bav[:, :, NH:2 * NH],
            in1=dtrep[:].unsqueeze(1).broadcast_to([128, 2, NH]),
            op=OP.add), waits=[evmm, ev_const], tag=f"z{it}")
        bank_read("PW2", ez)
        ea = V_(lambda e: e.scalar_tensor_tensor(
            out=tmsc1[:, :, :], in0=g_tm[:, :, :], scalar=-1.0,
            op0=OP.mult, in1=g_tm[:, :, :], op1=OP.max),
            waits=[ez], tag=f"abs{it}")
        ee = S_(lambda e: e.activation(
            out=tmsc1[:, :, :], in_=tmsc1[:, :, :], func=AFT.Exp,
            scale=-1.0), waits=[ea, elb], tag=f"spe{it}")
        el = S_(lambda e: e.activation(
            out=tmsc1[:, :, :], in_=tmsc1[:, :, :], func=AFT.Ln,
            bias=1.0), waits=[ee], tag=f"spl{it}")
        er = V_(lambda e: e.tensor_scalar(
            out=g_tm[:, :, :], in0=g_tm[:, :, :], scalar1=0.0,
            scalar2=None, op0=OP.max), waits=[el], tag=f"spr{it}")
        es2 = V_(lambda e: e.tensor_tensor(
            out=g_tm[:, :, :], in0=g_tm[:, :, :], in1=tmsc1[:, :, :],
            op=OP.add), waits=[er], tag=f"spa{it}")
        eg = V_(lambda e: e.tensor_tensor(
            out=g_tm[:, :, :], in0=g_tm[:, :, :],
            in1=negexpA[:].unsqueeze(1).broadcast_to([128, 2, NH]),
            op=OP.mult), waits=[es2, ev_negA], tag=f"gg{it}")
        evg = []
        for ck in range(2):
            for b in range(B):
                evg.append(T_(lambda e, ck=ck, b=b: e.matmul(
                    PW2[64 * b:64 * b + 64, 320 + 16 * ck:336 + 16 * ck],
                    lones[64 * b:64 * b + 64, 64 * b:64 * b + 64],
                    g_tm[64 * b:64 * b + 64, ck, :], start=True, stop=True,
                    tile_position=(64 * b, 64 * b)),
                    waits=[eg] + (bank_open("PW2") if (ck == 0 and b == 0) else []),
                    tag=f"G{it}_{ck}_{b}"))
        ecp = V_(lambda e: e.tensor_copy(
            out=G_tm[:, :, :],
            in_=PW2[:, 320:352].rearrange("p (c h) -> p c h", c=2)),
            waits=[evg[-1]], tag=f"Gc{it}")
        bank_read("PW2", ecp)
        eex = S_(lambda e: e.activation(
            out=expG_tm[:, :, :], in_=G_tm[:, :, :], func=AFT.Exp),
            waits=[ecp], tag=f"Ge{it}")
        evba[("expG", 0)] = evba[("expG", 1)] = eex
        for ck in range(2):
            gfc = 352 + 160 * ck
            eF = T_(lambda e, ck=ck, gfc=gfc: e.matmul(
                PW2[0:NH, gfc:gfc + 128],
                g_tm[:, ck, :],
                lones[:, 0:128],
                start=True, stop=True),
                waits=[eg] + (bank_open("PW2") if ck == 0 else []),
                tag=f"Gf{it}_{ck}")
            egfc = V_(lambda e, ck=ck, gfc=gfc: e.tensor_copy(
                out=Gfm[:, :, ck, :],
                in_=PW2[0:NH, gfc:gfc + 128].rearrange("p (b t) -> p b t", b=B)),
                waits=[eF], tag=f"Gfc{it}_{ck}")
            bank_read("PW2", egfc)
            evba[("gfm", ck)] = egfc
            evba[("gfmfree", ck + 1)] = egfc
        Gl = [Gpos[0:16].rearrange("p ck b r -> p b ck r"), Gl1s[:],
              Gpos[32:48].rearrange("p ck b r -> p b ck r")]
        Gn = [Gneg[0:16].rearrange("p ck b r -> p b ck r"), Gn1s[:],
              Gneg[32:48].rearrange("p ck b r -> p b ck r")]
        gsrc = Gfm[:].rearrange("p b c t -> p (b c t)")
        gl = ev_wfree.get("Gs")
        l1 = V_(lambda e: e.tensor_copy(
            out=Gl[0], in_=Gfm[:]),
            waits=[evba[("gfm", 1)], gl], tag=f"lv1{it}")
        r1 = V_(lambda e: e.tensor_tensor(
            out=Glv1[:].rearrange("p b (ck r) -> p b ck r", ck=2), in0=Gfm[:],
            in1=Gl[0], op=OP.subtract),
            waits=[l1], tag=f"rs1{it}")
        l2 = V_(lambda e: e.tensor_copy(
            out=Gl[1].rearrange("p b t -> p (b t)"),
            in_=Glv1[:].rearrange("p b t -> p (b t)")), waits=[r1], tag=f"lv2{it}")
        r2 = V_(lambda e: e.tensor_tensor(
            out=Glv2[:].rearrange("p b t -> p (b t)"),
            in0=Glv1[:].rearrange("p b t -> p (b t)"),
            in1=Gl[1].rearrange("p b t -> p (b t)"), op=OP.subtract),
            waits=[l2, ev_wfree.get("Glv2")], tag=f"rs2{it}")
        l3 = V_(lambda e: e.tensor_copy(
            out=Gl[2],
            in_=Glv2[:].rearrange("p b (ck r) -> p b ck r", ck=2)), waits=[r2], tag=f"lv3{it}")
        evng = []
        for i in range(3):
            evng.append(V_(lambda e, i=i: e.tensor_scalar(
                out=(Gn[i] if i != 1 else Gn[i].rearrange("p b (ck r) -> p b ck r", ck=2)),
                in0=(Gl[i] if i != 1 else Gl[i].rearrange("p b (ck r) -> p b ck r", ck=2)),
                scalar1=-1.0, scalar2=None, op0=OP.mult),
                waits=[l3], tag=f"ngl{it}_{i}"))
        ng = evng[-1]
        ef1 = S_(lambda e: e.activation(
            out=expGfm[:].rearrange("p b t -> p (b t)"), in_=gsrc, func=AFT.Exp),
            waits=[evba[("gfm", 1)], ev_wfree.get("expGfm")], tag=f"egf{it}")
        ec1 = V_(lambda e: e.tensor_tensor(
            out=Glv1[:].rearrange("p b (c t) -> p b c t", c=2),
            in0=Gfm[:, :, :, 63:64].broadcast_to([NH, B, 2, 64]),
            in1=Gfm[:], op=OP.subtract), waits=[evba[("gfm", 1)], l3], tag=f"gcf{it}")
        ec2 = S_(lambda e: e.activation(
            out=eGceFM[:].rearrange("p b t -> p (b t)"),
            in_=Glv1[:].rearrange("p b t -> p (b t)"), func=AFT.Exp),
            waits=[ec1, ev_wfree.get("eGceFM")], tag=f"gce2{it}")
        evlb = []
        for ck in range(2):
            evlb.append(T_(lambda e, ck=ck: e.transpose(
                PW2[0:NH, ck * 128:ck * 128 + 128],
                lnb_tm[:, ck, :],
                identf[:]),
                waits=[evba[("lnb", 0)], evba[("lnb", 1)],
                       ev_wfree.get("pw2")] + (bank_open("PW2") if ck == 0 else []),
                tag=f"lbt{it}_{ck}"))
        elnb = V_(lambda e: e.tensor_copy(
            out=lnbfm[:].rearrange("p b (c r) -> p b c r", c=2),
            in_=PW2[0:NH, 0:256].rearrange("p (c b r) -> p b c r", c=2, b=B)),
            waits=[evlb[-1], ec2], tag=f"lbc{it}")
        bank_read("PW2", elnb)
        lb1 = V_(lambda e: e.tensor_copy(
            out=Lb[0][:].rearrange("p b t -> p (b t)"),
            in_=lnbfm[:].rearrange("p b t -> p (b t)")), waits=[elnb], tag=f"lb1{it}")
        lb2 = V_(lambda e: e.tensor_tensor(
            out=Lb[1][:].rearrange("p b t -> p (b t)"),
            in0=lnbfm[:].rearrange("p b t -> p (b t)"),
            in1=Lb[0][:].rearrange("p b t -> p (b t)"), op=OP.subtract),
            waits=[lb1], tag=f"lb2{it}")
        ev_wfree["Glv2"] = elnb
        ev_Gs = lb2
        ev_asm = []
        for bb_ in range(B):
            ev_asm.append(dma("gpsimd", "gdf", Gpos[16:32, :, bb_, :],
                              Gl1s[:, bb_, :], waits=[l3, ev_wfree.get("rhs")],
                              tag=f"asmP{it}{bb_}"))
            ev_asm.append(dma("gpsimd", "gdf", Gneg[16:32, :, bb_, :],
                              Gn1s[:, bb_, :], waits=[ng, ev_wfree.get("rhs")],
                              tag=f"asmN{it}{bb_}"))
            for i in range(2):
                ev_asm.append(dma("gpsimd", "gdf",
                                  Gneg[54 + 16 * i:70 + 16 * i, :, bb_, :],
                                  Lb[i][:, bb_, :], waits=[lb2, ev_wfree.get("rhs")],
                                  tag=f"asmL{it}_{i}{bb_}"))
        ev_flats = list(ev_asm)
        for ck in range(2):
            for b in range(B):
                base = 48 if b == 0 else 53
                ev_flats.append(dma(
                    "gpsimd", "gdf", rhsA[base:base + 3, ck, :],
                    Gneg[0:48, ck, b, :],
                    waits=[ev_asm[-1]], tag=f"fA{it}_{ck}_{b}"))
                ev_flats.append(dma(
                    "gpsimd", "gdf", rhsA[base + 3:base + 5, ck, :],
                    Gneg[54:86, ck, b, :],
                    waits=[ev_asm[-1]], tag=f"fAl{it}_{ck}_{b}"))
                b3 = 48 if b == 0 else 51
                ev_flats.append(dma(
                    "gpsimd", "gdf", rhs3[b3:b3 + 3, ck, :],
                    Gpos[0:48, ck, b, :],
                    waits=[ev_asm[-1]], tag=f"f3{it}_{ck}_{b}"))
        ev_flats_done = ev_flats[-1]

        evq = []
        for ic in range(NCC):
            half = (ic % 2) * 512
            evb = T_(lambda e, ic=ic, half=half: e.matmul(
                PW2[:, half:half + FW], hmaskT[:, ic, :],
                expGfm[:].rearrange("p b t -> p (b t)"),
                start=True, stop=True),
                waits=[ef1, ev_sc_done["q"]] + bank_open("PW2"),
                tag=f"bq{it}{ic}")
            evq.append(V_(lambda e, ic=ic, half=half: e.tensor_tensor(
                out=qtil[:, ic, :, :].rearrange("p b t -> p (b t)"),
                in0=qhat[:, ic, :, :].rearrange("p b t -> p (b t)"),
                in1=PW2[:, half:half + FW], op=OP.mult),
                waits=[evb, ev_wfree.get("qtil")], tag=f"qt{it}{ic}"))
            bank_read("PW2", evq[-1])
        ev_pre["qtil"] = evq[-1]
        k2prod = dvb[:].rearrange("p c w -> p (c w)").rearrange(
            "p (a b t) -> p a b t", a=NCC, b=B)
        evk = []
        for ic in range(NCC):
            half = (ic % 2) * 512
            evb = T_(lambda e, ic=ic, half=half: e.matmul(
                PW2[:, half:half + FW], hmaskT[:, ic, :],
                eGceFM[:].rearrange("p b t -> p (b t)"),
                start=True, stop=True),
                waits=[ec2, ev_sc_done["k"]] + bank_open("PW2"),
                tag=f"bk{it}{ic}")
            evk.append(V_(lambda e, ic=ic, half=half: e.tensor_tensor(
                out=k2prod[:, ic, :, :],
                in0=khat[:, ic, :, :],
                in1=PW2[:, half:half + FW].rearrange("p (b t) -> p b t", b=B),
                op=OP.mult),
                waits=[evb, ev_wfree.get("k2prod")], tag=f"k2{it}{ic}"))
            bank_read("PW2", evk[-1])
        evkt = []
        for b in range(B):
            for ck in range(2):
                for g2 in range(2):
                    evt = None
                    for q4 in range(4):
                        ic = g2 * 4 + q4
                        evt = T_(lambda e, b=b, ck=ck, ic=ic, q4=q4: e.transpose(
                            PW2b[0:64, q4 * 128:q4 * 128 + 128],
                            k2prod[:, ic, b, ck * 64:ck * 64 + 64],
                            ident[:]),
                            waits=[evk[-1]] + (bank_open("PW2") if q4 == 0 else []),
                            tag=f"k2t{it}_{b}{ck}{ic}")
                    ekt = S_(lambda e, b=b, ck=ck, g2=g2: e.activation(
                        out=k2tm[64 * b:64 * b + 64, ck, g2 * 512:g2 * 512 + 512],
                        in_=PW2b[0:64, 0:512], func=AFT.Copy),
                        waits=[evt, ev_wfree.get("k2tm")], tag=f"k2e{it}_{b}{ck}{g2}")
                    evkt.append(ekt)
                    bank_read("PW2", ekt)
        ev_pre["k2tm"] = evkt[-1]

        evD = {}
        ev_po_evac = {}
        prev_serial = []
        for ck in range(2):
            wA = [ev_Gs, ev_flats_done, ev_const] + bank_open("PW1")
            for hw_ in range(2):
                mA1 = T_(lambda e, ck=ck, hw_=hw_: e.matmul(
                    PW1[:, hw_ * 512:hw_ * 512 + 512],
                    Gpos[:, ck, :, :],
                    rhsA[:, ck, hw_ * 512:hw_ * 512 + 512], start=True, stop=False),
                    waits=wA, tag=f"dgA{it}_{ck}_{hw_}")
                mA2 = T_(lambda e, hw_=hw_: e.matmul(
                    PW1[:, hw_ * 512:hw_ * 512 + 512], ident[:],
                    maskA[:, 0:512],
                    start=False, stop=True), waits=[mA1], tag=f"dgAm{it}_{ck}_{hw_}")
            x1 = S_(lambda e: e.activation(out=Dexp1[:], in_=PW1[:, 0:WIDE],
                                           func=AFT.Exp),
                    waits=[mA2, evD.get(("d1f", ck))] + prev_serial,
                    tag=f"dxA{it}_{ck}")
            bank_read("PW1", x1)
            w2 = [x1] + bank_open("PW1")
            for hw_ in range(2):
                m21 = T_(lambda e, ck=ck, hw_=hw_: e.matmul(
                    PW1[:, hw_ * 512:hw_ * 512 + 512],
                    Gneg[0:54, ck, :, :],
                    rhs3[0:54, ck, hw_ * 512:hw_ * 512 + 512], start=True, stop=False),
                    waits=w2, tag=f"dg2{it}_{ck}_{hw_}")
                m22 = T_(lambda e, hw_=hw_: e.matmul(
                    PW1[:, hw_ * 512:hw_ * 512 + 512], ident[:],
                    mask2[:, 0:512],
                    start=False, stop=True), waits=[m21], tag=f"dg2m{it}_{ck}_{hw_}")
            x2 = S_(lambda e: e.activation(out=Dexp2[:], in_=PW1[:, 0:WIDE],
                                           func=AFT.Exp),
                    waits=[m22, evD.get(("d2f", ck))] + prev_serial,
                    tag=f"dx2{it}_{ck}")
            bank_read("PW1", x2)
            w3 = [x2] + bank_open("PW1")
            for hw_ in range(2):
                m31 = T_(lambda e, ck=ck, hw_=hw_: e.matmul(
                    PW1[:, hw_ * 512:hw_ * 512 + 512],
                    Gneg[:, ck, :, :],
                    rhs3[:, ck, hw_ * 512:hw_ * 512 + 512], start=True, stop=False),
                    waits=w3, tag=f"dg3{it}_{ck}_{hw_}")
                m32 = T_(lambda e, hw_=hw_: e.matmul(
                    PW1[:, hw_ * 512:hw_ * 512 + 512], ident[:],
                    mask3[:, 0:512],
                    start=False, stop=True), waits=[m31], tag=f"dg3m{it}_{ck}_{hw_}")
            x3 = S_(lambda e: e.activation(out=Dexp3[:], in_=PW1[:, 0:WIDE],
                                           func=AFT.Exp),
                    waits=[m32, evD.get(("d3f", ck))] + prev_serial,
                    tag=f"dx3{it}_{ck}")
            bank_read("PW1", x3)
            evkk = []
            for s in range(NH):
                par, j = s // 8, s % 8
                for b in range(B):
                    evkk.append(T_(lambda e, ck=ck, s=s, par=par, j=j, b=b: e.matmul(
                        PW2[64 * b:64 * b + 64, s * 64:s * 64 + 64],
                        khat[64 * par:64 * par + 64, j, b, ck * 64:ck * 64 + 64],
                        khat[64 * par:64 * par + 64, j, b, ck * 64:ck * 64 + 64],
                        start=True, stop=True, tile_position=(64 * par, 64 * b)),
                        waits=[ev_pre["khat"], ev_pre["vtm"]]
                        + (bank_open("PW2") if (s == 0 and b == 0) else []),
                        tag=f"kk{it}_{ck}_{s}_{b}"))
            eB = V_(lambda e, ck=ck: e.scalar_tensor_tensor(
                out=Bw[:], in0=PW2[:, 0:WIDE], scalar=-1.0, in1=Dexp1[:],
                op0=OP.mult, op1=OP.mult),
                waits=[evkk[-1], x1, ev_wfree.get("Bw")], tag=f"B{it}_{ck}")
            bank_read("PW2", eB)
            eBT = V_(lambda e, ck=ck: e.scalar_tensor_tensor(
                out=BTw[:], in0=PW2[:, 0:WIDE], scalar=-1.0, in1=Dexp3[:],
                op0=OP.mult, op1=OP.mult),
                waits=[x3, ev_wfree.get("BTw")], tag=f"BT{it}_{ck}")
            bank_read("PW2", eBT)
            evD[("d1f", ck + 1)] = eB
            evD[("d3f", ck + 1)] = eBT
            evqk = []
            for s in range(NH):
                par, j = s // 8, s % 8
                for b in range(B):
                    evqk.append(T_(lambda e, ck=ck, s=s, par=par, j=j, b=b: e.matmul(
                        PW1[64 * b:64 * b + 64, s * 64:s * 64 + 64],
                        khat[64 * par:64 * par + 64, j, b, ck * 64:ck * 64 + 64],
                        qhat[64 * par:64 * par + 64, j, b, ck * 64:ck * 64 + 64],
                        start=True, stop=True, tile_position=(64 * par, 64 * b)),
                        waits=[ev_pre["qhat"], x3]
                        + (bank_open("PW1") if (s == 0 and b == 0) else []),
                        tag=f"qk{it}_{ck}_{s}_{b}"))
            eM = V_(lambda e, ck=ck: e.tensor_tensor(
                out=Mmw[:], in0=PW1[:, 0:WIDE], in1=Dexp2[:], op=OP.mult),
                waits=[evqk[-1], x2, ev_wfree.get("Mmw")], tag=f"Mm{it}_{ck}")
            bank_read("PW1", eM)
            evD[("d2f", ck + 1)] = eM
            evD[("B", ck)] = eB
            evD[("BT", ck)] = eBT
            evD[("Mm", ck)] = eM
            evD[("pw1f", ck + 1)] = eM
            evD[("pw2f", ck + 1)] = eBT

            evp = []
            for s in range(NH):
                par, j = s // 8, s % 8
                for b in range(B):
                    evp.append(T_(lambda e, s=s, par=par, j=j, b=b, ck=ck: e.matmul(
                        PW1[64 * b:64 * b + 64, s * 64:s * 64 + 64],
                        khat[64 * par:64 * par + 64, j, b, ck * 64:ck * 64 + 64],
                        Sb16[64 * par:64 * par + 64, b, j, :],
                        start=True, stop=True, tile_position=(64 * par, 64 * b)),
                        waits=[ev_state_Sb, eM, ev_pre["khat"]]
                        + (bank_open("PW1") if (s == 0 and b == 0) else []),
                        tag=f"pr{it}_{ck}_{s}_{b}"))
            ev_last_pr = evp[-1]
            eu1 = V_(lambda e, ck=ck: e.tensor_tensor(
                out=tscr[:].rearrange("p (h v) -> p h v", h=NH),
                in0=PW1[:, 0:WIDE].rearrange("p (h v) -> p h v", h=NH),
                in1=expG_tm[:, ck, :].unsqueeze(2).broadcast_to([128, NH, 64]),
                op=OP.mult), waits=[evp[-1], evba[("expG", ck)],
                                    ev_wfree.get("oxs")], tag=f"u1{it}_{ck}")
            bank_read("PW1", eu1)
            eu2 = V_(lambda e, ck=ck: e.tensor_tensor(
                out=uh[:, ck, :].rearrange("p (par j d) -> p par j d", par=2, j=8),
                in0=vtm[:, ck, :].rearrange("p (j par d) -> p par j d", j=8, par=2),
                in1=tscr[:].rearrange("p (par j d) -> p par j d", par=2, j=8),
                op=OP.subtract),
                waits=[eu1, ev_pre["vtm"], ev_uh_use], tag=f"u2{it}_{ck}")
            ev_uh_w = eu2

            Msrc = [Bw[:], McT[:], Mx[:], McT[:], Mx[:]]
            Nsrc = [BTw[:], Ncr[:], Nx[:], Ncr[:], Nx[:], Ncr[:]]
            evMw = {0: evD[("B", ck)]}
            evNw = {0: evD[("BT", ck)]}
            ycur, ev_y = uh[:, ck, :], [eu2, eu2]
            ev_pw1_reader = eu1
            ev_pw2_reader = eBT
            for l in range(6):
                evs = []
                for half in range(2):
                    evh = []
                    for b in range(B):
                        for h in range(half * 8, half * 8 + 8):
                            evh.append(T_(lambda e, b=b, h=h, l=l, N=Nsrc[l], yc=ycur: e.matmul(
                                PW1[64 * b:64 * b + 64, h * 64:h * 64 + 64],
                                N[64 * b:64 * b + 64, h * 64:h * 64 + 64],
                                yc[64 * b:64 * b + 64, h * 64:h * 64 + 64],
                                start=True, stop=True, tile_position=(64 * b, 64 * b)),
                                waits=[evNw[l], ev_y[half]]
                                + (bank_open("PW1") if (half == 0 and b == 0 and h == 0) else []),
                                tag=f"ap{it}_{ck}_{l}_{b}_{h}"))
                    evs.append(evh)
                ynext = (yv if l % 2 == 0 else yv2)
                ev_yh = []
                for half in range(2):
                    ev_yh.append(V_(lambda e, yn=ynext, yc=ycur, half=half: e.tensor_tensor(
                        out=yn[:, half * 512:half * 512 + 512],
                        in0=PW1[:, half * 512:half * 512 + 512],
                        in1=yc[:, half * 512:half * 512 + 512], op=OP.add),
                        waits=[evs[half][-1], ev_wfree.get("yv")],
                        tag=f"ye{it}_{ck}_{l}_{half}"))
                    bank_read("PW1", ev_yh[-1])
                ycur, ev_y = ynext[:], ev_yh
                if l < 5:
                    evsqN = []
                    for b in range(B):
                        for h in range(NH):
                            evsqN.append(T_(lambda e, b=b, h=h, M=Msrc[l], N=Nsrc[l]: e.matmul(
                                PW2[64 * b:64 * b + 64, h * 64:h * 64 + 64],
                                M[64 * b:64 * b + 64, h * 64:h * 64 + 64],
                                N[64 * b:64 * b + 64, h * 64:h * 64 + 64],
                                start=True, stop=True, tile_position=(64 * b, 64 * b)),
                                waits=[evMw[l], evNw[l]]
                                + (bank_open("PW2") if (b == 0 and h == 0) else []),
                                tag=f"sqN{it}_{ck}_{l}_{b}_{h}"))
                    eNev = S_(lambda e, dstN=Nsrc[l + 1]: e.activation(
                        out=dstN, in_=PW2[:, 0:WIDE], func=AFT.Copy),
                        waits=[evsqN[-1], evNw.get(l - 1)], tag=f"Ne{it}_{ck}_{l}")
                    evNw[l + 1] = eNev
                    bank_read("PW2", eNev)
                    if l < 4:
                        evsqM = []
                        for b in range(B):
                            for h in range(NH):
                                evsqM.append(T_(lambda e, b=b, h=h, M=Msrc[l], N=Nsrc[l]: e.matmul(
                                    PO[64 * b:64 * b + 64, h * 64:h * 64 + 64],
                                    N[64 * b:64 * b + 64, h * 64:h * 64 + 64],
                                    M[64 * b:64 * b + 64, h * 64:h * 64 + 64],
                                    start=True, stop=True, tile_position=(64 * b, 64 * b)),
                                    waits=[evNw[l], evMw[l]]
                                    + (bank_open("PO") if (b == 0 and h == 0) else []),
                                    tag=f"sqM{it}_{ck}_{l}_{b}_{h}"))
                        eMev = S_(lambda e, dstM=Msrc[l + 1]: e.activation(
                            out=dstM, in_=PO[:, 0:WIDE], func=AFT.Copy),
                            waits=[evsqM[-1], evMw.get(l - 1)], tag=f"Me{it}_{ck}_{l}")
                        evMw[l + 1] = eMev
                        bank_read("PO", eMev)
            ev_wfree["yv"] = None
            edv = V_(lambda e, ck=ck, yc=ycur: e.tensor_tensor(
                out=dvb[:, ck, :].rearrange("p (h v) -> p h v", h=NH),
                in0=yc.rearrange("p (h v) -> p h v", h=NH),
                in1=beta_tm[:, ck, :].unsqueeze(2).broadcast_to([128, NH, 64]),
                op=OP.mult), waits=[ev_y[0], ev_y[1], ev_pre["k2tm"],
                                    ev_wfree.get("dvb")],
                tag=f"dv{it}_{ck}")

            evo = []
            for s in range(NH):
                par, j = s // 8, s % 8
                for b in range(B):
                    evo.append(T_(lambda e, s=s, par=par, j=j, b=b, ck=ck: e.matmul(
                        PO[64 * par:64 * par + 64, b * 512 + j * 64:b * 512 + j * 64 + 64],
                        dvb[64 * b:64 * b + 64, ck, s * 64:s * 64 + 64],
                        Mmw[64 * b:64 * b + 64, s * 64:s * 64 + 64],
                        start=True, stop=True, tile_position=(64 * b, 64 * par)),
                        waits=[edv, eM]
                        + (bank_open("PO") if (s == 0 and b == 0) else []),
                        tag=f"oi{it}_{ck}_{s}_{b}"))
            evox = []
            for s in range(NH):
                par, j = s // 8, s % 8
                for b in range(B):
                    evox.append(T_(lambda e, par=par, j=j, b=b, ck=ck: e.matmul(
                        PW1[64 * par:64 * par + 64, j * 128 + b * 64:j * 128 + b * 64 + 64],
                        Sb16[64 * par:64 * par + 64, b, j, :],
                        qtil[64 * par:64 * par + 64, j, b, ck * 64:ck * 64 + 64],
                        start=True, stop=True, tile_position=(64 * par, 64 * par)),
                        waits=[ev_state_Sb, ev_pre["qtil"]]
                        + (bank_open("PW1") if (s == 0 and b == 0) else []),
                        tag=f"ox{it}_{ck}_{s}_{b}"))
            oxs = oxsv
            ev_oxe = S_(lambda e, oxs=oxs: e.activation(
                out=oxs, in_=PW1[:, 0:1024], func=AFT.Copy),
                waits=[evox[-1], eu2, ev_wfree.get("oxs")],
                tag=f"oxe{it}_{ck}")
            bank_read("PW1", ev_oxe)
            ev_po_evac[ck + 1] = V_(lambda e, ck=ck, oxs=oxs: e.tensor_tensor(
                out=qhat[:, :, :, ck * 64:ck * 64 + 64],
                in0=PO[:, 0:1024].rearrange("p (b j t) -> p j b t", b=B, j=NCC),
                in1=oxs.rearrange("p (j b t) -> p j b t", j=NCC, b=B),
                op=OP.add),
                waits=[evo[-1], ev_oxe, ev_wfree.get("obuf")], tag=f"oe{it}_{ck}")
            bank_read("PO", ev_po_evac[ck + 1])
            ev_wfree["oxs"] = ev_po_evac[ck + 1]

            egr = []
            for b in range(B):
                egr.append(dma("gpsimd", "gdr", rowb[0:1, b, :],
                               G_tm[64 * b + 63:64 * b + 64, ck, :],
                               waits=[ecp, ev_wfree.get("rowbfree")],
                               tag=f"rb{it}_{ck}_{b}"))
            eeg = S_(lambda e: e.activation(
                out=rowbb[0:1].rearrange("p par (b j) -> p par b j", b=B),
                in_=rowb[0:1, :, :].rearrange("p b (par j) -> p par b j", par=2),
                func=AFT.Exp),
                waits=[egr[-1], ev_wfree.get("dwfree")], tag=f"egc{it}_{ck}")
            ev_wfree["rowbfree"] = eeg
            evba[("eGCp", ck)] = eeg
            edw = None
            for par in range(2):
                edw = T_(lambda e, par=par: e.matmul(
                    PW1[64 * par:64 * par + 64, 0:B * NCC],
                    onesrow[0:1, 0:64],
                    rowbb[0:1, par, :],
                    start=True, stop=True, tile_position=(0, 64 * par)),
                    waits=[evba[("eGCp", ck)]]
                    + (bank_open("PW1") if par == 0 else []),
                    tag=f"dw{it}_{ck}_{par}")
            ev_wfree["dwfree"] = edw
            es1 = V_(lambda e, ck=ck: e.tensor_tensor(
                out=S32b[:].rearrange("p (a b) -> p a b", a=B * NCC),
                in0=S32[:].rearrange("p a b c -> p (a b) c"),
                in1=PW1[:, 0:B * NCC].unsqueeze(2).broadcast_to([128, B * NCC, DH]),
                op=OP.mult), waits=[edw, ev_state_S, ev_po_evac[ck + 1],
                                    ev_wfree.get("oxs")], tag=f"sd{it}_{ck}")
            bank_read("PW1", es1)
            evst = []
            for s in range(NH):
                par, j = s // 8, s % 8
                hn = 2 * j + par
                for b in range(B):
                    evst.append(T_(lambda e, s=s, par=par, j=j, hn=hn, b=b, ck=ck: e.matmul(
                        PW2[64 * par:64 * par + 64, b * 512 + j * 64:b * 512 + j * 64 + 64],
                        k2tm[64 * b:64 * b + 64, ck, hn * 64:hn * 64 + 64],
                        dvb[64 * b:64 * b + 64, ck, s * 64:s * 64 + 64],
                        start=True, stop=True, tile_position=(64 * b, 64 * par)),
                        waits=[edv, ev_pre["k2tm"]]
                        + (bank_open("PW2") if (s == 0 and b == 0) else []),
                        tag=f"st{it}_{ck}_{s}_{b}"))
            es2b = V_(lambda e: e.tensor_tensor(
                out=S32[:].rearrange("p a b c -> p (a b c)"),
                in0=S32b[:, 0:1024],
                in1=PW2[:, 0:1024], op=OP.add),
                waits=[es1, evst[-1]], tag=f"sa{it}_{ck}")
            bank_read("PW2", es2b)
            ev_state_S = es2b
            ev_state_Sb = S_(lambda e: e.activation(
                out=Sb16[:].rearrange("p a b c -> p (a b c)"),
                in_=S32[:].rearrange("p a b c -> p (a b c)"), func=AFT.Copy),
                waits=[es2b], tag=f"sb{it}_{ck}")
            ev_pw2_reader = es2b
            evD[("pw2f", ck + 1)] = es2b
            evD[("pw1f", ck + 1)] = ev_pw1_reader
            prev_serial = [ev_y[0], ev_y[1], edv]
        ev_wfree["dvb"] = None
        ev_wfree["Bw"] = ev_y
        ev_wfree["BTw"] = list(ev_y)
        ev_wfree["Mmw"] = ev_po_evac[2]
        ev_wfree["vtm"] = ev_uh_w
        ev_wfree["k2tm"] = ev_state_Sb
        ev_obuf_done = ev_po_evac[2]

        eo2 = V_(lambda e: e.tensor_tensor(
            out=osq[:].rearrange("p a b t -> p (a b t)"),
            in0=qhat[:].rearrange("p a b t -> p (a b t)"),
            in1=qhat[:].rearrange("p a b t -> p (a b t)"), op=OP.mult),
            waits=[ev_obuf_done, ev_state_S], tag=f"o2{it}")
        evmm = None
        for ic in range(NCC):
            evmm = T_(lambda e, ic=ic: e.matmul(
                PW2[0:16, 0:FW], hmask[:, ic, 0:16],
                osq[:, ic, :, :].rearrange("p b t -> p (b t)"),
                start=(ic == 0), stop=(ic == NCC - 1)),
                waits=[eo2] + (bank_open("PW2") if ic == 0 else []),
                tag=f"os{it}{ic}",
                inc="tp" if ic == NCC - 1 else None)
        eol = S_(lambda e: e.activation(
            out=osc[0:16, :], in_=PW2[0:16, 0:FW], func=AFT.Ln,
            bias=epso[0:16, 0:1], scale=1.0 / 64.0), waits=[evmm], tag=f"ol{it}")
        bank_read("PW2", eol)
        eoe = S_(lambda e: e.activation(
            out=osc[0:16, :], in_=osc[0:16, :], func=AFT.Exp, scale=-0.5),
            waits=[eol], tag=f"oe2{it}")
        evon = []
        for ic in range(NCC):
            half = (ic % 2) * 512
            evb = T_(lambda e, ic=ic, half=half: e.matmul(
                PW1[:, half:half + FW], hmaskT[:, ic, :], osc[0:16, :],
                start=True, stop=True),
                waits=[eoe] + bank_open("PW1"),
                tag=f"ob{it}{ic}")
            ev1 = V_(lambda e, ic=ic, half=half: e.tensor_tensor(
                out=osq[:, ic, :, :].rearrange("p b t -> p (b t)"),
                in0=qhat[:, ic, :, :].rearrange("p b t -> p (b t)"),
                in1=PW1[:, half:half + FW], op=OP.mult),
                waits=[evb], tag=f"on{it}{ic}")
            bank_read("PW1", ev1)
            evon.append(V_(lambda e, ic=ic: e.tensor_tensor(
                out=qhat[:, ic, :, :].rearrange("p b t -> p (b t)"),
                in0=osq[:, ic, :, :].rearrange("p b t -> p (b t)"),
                in1=gatebuf[:, ic, :, :].rearrange("p b t -> p (b t)"), op=OP.mult),
                waits=[ev1, ev_gate_done], tag=f"og{it}{ic}"))
        ev_ofinal = evon[-1]

        for oc in range(NCC):
            pocol = ((oc // 2) % 2) * 512 + (oc % 2) * FW
            ex = dma("sync", "sdf0" if oc % 2 == 0 else "sdf1", xfm[:, oc % 2, :].rearrange("p (b t) -> p b t", b=B),
                     x_fm_d[:, oc * 128:(oc + 1) * 128, t0:t0 + TILE].rearrange("b p t -> p b t"),
                     waits=[ev_laststore.get((it - 1, oc)), ev_laststore.get((it, oc - 2))],
                     tag=f"xf{it}{oc}")
            evmm = None
            for kc in range(NCC):
                evmm = T_(lambda e, oc=oc, kc=kc, pocol=pocol: e.matmul(
                    PO[:, pocol:pocol + FW], W["o"][:, kc, oc * 128:(oc + 1) * 128],
                    qhat[:, kc, :, :].rearrange("p b t -> p (b t)"),
                    start=(kc == 0), stop=(kc == NCC - 1)),
                    waits=[ev_ofinal, ev_wcast[("o", NCC - 1)]]
                    + (bank_open("PO") if kc == 0 else []),
                    tag=f"wo{it}{oc}_{kc}", inc="tp" if kc == NCC - 1 else None)
            er1 = V_(lambda e, oc=oc, pocol=pocol: e.scalar_tensor_tensor(
                out=resid[:, oc % 2, :], in0=PO[:, pocol:pocol + FW], scalar=1.0,
                in1=ffng[:, oc, :, :].rearrange("p b t -> p (b t)"),
                op0=OP.mult, op1=OP.mult),
                waits=[evmm, ev_ffn_done, ev_state_Sb, ev_ofinal,
                       ev_laststore.get((it, oc - 2)) or ev_laststore.get((it - 1, oc + 6))],
                tag=f"r1{it}{oc}")
            bank_read("PO", er1)
            er2 = V_(lambda e, oc=oc: e.tensor_tensor(
                out=resid[:, oc % 2, :], in0=resid[:, oc % 2, :],
                in1=xfm[:, oc % 2, :], op=OP.add),
                waits=[er1, ex], tag=f"r2{it}{oc}")
            ev_laststore[(it, oc)] = dma(
                "sync", "sdo0" if oc % 2 == 0 else "sdo1", y_fm_d[:, oc * 128:(oc + 1) * 128, t0:t0 + TILE].rearrange("b p t -> p b t"),
                resid[:, oc % 2, :].rearrange("p (b t) -> p b t", b=B),
                waits=[er2], tag=f"st{it}{oc}")
        ev_wfree["gate"] = er1
        ev_wfree["ffn"] = er1
        ev_wfree["oxs"] = [ev_laststore[(it, NCC - 2)], ev_laststore[(it, NCC - 1)]]
        ev_wfree["k2prod"] = [evo[-1], evst[-1]]
        ev_wfree["qhat"] = evmm
        ev_wfree["obuf"] = evmm
        ev_wfree["qtil"] = ev_po_evac[2]
        ev_Tend = P.last_ev["tensor"]
        ev_wfree["Tend"] = ev_Tend
        ev_wfree["Gs"] = ev_Tend
        ev_wfree["rhs"] = ev_Tend
        ev_wfree["expGfm"] = ev_Tend
        ev_wfree["eGceFM"] = ev_Tend
        ev_wfree["qsc"] = [ev_sc_done["q"][1], ev_sc_done["k"][1]]
        ev_wfree["tmsmalls"] = ev_Tend
        ev_wfree["khat"] = ev_last_pr
        ev_wfree["PO"] = ev_po_evac[2]
        ev_wfree["pw2"] = ev_state_Sb

    def _finalwait(e):
        last = None
        for snm in DMA_SEMS:
            n = P.final_cnt.get(snm, 0)
            if n:
                last = e.wait_ge(sems[snm], n * 16)
        return last
    P.stream = ("F",)
    P.op("sync", _finalwait, tag="final_wait")

    emit_groups = [[("P",), ("L", 0)]]
    for t in range(NT):
        g = [("S", t)]
        if t + 1 < NT:
            g.append(("L", t + 1))
        emit_groups.append(g)
    emit_groups.append([("F",)])

    blk = es.enter_context(nc.Block())
    P.emit(blk, sems, groups=emit_groups)
    es.close()
    return nc



_PROG_CACHE = {}


def _get_prog(T):
    if T not in _PROG_CACHE:
        cfg = CFG(T)
        _PROG_CACHE[T] = (cfg, build_program(cfg))
    return _PROG_CACHE[T]


def make_in_maps(cfg, inputs):
    x = np.ascontiguousarray(inputs["L0_attn_hidden"], dtype=np.float32)
    NBAT = x.shape[0]
    T = x.shape[1]
    consts = host_constants(cfg)

    def t(a):
        return np.ascontiguousarray(np.asarray(a, dtype=np.float32).T)

    shared = {
        "w_qt": t(inputs["L0_attn_Wq"]), "w_kt": t(inputs["L0_attn_Wk"]),
        "w_vt": t(inputs["L0_attn_Wv"]), "w_gt": t(inputs["L0_attn_Wg"]),
        "w_ot": t(inputs["L0_attn_Wo"]), "w_ft": t(inputs["L0_ffn_weight"]),
        "wbat": np.ascontiguousarray(
            np.concatenate([t(inputs["L0_attn_Wb"])[:, SIG],
                            t(inputs["L0_attn_Wa"])[:, SIG]], axis=1)),
        "alog": np.asarray(inputs["L0_attn_A_log"], np.float32)[SIG].reshape(1, -1),
        "dtrep": np.tile(np.asarray(inputs["L0_attn_dt_bias"], np.float32)[SIG][None, :],
                         (128, 1)),
        "normw2": np.tile(np.asarray(inputs["L0_attn_norm_w"], np.float32), 2)[:, None].copy(),
    }
    cw = np.stack([np.asarray(inputs["L0_attn_conv_q"], np.float32),
                   np.asarray(inputs["L0_attn_conv_k"], np.float32),
                   np.asarray(inputs["L0_attn_conv_v"], np.float32)])
    shared["convw"] = np.ascontiguousarray(
        cw.reshape(3, cfg.NCC, 128, 4).transpose(2, 0, 1, 3))
    for nm, a in consts.items():
        shared["c_" + nm] = np.ascontiguousarray(a)

    in_maps = []
    for ci in range(NCORES):
        m = dict(shared)
        xs = x[2 * ci:2 * ci + 2]
        m["x_tm"] = np.ascontiguousarray(xs)
        m["x_fm"] = np.ascontiguousarray(xs.transpose(0, 2, 1))
        in_maps.append(m)
    return in_maps


def kernel(**inputs):
    x = inputs["L0_attn_hidden"]
    NBAT, T, H = x.shape
    cfg, nc = _get_prog(T)
    in_maps = make_in_maps(cfg, inputs)
    res = run_bass_kernel_spmd(nc, in_maps, core_ids=list(range(NCORES)))
    outs = []
    for ci in range(NCORES):
        y = np.asarray(res.results[ci]["y_fm"])
        outs.append(y.transpose(0, 2, 1))
    return np.ascontiguousarray(np.concatenate(outs, axis=0), dtype=np.float32)


# version 19
# speedup vs baseline: 1.0780x; 1.0240x over previous
import numpy as np
import ml_dtypes
import contextlib

import concourse.bass as bass
import concourse.mybir as mybir
from concourse.bass_utils import run_bass_kernel_spmd
from concourse.alu_op_type import AluOpType as OP

F32 = mybir.dt.float32
BF16 = mybir.dt.bfloat16
AFT = mybir.ActivationFunctionType
BF = ml_dtypes.bfloat16

NCORES = 8
ENGINES = ("sync", "gpsimd", "vector", "scalar", "tensor")
DMA_SEMS = {"sdc", "sdw", "sdw1", "sdx", "sdx1", "sdf0", "sdf1", "sdo0", "sdo1", "gdr", "gdf"}
SIG = np.array([(s % 8) * 2 + s // 8 for s in range(16)])


class CFG:
    def __init__(self, T=2048):
        self.T = T
        self.H = 1024
        self.NH = 16
        self.DH = 64
        self.B = 2
        self.TILE = 128
        self.NT = T // self.TILE
        self.NCC = 8
        self.FW = self.B * self.TILE
        self.WIDE = self.NH * self.DH


class Ev:
    __slots__ = ("sem", "val", "eng")
    def __init__(self, sem, val, eng):
        self.sem, self.val, self.eng = sem, val, eng


class Prog:
    def __init__(self):
        self.ops = {e: [] for e in ENGINES}
        self.seq = 0
        self.stream = ("P",)
        self.last_ev = {}
        self._lastfam = None
        self.final_cnt = {}

    def unit(self, sem):
        return 16 if sem in DMA_SEMS else 1

    def op(self, eng, fn, waits=(), inc=None, tag=""):
        ev = None
        if inc is not None:
            ev = Ev(inc, None, eng)
            self.last_ev[eng] = ev
        flat = []
        def _fl(w):
            if w is None:
                return
            if isinstance(w, (list, tuple)):
                for x in w:
                    _fl(x)
            else:
                flat.append(w)
        for w in waits:
            _fl(w)
        self.ops[eng].append((fn, tuple(flat), inc, ev, tag, self.seq, self.stream))
        self.seq += 1
        return ev

    _COST = (
        ("pj", 160), ("wo", 160), ("vt", 130), ("xnt", 130), ("k2t", 130),
        ("kk", 105), ("qk", 105), ("pr", 105), ("ap", 105), ("sqN", 105),
        ("sqM", 105), ("oi", 105), ("ox", 105), ("st", 105),
        ("dg", 380), ("G", 100), ("Gf", 120), ("ba", 120),
        ("bc", 120), ("bq", 120), ("bk", 120), ("ob", 120),
        ("ss", 150), ("os", 150), ("dw", 60),
        ("cv", 520), ("ye", 1150), ("B", 1150), ("BT", 1150), ("Mm", 1150),
        ("u1", 1200), ("u2", 1200), ("dv", 1150), ("oe", 1200),
        ("sd", 1150), ("sa", 1150), ("sc", 560), ("qt", 560), ("k2", 560),
        ("on", 560), ("og", 560), ("gn", 2300), ("lns", 450), ("lnq", 450),
        ("lna", 450), ("r1", 600), ("r2", 600),
        ("pe", 1250), ("sv", 1250), ("sl", 1800), ("sq", 1800), ("dx", 1250),
        ("k2e", 650), ("vte", 650), ("xne", 650), ("oxe", 1150),
        ("sb", 650), ("Ne", 1250), ("Me", 1250),
        ("xtm", 2500), ("xf", 2500), ("st", 2500), ("c_", 2000),
        ("wd", 2500), ("fA", 1200), ("f3", 1200), ("asm", 800), ("rb", 800),
    )

    _FAM = (
        ("sig", 1), ("sl", 1), ("sv", 1), ("lnq", 2),
        ("lnl", 0), ("lne", 0), ("re", 0), ("spe", 0), ("spl", 0),
        ("Ge", 0), ("egc", 0), ("gce2", 0), ("egf", 0), ("ol", 0),
        ("oe2", 0),
    )

    def _family(self, tag):
        for pfx, f in self._FAM:
            if tag.startswith(pfx):
                return f
        return None

    def _cost(self, tag, eng):
        if eng == "gpsimd" and tag[:2] in ("sq", "o2", "gn"):
            return 3500
        if eng in ("sync", "gpsimd"):
            return 900
        for pfx, c in self._COST:
            if tag.startswith(pfx):
                return c
        return 300

    def _final_order(self, groups):
        if groups is None:
            return {e: list(self.ops[e]) for e in ENGINES}
        sub = {e: {} for e in ENGINES}
        for e in ENGINES:
            for op in self.ops[e]:
                sub[e].setdefault(op[6], []).append(op)
        seen = set()
        for g in groups:
            seen.update(g)
        for e in ENGINES:
            for k in sub[e]:
                assert k in seen, f"stream {k} not in emission groups"
        final = {e: [] for e in ENGINES}
        evt = {}
        clock = {e: 0.0 for e in ENGINES}
        for g in groups:
            lists = {}
            for e in ENGINES:
                for k in g:
                    lst = sub[e].get(k)
                    if lst:
                        lists[(e, k)] = lst
            idxs = {key: 0 for key in lists}
            remaining = sum(len(v) for v in lists.values())
            while remaining:
                best, beststart = None, None
                for key, lst in lists.items():
                    i = idxs[key]
                    if i >= len(lst):
                        continue
                    op = lst[i]
                    ok = True
                    start = clock[key[0]]
                    for w in op[1]:
                        t = evt.get(id(w))
                        if t is None:
                            ok = False
                            break
                        if t > start:
                            start = t
                    if not ok:
                        continue
                    if key[0] == "scalar":
                        fam = self._family(op[4])
                        if fam is not None and fam != self._lastfam:
                            start += 1300
                    if beststart is None or start < beststart:
                        best, beststart = key, start
                assert best is not None, (
                    f"stream merge stalled in group {g}")
                e, k = best
                op = lists[best][idxs[best]]
                idxs[best] += 1
                remaining -= 1
                final[e].append(op)
                dur = self._cost(op[4], e)
                if e == "scalar":
                    fam = self._family(op[4])
                    if fam is not None:
                        self._lastfam = fam
                end = beststart + dur
                clock[e] = end
                if op[3] is not None:
                    evt[id(op[3])] = end
        return final

    def emit(self, block, sems, groups=None):
        import os as _os
        import re as _re
        stubk = int(_os.environ.get("STUBK", "-1"))
        stubtag = _os.environ.get("STUBTAG", "")
        stubwin = _os.environ.get("STUBWIN", "")
        wlo, whi = (int(v) for v in stubwin.split(":")) if stubwin else (0, 0)
        final = self._final_order(groups)
        cnt = {}
        for e in ENGINES:
            for op in final[e]:
                if op[2] is not None:
                    cnt[op[2]] = cnt.get(op[2], 0) + 1
                    op[3].val = cnt[op[2]]
        self.final_cnt = cnt
        def mk(ename):
            oplist = final[ename]
            def body(eng):
                waited = {}
                for fn, waits, inc, ev, tag, seq, stream in oplist:
                    stub = (stubk >= 0 and seq >= stubk) or \
                           (wlo <= seq < whi) or \
                           (stubtag and _re.match(stubtag, tag))
                    for w in waits:
                        assert w.val is not None, f"unnumbered wait in {tag}"
                        tgt = w.val * self.unit(w.sem)
                        if waited.get(w.sem, -1) >= tgt:
                            continue
                        eng.wait_ge(sems[w.sem], tgt)
                        waited[w.sem] = tgt
                    if stub:
                        if inc is not None:
                            eng.sem_inc(sems[inc], self.unit(inc))
                        continue
                    instr = fn(eng)
                    if inc is not None:
                        assert instr is not None, f"inc-op returned None: {tag}"
                        instr.then_inc(sems[inc], self.unit(inc))
            return body
        for e in ENGINES:
            getattr(block, e)(mk(e))


def host_constants(cfg):
    NH = cfg.NH
    c = {}
    c["ident128"] = np.eye(128, dtype=BF)
    L = np.zeros((128, 128), np.float32)
    for blk in range(2):
        for s in range(64):
            L[blk * 64 + s, blk * 64 + s:blk * 64 + 64] = 1.0
    c["lones"] = L
    bo = np.zeros((3 * NH, NH * 64), np.float32)
    for l in range(3):
        for h in range(NH):
            bo[l * NH + h, h * 64:h * 64 + 64] = 1.0
    c["bones48"] = bo.astype(BF)
    c["bones32"] = bo[:2 * NH].astype(BF)
    r = np.arange(64)
    mA = np.where(r[:, None] > r[None, :], 0.0, -30000.0)
    m2 = np.where(r[None, :] >= r[:, None], 0.0, -30000.0)
    m3 = np.where(r[None, :] > r[:, None], 0.0, -30000.0)
    c["maskA"] = np.tile(np.concatenate([mA, mA], 0), (1, 8)).astype(BF)
    c["mask2"] = np.tile(np.concatenate([m2, m2], 0), (1, 8)).astype(BF)
    c["mask3"] = np.tile(np.concatenate([m3, m3], 0), (1, 8)).astype(BF)
    hm = np.zeros((cfg.NCC, 128, NH), np.float32)
    hmT = np.zeros((cfg.NCC, NH, 128), np.float32)
    for cc in range(cfg.NCC):
        hm[cc, 0:64, cc] = 1.0
        hm[cc, 64:128, 8 + cc] = 1.0
        hmT[cc, cc, 0:64] = 1.0
        hmT[cc, 8 + cc, 64:128] = 1.0
    c["hmask"] = hm.astype(BF)
    c["hmaskT"] = hmT.astype(BF)
    ho = np.zeros((2, 2 * 2 * 64), np.float32)
    ho[0, 0:64] = 1.0; ho[0, 128:192] = 1.0
    ho[1, 64:128] = 1.0; ho[1, 192:256] = 1.0
    c["halfones"] = ho.astype(BF)
    c["onesrow"] = np.ones((1, 64), dtype=BF)
    c["identf32"] = np.eye(128, dtype=np.float32)
    return c


def build_program(cfg):
    nc = bass.Bass("TRN2", target_bir_lowering=False)
    H, NH, DH, B = cfg.H, cfg.NH, cfg.DH, cfg.B
    TILE, NT, NCC, FW, WIDE = cfg.TILE, cfg.NT, cfg.NCC, cfg.FW, cfg.WIDE
    T = cfg.T
    consts = host_constants(cfg)

    def din(name, shape, dt=F32):
        return nc.dram_tensor(name, shape, dt, kind="ExternalInput").ap()

    x_tm_d = din("x_tm", [B, T, H])
    x_fm_d = din("x_fm", [B, H, T])
    w_d = {w: din(f"w_{w}t", [H, H]) for w in ("q", "k", "v", "g", "o", "f")}
    wbat_d = din("wbat", [H, 2 * NH])
    convw_d = din("convw", [128, 3, NCC, 4])
    alog_d = din("alog", [1, NH])
    dtrep_d = din("dtrep", [128, NH])
    normw2_d = din("normw2", [128, 1])
    cst_d = {nm: din("c_" + nm, list(a.shape), BF16 if a.dtype == BF else F32)
             for nm, a in consts.items()}
    y_fm_d = nc.dram_tensor("y_fm", [B, H, T], F32, kind="ExternalOutput").ap()

    es = contextlib.ExitStack()
    sb = lambda nm, sh, dt=F32: es.enter_context(nc.sbuf_tensor(nm, sh, dt))
    ps = lambda nm, sh, dt=F32: es.enter_context(nc.psum_tensor(nm, sh, dt))

    W = {w: sb(f"W{w}", [128, NCC, H], BF16) for w in ("q", "k", "v", "g", "o", "f")}
    Wba = sb("Wba", [128, NCC, 2 * NH], BF16)
    wstage = sb("wstage", [128, 1024])
    convw = sb("convw_t", [128, 3, NCC, 4])
    ident = sb("identt", [128, 128], BF16)
    identf = sb("identft", [128, 128])
    lones = sb("lonest", [128, 128])
    maskA = sb("maskAt", [128, 512], BF16)
    mask2 = sb("mask2t", [128, 512], BF16)
    mask3 = sb("mask3t", [128, 512], BF16)
    hmask = sb("hmaskt", [128, NCC, NH], BF16)
    hmaskT = sb("hmaskTt", [NH, NCC, 128], BF16)
    onesrow = sb("onesrowt", [1, 64], BF16)
    dtrep = sb("dtrept", [128, NH])
    normw2 = sb("normw2t", [128, 1])
    alog = sb("alogt", [1, NH])
    negexpA = sb("negexpAt", [128, NH])
    epsln = sb("epsln", [128, 1])
    epsl2 = sb("epsl2", [128, 1])
    epso = sb("epso", [128, 1])
    lnq8 = sb("lnq8", [128, 1])
    lnsq = sb("lnsq", [128, 1])
    lnsq2 = sb("lnsq2", [128, 1])
    rhsA = sb("rhsA", [58, 2, WIDE], BF16)
    rhs3 = sb("rhs3", [86, 2, WIDE], BF16)
    Gpos = sb("Gpos", [58, 2, B, 64], BF16)
    Gneg = sb("Gneg", [86, 2, B, 64], BF16)
    xtm = sb("xtm", [128, H])
    xfm = sb("xfm", [128, 2, FW])
    xnf = sb("xnf", [128, NCC, B, TILE], BF16)
    lnsc = sb("lnsc", [128, B, 2])
    prebuf = sb("prebuf", [128, NCC, B, TILE + 3], BF16)
    xnbv = prebuf[:].rearrange("p a b t -> p (a b t)")[:, 0:2 * H].rearrange(
        "p (b h) -> p b h", b=2)
    halo = sb("halo", [128, 3, NCC, B, 3], BF16)
    gatebuf = sb("gatebuf", [128, NCC, B, TILE], BF16)
    ffng = sb("ffng", [128, NCC, B, TILE], BF16)
    convacc = sb("convacc", [128, NCC, B, TILE], BF16)
    silbuf = sb("silbuf", [128, NCC, B, TILE], BF16)
    xtm2 = silbuf[:].rearrange("p a b t -> p (a b t)").bitcast(F32)[:, 0:H]
    khat = sb("khat", [128, NCC, B, TILE], BF16)
    qtil = sb("qtil", [128, NCC, B, TILE], BF16)
    qhat = sb("qhat", [128, NCC, B, TILE], BF16)
    vtm = sb("vtm", [128, 2, WIDE], BF16)
    k2tm = sb("k2tm", [128, 2, WIDE], BF16)
    uh = sb("uh", [128, 2, WIDE], BF16)
    dvb = sb("dvb", [128, 2, WIDE], BF16)
    yv = sb("yv", [128, WIDE], BF16)
    yv2 = sb("yv2", [128, WIDE], BF16)
    Dexp1 = yv
    Dexp3 = yv2
    Mc = sb("Mcc2", [128, WIDE], BF16) if False else None
    tscr = wstage
    S32b = wstage
    Dexp2 = sb("Dexp2", [128, WIDE], BF16)
    McT = Dexp2
    Bw = sb("Bww", [128, WIDE], BF16)
    BTw = sb("BTww", [128, WIDE], BF16)
    Mmw = sb("Mmww", [128, WIDE], BF16)

    Ncr = sb("Ncc", [128, WIDE], BF16)
    Mx = sb("Mxx", [128, WIDE], BF16)
    Nx = sb("Nxx", [128, WIDE], BF16)
    S32 = sb("S32", [128, B, NCC, DH])
    Sb16 = sb("Sb16", [128, B, NCC, DH], BF16)
    qscq = sb("qscq", [16, FW], BF16)
    qsck = sb("qsck", [16, FW], BF16)
    osc = sb("osc", [16, FW], BF16)
    beta_tm = sb("beta_tm", [128, 2, NH])
    lnb_tm = sb("lnb_tm", [128, 2, NH])
    g_tm = sb("g_tm", [128, 2, NH])
    G_tm = sb("G_tm", [128, 2, NH])
    expG_tm = sb("expG_tm", [128, 2, NH])
    tmsc1 = sb("tmsc1", [128, 2, NH])
    Gfm = sb("Gfm", [NH, B, 2, 64])
    Gl1s = sb("Gl1s", [NH, B, TILE], BF16)
    Gn1s = sb("Gn1s", [NH, B, TILE], BF16)
    Lb = [sb(f"Lb{i}", [NH, B, TILE], BF16) for i in range(2)]
    Glv1 = sb("Glv1", [NH, B, TILE])
    Glv2 = sb("Glv2", [NH, B, TILE])
    lnbfm = Glv2
    eGceFM = sb("eGceFM", [NH, B, TILE], BF16)
    expGfm = sb("expGfm", [NH, B, TILE], BF16)
    rowb = sb("rowb", [1, B, NH])
    rowbb = sb("rowbb", [1, 2, B * NCC], BF16)
    resid = wstage[:].rearrange("p (a b) -> p a b", a=2)[:, :, 0:FW]
    osq = wstage[:].bitcast(BF16).rearrange("p (a b t) -> p a b t", a=NCC, b=B)
    oxsv = wstage[:].bitcast(BF16)[:, 0:1024]

    PA = ps("PA", [128, 512])
    PB = ps("PBb", [128, 512])
    PW1 = ps("PW1", [128, 1024])
    PW2 = ps("PW2", [128, 1024])
    PO = ps("PO", [128, 1024])
    PAb = PA[:].bitcast(BF16)
    PW2b = PW2[:].bitcast(BF16)

    sems = {nm: es.enter_context(nc.semaphore(nm))
            for nm in ("sdc", "sdw", "sdw1", "sdx", "sdx1", "sdf0", "sdf1",
                       "sdo0", "sdo1", "gdr", "gdf", "tp", "vp", "sp", "gp")}
    P = Prog()

    def dma(q, sem, out, in_, waits=(), tag="", slow=False):
        return P.op(q, lambda e, o=out, i=in_, s=slow: e.dma_start(
            out=o, in_=i, allow_slow_non_contiguous=s),
            waits=waits, inc=sem, tag=tag)

    def T_(fn, waits=(), tag="", inc="tp"):
        return P.op("tensor", fn, waits=waits, inc=inc, tag=tag)

    def V_(fn, waits=(), tag="", inc="vp"):
        return P.op("vector", fn, waits=waits, inc=inc, tag=tag)

    def S_(fn, waits=(), tag="", inc="sp"):
        return P.op("scalar", fn, waits=waits, inc=inc, tag=tag)

    def G_(fn, waits=(), tag="", inc="gp"):
        return P.op("gpsimd", fn, waits=waits, inc=inc, tag=tag)

    bkr = {k: [] for k in ("PA", "PB", "PW1", "PW2", "PO")}

    def bank_open(bank):
        ws = bkr[bank]
        bkr[bank] = []
        return ws

    def bank_read(bank, ev):
        bkr[bank].append(ev)
        return ev

    evc = []
    for dst, nm in [(ident[:], "ident128"), (identf[:], "identf32"),
                    (lones[:], "lones"),
                    (maskA[:], "maskA"), (mask2[:], "mask2"), (mask3[:], "mask3"),
                    (onesrow[:], "onesrow")]:
        evc.append(dma("sync", "sdc", dst, cst_d[nm], tag=f"c_{nm}"))
    evc.append(dma("sync", "sdc", hmask[:], cst_d["hmask"].rearrange("a p b -> p a b"),
                   tag="c_hm"))
    evc.append(dma("sync", "sdc", hmaskT[:], cst_d["hmaskT"].rearrange("a p b -> p a b"),
                   tag="c_hmT"))
    evc.append(dma("sync", "sdc", convw[:].rearrange("p a b c -> p (a b c)"),
                   convw_d.rearrange("p a b c -> p (a b c)"), tag="c_cw"))
    evc.append(dma("sync", "sdc", dtrep[:], dtrep_d, tag="c_dt"))
    evc.append(dma("sync", "sdc", normw2[:], normw2_d, tag="c_nw"))
    evc.append(dma("sync", "sdc", alog[:], alog_d, tag="c_al"))
    for ck in range(2):
        evc.append(dma("sync", "sdc", rhsA[0:48, ck, :], cst_d["bones48"], tag="c_bA"))
        evc.append(dma("sync", "sdc", rhs3[0:48, ck, :], cst_d["bones48"], tag="c_b3a"))
        evc.append(dma("sync", "sdc", rhs3[54:86, ck, :], cst_d["bones32"], tag="c_b3b"))
    sel0 = cst_d["halfones"][0:1, :]
    sel1 = cst_d["halfones"][1:2, :]
    row = lambda ap: ap.rearrange("p ck b r -> p (ck b r)")
    for i in range(5):
        evc.append(dma("sync", "sdc", row(Gpos[48 + i:49 + i]), sel0, tag="hoA"))
        evc.append(dma("sync", "sdc", row(Gpos[53 + i:54 + i]), sel1, tag="hoB"))
    for i in range(3):
        evc.append(dma("sync", "sdc", row(Gneg[48 + i:49 + i]), sel0, tag="ho3A"))
        evc.append(dma("sync", "sdc", row(Gneg[51 + i:52 + i]), sel1, tag="ho3B"))
    ev_const = evc[-1]

    evm = []
    evm.append(V_(lambda e: e.memset(epsln[:], 1e-5), tag="m1"))
    evm.append(V_(lambda e: e.memset(epsl2[:], 1e-6), tag="m2"))
    evm.append(V_(lambda e: e.memset(epso[:], 1e-5), tag="m3"))
    evm.append(V_(lambda e: e.memset(lnq8[:], float(np.log(1.0 / 8.0))), tag="m4"))
    ev_S_w = V_(lambda e: e.memset(S32[:].rearrange("p a b c -> p (a b c)"), 0.0), tag="m5")
    ev_Sb_w = V_(lambda e: e.memset(Sb16[:].rearrange("p a b c -> p (a b c)"), 0.0), tag="m6")
    ev_halo0 = V_(lambda e: e.memset(halo[:].rearrange("p a b c d -> p (a b c d)"), 0.0), tag="m7")
    ev_minit = ev_halo0

    ev_e1 = S_(lambda e: e.activation(out=alog[:], in_=alog[:], func=AFT.Exp),
               waits=[ev_const], tag="expA")
    wA0 = [ev_e1] + bank_open("PB")
    ev_e2a = T_(lambda e: e.matmul(PB[0:64, 496:496 + NH], lones[0:1, 0:64], alog[:],
                                   start=True, stop=True, tile_position=(0, 0)),
                waits=wA0, tag="bcA0")
    ev_e2 = T_(lambda e: e.matmul(PB[64:128, 496:496 + NH], lones[0:1, 0:64], alog[:],
                                  start=True, stop=True, tile_position=(0, 64)),
               waits=[ev_e2a], tag="bcA1")
    ev_negA = V_(lambda e: e.tensor_scalar(out=negexpA[:], in0=PB[:, 496:496 + NH],
                                           scalar1=-1.0, scalar2=None, op0=OP.mult),
                 waits=[ev_e2], tag="negA")
    bank_read("PB", ev_negA)

    ev_wcast = {}
    half_cast = [None, None]
    nslot = 0
    for ic in range(NCC):
        hh = nslot % 2
        nslot += 1
        evd = dma("sync" if hh == 0 else "scalar", "sdw" if hh == 0 else "sdw1",
                  wstage[:, hh * 512:hh * 512 + 2 * NH],
                  wbat_d[ic * 128:(ic + 1) * 128, :],
                  waits=[half_cast[hh]], tag=f"wdba{ic}")
        half_cast[hh] = V_(lambda e, ic=ic, hh=hh: e.tensor_scalar(
            out=Wba[:, ic, :], in0=wstage[:, hh * 512:hh * 512 + 2 * NH],
            scalar1=1.0, scalar2=None, op0=OP.mult),
            waits=[evd], tag=f"wcba{ic}")
        ev_wcast[("ba", ic)] = half_cast[hh]
    for wname in ("q", "k", "v", "g", "o", "f"):
        for ic in range(NCC):
            last = None
            for ch in range(2):
                hh = nslot % 2
                nslot += 1
                evd = dma("sync" if hh == 0 else "scalar",
                          "sdw" if hh == 0 else "sdw1",
                          wstage[:, hh * 512:hh * 512 + 512],
                          w_d[wname][ic * 128:(ic + 1) * 128, ch * 512:ch * 512 + 512],
                          waits=[half_cast[hh]], tag=f"wd{wname}{ic}{ch}")
                if wname == "f":
                    last = S_(lambda e, ic=ic, hh=hh, ch=ch: e.sign(
                        out=W["f"][:, ic, ch * 512:ch * 512 + 512],
                        in_=wstage[:, hh * 512:hh * 512 + 512]),
                        waits=[evd], tag=f"wsg{ic}{ch}")
                else:
                    last = V_(lambda e, wn=wname, ic=ic, hh=hh, ch=ch: e.tensor_scalar(
                        out=W[wn][:, ic, ch * 512:ch * 512 + 512],
                        in0=wstage[:, hh * 512:hh * 512 + 512],
                        scalar1=1.0, scalar2=None, op0=OP.mult),
                        waits=[evd], tag=f"wc{wname}{ic}{ch}")
                half_cast[hh] = last
            ev_wcast[(wname, ic)] = last


    ev_xtm_use = ev_minit
    ev_xtm2_use = ev_minit
    ev_lnsq = None
    ev_uh_use = None
    ev_uh_w = None
    ev_state_S = ev_S_w
    ev_state_Sb = ev_Sb_w
    ev_halo = [ev_halo0, ev_halo0, ev_halo0]
    ev_prebuf_free = [None, None, None]
    bankfree = {0: None, 1: None}
    ev_wfree = {}
    ev_laststore = {}

    for it in range(NT):
        P.stream = ("L", it)
        t0 = it * TILE
        ev_lnap = []
        for b in range(B):
            xb = xtm if b == 0 else xtm2
            ev_xt = dma("sync", "sdx" if b == 0 else "sdx1", xb[:, :],
                        x_tm_d[b, t0:t0 + TILE, :],
                        waits=[ev_xtm_use if b == 0 else
                               [ev_xtm2_use, ev_wfree.get("silbuf")]],
                        tag=f"xtm{it}_{b}")
            e1 = V_(lambda e, b=b, xb=xb: e.tensor_scalar(
                out=xb[:, :], in0=xb[:, :], scalar1=1.0, scalar2=0.0,
                op0=OP.mult, op1=OP.add, accum_out=lnsc[:, b, 0:1]),
                waits=[ev_xt], tag=f"lns{it}_{b}")
            e2w = bank_open("PA")
            e2a = S_(lambda e, b=b, xb=xb: e.activation(
                out=PA[:, 0:512], in_=xb[:, 0:512], func=AFT.Square,
                accum_out=lnsc[:, b, 1:2]), waits=[ev_xt, e1, ev_lnsq] + e2w,
                tag=f"lnq{it}_{b}")
            e2b = S_(lambda e, b=b, xb=xb: e.activation(
                out=PA[:, 0:512], in_=xb[:, 512:1024], func=AFT.Square,
                accum_out=lnsq[:, 0:1] if b == 0 else lnsq2[:, 0:1]),
                waits=[e2a], tag=f"lnq2{it}_{b}")
            bank_read("PA", e2b)
            e2 = V_(lambda e, b=b: e.tensor_tensor(
                out=lnsc[:, b, 1:2], in0=lnsc[:, b, 1:2],
                in1=lnsq[:, 0:1] if b == 0 else lnsq2[:, 0:1],
                op=OP.add), waits=[e2b], tag=f"lnq3{it}_{b}")
            ev_lnsq = e2
            e3 = V_(lambda e, b=b: e.tensor_scalar(
                out=lnsc[:, b, :], in0=lnsc[:, b, :], scalar1=1.0 / H,
                scalar2=None, op0=OP.mult), waits=[e1, e2], tag=f"lnm{it}_{b}")
            e5 = V_(lambda e, b=b: e.scalar_tensor_tensor(
                out=lnsc[:, b, 1:2], in0=lnsc[:, b, 0:1], scalar=lnsc[:, b, 0:1],
                in1=lnsc[:, b, 1:2], op0=OP.mult, op1=OP.subtract),
                waits=[e3], tag=f"lnv{it}_{b}")
            e6 = V_(lambda e, b=b: e.tensor_scalar(
                out=lnsc[:, b, 1:2], in0=lnsc[:, b, 1:2], scalar1=-1.0,
                scalar2=None, op0=OP.mult), waits=[e5], tag=f"lnn{it}_{b}")
            e7 = S_(lambda e, b=b: e.activation(
                out=lnsc[:, b, 1:2], in_=lnsc[:, b, 1:2], func=AFT.Ln,
                bias=epsln[:, 0:1]), waits=[e6], tag=f"lnl{it}_{b}")
            e8 = S_(lambda e, b=b: e.activation(
                out=lnsc[:, b, 1:2], in_=lnsc[:, b, 1:2], func=AFT.Exp,
                scale=-0.5), waits=[e7], tag=f"lne{it}_{b}")
            e9 = V_(lambda e, b=b, xb=xb: e.tensor_scalar(
                out=xnbv[:, b, :], in0=xb[:, :],
                scalar1=lnsc[:, b, 0:1], scalar2=lnsc[:, b, 1:2],
                op0=OP.subtract, op1=OP.mult),
                waits=[e8, ev_prebuf_free[0], ev_prebuf_free[1],
                       ev_prebuf_free[2], ev_halo[0], ev_halo[1], ev_halo[2]],
                tag=f"lna{it}_{b}")
            ev_lnap.append(e9)
            if b == 0:
                ev_xtm_use = [e9, e1, e2b]
            else:
                ev_xtm2_use = [e9, e1, e2b]
                ev_wfree["silbuf"] = [e9, e2b, ev_wfree.get("silbuf")]

        ev_xnf = []
        for b in range(B):
            for g4 in range(2):
                evt = None
                for q4 in range(4):
                    ic = g4 * 4 + q4
                    evt = T_(lambda e, b=b, ic=ic, q4=q4: e.transpose(
                        PAb[:, q4 * 128:q4 * 128 + 128],
                        xnbv[:, b, ic * 128:ic * 128 + 128],
                        ident[:]),
                        waits=[ev_lnap[b]] + (bank_open("PA") if q4 == 0 else []),
                        tag=f"xnt{it}_{b}_{ic}")
                enf = S_(lambda e, b=b, g4=g4: e.activation(
                    out=xnf[:, g4 * 4:g4 * 4 + 4, b, :],
                    in_=PAb[:, 0:512].rearrange("p (a t) -> p a t", a=4), func=AFT.Copy),
                    waits=[evt, ev_wfree.get("xnf")], tag=f"xne{it}_{b}_{g4}")
                ev_xnf.append(enf)
                bank_read("PA", enf)
        ev_xnf_done = ev_xnf[-1]
        ev_uh_use = ev_xnf_done

        ev_pre = {}
        ev_sc_done = {}
        banks = [PA, PB]
        for wname in ("v", "k", "q"):
            wi = {"q": 0, "k": 1, "v": 2}[wname]
            ehc = V_(lambda e, wi=wi: e.tensor_copy(
                out=prebuf[:, :, :, 0:3],
                in_=halo[:, wi, :, :, :]),
                waits=[ev_halo[wi], ev_prebuf_free[wi], ev_xnf_done],
                tag=f"hi{it}_{wi}")
            evac = {}
            for oc in range(NCC):
                bank = banks[(oc // 2) % 2]
                bnm = "PA" if (oc // 2) % 2 == 0 else "PB"
                half = (oc % 2) * FW
                evmm = None
                for kc in range(NCC):
                    evmm = T_(lambda e, wn=wname, oc=oc, kc=kc, bank=bank, half=half: e.matmul(
                        bank[:, half:half + FW], W[wn][:, kc, oc * 128:(oc + 1) * 128],
                        xnf[:, kc, :, :].rearrange("p b t -> p (b t)"),
                        start=(kc == 0), stop=(kc == NCC - 1)),
                        waits=[ev_xnf_done, ev_wcast[(wname, NCC - 1)]]
                        + (bank_open(bnm) if kc == 0 else []),
                        tag=f"pj{it}{wname}{oc}_{kc}",
                        inc="tp" if kc == NCC - 1 else None)
                if oc % 2 == 1:
                    eev = S_(lambda e, oc=oc, bank=bank: e.activation(
                        out=prebuf[:, oc - 1:oc + 1, :, 3:3 + TILE],
                        in_=bank[:, 0:2 * FW].rearrange("p (o b t) -> p o b t", o=2, b=B),
                        func=AFT.Copy),
                        waits=[evmm, ehc], tag=f"pe{it}{wname}{oc}")
                    evac[oc - 1] = evac[oc] = eev
                    bank_read(bnm, eev)
            ev_pre[wname] = evac[NCC - 1]
            ev_halo[wi] = V_(lambda e, wi=wi: e.tensor_copy(
                out=halo[:, wi, :, :, :],
                in_=prebuf[:, :, :, TILE:TILE + 3]),
                waits=[ev_pre[wname]], tag=f"ho{it}_{wi}")

            if wname == "v":
                cwb = lambda j: convw[:, 2, :, j].unsqueeze(2).unsqueeze(3).broadcast_to(
                    [128, NCC, B, TILE])
                ecv = V_(lambda e, cw0=cwb(0): e.tensor_tensor(
                    out=convacc[:, :, :, :], in0=prebuf[:, :, :, 0:TILE],
                    in1=cw0, op=OP.mult),
                    waits=[evac[NCC - 1], ev_const, ev_wfree.get("convacc")],
                    tag=f"cv{it}v0")
                for j in range(1, 4):
                    ez2 = V_(lambda e, j=j, cwj=cwb(j): e.tensor_tensor(
                        out=silbuf[:, :, :, :], in0=prebuf[:, :, :, j:j + TILE],
                        in1=cwj, op=OP.mult),
                        waits=[ecv,
                               ev_wfree.get("silbuf") if j == 1 else None],
                        tag=f"cw{it}v{j}")
                    ecv = V_(lambda e: e.tensor_tensor(
                        out=convacc[:, :, :, :], in0=convacc[:, :, :, :],
                        in1=silbuf[:, :, :, :], op=OP.add),
                        waits=[ez2], tag=f"cv{it}v{j}")
                ev_prebuf_free[wi] = ecv
                ev_wfree["convacc"] = None
                evsil = S_(lambda e: e.activation(
                    out=silbuf[:].rearrange("p a b t -> p (a b t)"),
                    in_=convacc[:].rearrange("p a b t -> p (a b t)"), func=AFT.Silu),
                    waits=[ecv, ev_wfree.get("silbuf")], tag=f"sl{it}v")
                evvt = []
                for b in range(B):
                    for ck in range(2):
                        for g2 in range(2):
                            evt = None
                            for q4 in range(4):
                                ic = g2 * 4 + q4
                                evt = T_(lambda e, b=b, ck=ck, ic=ic, q4=q4: e.transpose(
                                    PAb[0:64, q4 * 128:q4 * 128 + 128],
                                    silbuf[:, ic, b, ck * 64:ck * 64 + 64],
                                    ident[:]),
                                    waits=[evsil] + (bank_open("PA") if q4 == 0 else []),
                                    tag=f"vt{it}_{b}{ck}{ic}")
                            evte = S_(lambda e, b=b, ck=ck, g2=g2: e.activation(
                                out=vtm[64 * b:64 * b + 64, ck, g2 * 512:g2 * 512 + 512],
                                in_=PAb[0:64, 0:512], func=AFT.Copy),
                                waits=[evt, ev_wfree.get("vtm")], tag=f"vte{it}_{b}{ck}{g2}")
                            evvt.append(evte)
                            bank_read("PA", evte)
                ev_pre["vtm"] = evvt[-1]
                ev_wfree["silbuf"] = evvt[-1]
                ev_wfree["convacc"] = evsil
                continue

            cwb = lambda j: convw[:, wi, :, j].unsqueeze(2).unsqueeze(3).broadcast_to(
                [128, NCC, B, TILE])
            ecv = V_(lambda e, cw0=cwb(0): e.tensor_tensor(
                out=convacc[:, :, :, :], in0=prebuf[:, :, :, 0:TILE],
                in1=cw0, op=OP.mult),
                waits=[evac[NCC - 1], ev_const, ev_wfree.get("convacc")],
                tag=f"cv{it}{wname}0")
            for j in range(1, 4):
                ez2 = V_(lambda e, j=j, cwj=cwb(j): e.tensor_tensor(
                    out=silbuf[:, :, :, :], in0=prebuf[:, :, :, j:j + TILE],
                    in1=cwj, op=OP.mult),
                    waits=[ecv,
                           ev_wfree.get("silbuf") if j == 1 else None],
                    tag=f"cw{it}{wname}{j}")
                ecv = V_(lambda e: e.tensor_tensor(
                    out=convacc[:, :, :, :], in0=convacc[:, :, :, :],
                    in1=silbuf[:, :, :, :], op=OP.add),
                    waits=[ez2], tag=f"cv{it}{wname}{j}")
            ev_prebuf_free[wi] = ecv
            esil = S_(lambda e: e.activation(
                out=silbuf[:].rearrange("p a b t -> p (a b t)"),
                in_=convacc[:].rearrange("p a b t -> p (a b t)"), func=AFT.Silu),
                waits=[ecv, ev_wfree.get("silbuf")], tag=f"sl{it}{wname}")
            esq = V_(lambda e: e.tensor_tensor(
                out=convacc[:].rearrange("p a b t -> p (a b t)"),
                in0=silbuf[:].rearrange("p a b t -> p (a b t)"),
                in1=silbuf[:].rearrange("p a b t -> p (a b t)"), op=OP.mult),
                waits=[esil], tag=f"sq{it}{wname}")
            ev_wfree["convacc"] = esq
            evmm = None
            for ic in range(NCC):
                evmm = T_(lambda e, ic=ic: e.matmul(
                    PB[0:16, 0:FW], hmask[:, ic, 0:16],
                    convacc[:, ic, :, :].rearrange("p b t -> p (b t)"),
                    start=(ic == 0), stop=(ic == NCC - 1)),
                    waits=[esq] + (bank_open("PB") if ic == 0 else []),
                    tag=f"ss{it}{wname}{ic}",
                    inc="tp" if ic == NCC - 1 else None)
            qst = qscq if wname == "q" else qsck
            el1 = S_(lambda e, qst=qst: e.activation(
                out=qst[:], in_=PB[0:16, 0:FW], func=AFT.Ln,
                bias=epsl2[0:16, 0:1]), waits=[evmm, ev_wfree.get("qsc")],
                tag=f"rl{it}{wname}")
            bank_read("PB", el1)
            if wname == "q":
                el2 = S_(lambda e, qst=qst: e.activation(
                    out=qst[:], in_=qst[:],
                    func=AFT.Exp, scale=-0.5, bias=lnq8[0:16, 0:1]),
                    waits=[el1], tag=f"re{it}{wname}")
            else:
                el2 = S_(lambda e, qst=qst: e.activation(
                    out=qst[:], in_=qst[:],
                    func=AFT.Exp, scale=-0.5), waits=[el1], tag=f"re{it}{wname}")
            dst = qhat if wname == "q" else khat
            evsc = []
            for ic in range(NCC):
                bbank = banks[ic % 2]
                bbnm = "PA" if ic % 2 == 0 else "PB"
                evb = T_(lambda e, ic=ic, qst=qst, bbank=bbank: e.matmul(
                    bbank[:, 0:FW], hmaskT[:, ic, :], qst[:],
                    start=True, stop=True),
                    waits=[el2] + bank_open(bbnm),
                    tag=f"bc{it}{wname}{ic}")
                evsc.append(V_(lambda e, ic=ic, dst=dst, bbank=bbank: e.tensor_tensor(
                    out=dst[:, ic, :, :].rearrange("p b t -> p (b t)"),
                    in0=silbuf[:, ic, :, :].rearrange("p b t -> p (b t)"),
                    in1=bbank[:, 0:FW], op=OP.mult),
                    waits=[evb, ev_wfree.get("qhat") if wname == "q"
                           else ev_wfree.get("khat")],
                    tag=f"sc{it}{wname}{ic}"))
                bank_read(bbnm, evsc[-1])
            ev_pre[wname + "hat"] = evsc[-1]
            ev_sc_done[wname] = [evsc[-1], evb]
            ev_wfree["silbuf"] = evsc[-1]


        for wname, dstb in (("g", gatebuf), ("f", ffng)):
            evacs = []
            for oc in range(NCC):
                bank = banks[(oc // 2) % 2]
                bnm = "PA" if (oc // 2) % 2 == 0 else "PB"
                half = (oc % 2) * FW
                evmm = None
                for kc in range(NCC):
                    evmm = T_(lambda e, wn=wname, oc=oc, kc=kc, bank=bank, half=half: e.matmul(
                        bank[:, half:half + FW], W[wn][:, kc, oc * 128:(oc + 1) * 128],
                        xnf[:, kc, :, :].rearrange("p b t -> p (b t)"),
                        start=(kc == 0), stop=(kc == NCC - 1)),
                        waits=[ev_xnf_done, ev_wcast[(wname, NCC - 1)]]
                        + (bank_open(bnm) if kc == 0 else []),
                        tag=f"pj{it}{wname}{oc}_{kc}",
                        inc="tp" if kc == NCC - 1 else None)
                if oc % 2 == 1:
                    eev = S_(lambda e, oc=oc, bank=bank, dstb=dstb: e.activation(
                        out=dstb[:, oc - 1:oc + 1, :, :],
                        in_=bank[:, 0:2 * FW].rearrange("p (o b t) -> p o b t", o=2, b=B),
                        func=AFT.Silu),
                        waits=[evmm, ev_wfree.get("gate" if dstb is gatebuf else "ffn")],
                        tag=f"sv{it}{wname}{oc}")
                    evacs.append(eev)
                    bank_read(bnm, eev)
            if wname == "g":
                ev_gate_raw = evacs[-1]
            else:
                ev_ffn_done = evacs[-1]
                ev_wfree["xnf"] = evmm
        eg1 = V_(lambda e: e.tensor_copy(
            out=convacc[:].rearrange("p a b t -> p (a b t)"),
            in_=gatebuf[:].rearrange("p a b t -> p (a b t)")),
            waits=[ev_gate_raw, ev_wfree.get("convacc")], tag=f"gn1{it}")
        ev_gate_done = V_(lambda e: e.tensor_scalar(
            out=gatebuf[:].rearrange("p a b t -> p (a b t)"),
            in0=convacc[:].rearrange("p a b t -> p (a b t)"),
            scalar1=normw2[:, 0:1], scalar2=None, op0=OP.mult),
            waits=[eg1], tag=f"gn2{it}")
        ev_wfree["convacc"] = ev_gate_done

        P.stream = ("S", it)

        evba = {}
        evmm = None
        for ck in range(2):
            for b in range(B):
                for kc in range(NCC):
                    evmm = T_(lambda e, ck=ck, b=b, kc=kc: e.matmul(
                        PW2[64 * b:64 * b + 64, 256 + 32 * ck:288 + 32 * ck],
                        xnf[:, kc, b, ck * 64:ck * 64 + 64],
                        Wba[:, kc, :], start=(kc == 0), stop=(kc == NCC - 1),
                        tile_position=(0, 64 * b)),
                        waits=[ev_xnf_done, ev_wcast[("ba", NCC - 1)]]
                        + (bank_open("PW2") if (ck == 0 and b == 0 and kc == 0) else []),
                        tag=f"ba{it}_{ck}_{b}_{kc}",
                        inc="tp" if kc == NCC - 1 else None)
        bav = PW2[:, 256:320].rearrange("p (c w) -> p c w", c=2)
        eb = S_(lambda e: e.activation(
            out=beta_tm[:, :, :], in_=bav[:, :, 0:NH], func=AFT.Sigmoid),
            waits=[evmm, ev_wfree.get("tmsmalls")], tag=f"sig{it}")
        bank_read("PW2", eb)
        elb = S_(lambda e: e.activation(
            out=lnb_tm[:, :, :], in_=beta_tm[:, :, :], func=AFT.Ln),
            waits=[eb], tag=f"lnb{it}")
        evba[("lnb", 0)] = evba[("lnb", 1)] = elb
        ez = V_(lambda e: e.tensor_tensor(
            out=g_tm[:, :, :], in0=bav[:, :, NH:2 * NH],
            in1=dtrep[:].unsqueeze(1).broadcast_to([128, 2, NH]),
            op=OP.add), waits=[evmm, ev_const], tag=f"z{it}")
        bank_read("PW2", ez)
        ea = V_(lambda e: e.scalar_tensor_tensor(
            out=tmsc1[:, :, :], in0=g_tm[:, :, :], scalar=-1.0,
            op0=OP.mult, in1=g_tm[:, :, :], op1=OP.max),
            waits=[ez], tag=f"abs{it}")
        ee = S_(lambda e: e.activation(
            out=tmsc1[:, :, :], in_=tmsc1[:, :, :], func=AFT.Exp,
            scale=-1.0), waits=[ea, elb], tag=f"spe{it}")
        el = S_(lambda e: e.activation(
            out=tmsc1[:, :, :], in_=tmsc1[:, :, :], func=AFT.Ln,
            bias=1.0), waits=[ee], tag=f"spl{it}")
        er = V_(lambda e: e.tensor_scalar(
            out=g_tm[:, :, :], in0=g_tm[:, :, :], scalar1=0.0,
            scalar2=None, op0=OP.max), waits=[el], tag=f"spr{it}")
        es2 = V_(lambda e: e.tensor_tensor(
            out=g_tm[:, :, :], in0=g_tm[:, :, :], in1=tmsc1[:, :, :],
            op=OP.add), waits=[er], tag=f"spa{it}")
        eg = V_(lambda e: e.tensor_tensor(
            out=g_tm[:, :, :], in0=g_tm[:, :, :],
            in1=negexpA[:].unsqueeze(1).broadcast_to([128, 2, NH]),
            op=OP.mult), waits=[es2, ev_negA], tag=f"gg{it}")
        evg = []
        for ck in range(2):
            for b in range(B):
                evg.append(T_(lambda e, ck=ck, b=b: e.matmul(
                    PW2[64 * b:64 * b + 64, 320 + 16 * ck:336 + 16 * ck],
                    lones[64 * b:64 * b + 64, 64 * b:64 * b + 64],
                    g_tm[64 * b:64 * b + 64, ck, :], start=True, stop=True,
                    tile_position=(64 * b, 64 * b)),
                    waits=[eg] + (bank_open("PW2") if (ck == 0 and b == 0) else []),
                    tag=f"G{it}_{ck}_{b}"))
        ecp = V_(lambda e: e.tensor_copy(
            out=G_tm[:, :, :],
            in_=PW2[:, 320:352].rearrange("p (c h) -> p c h", c=2)),
            waits=[evg[-1]], tag=f"Gc{it}")
        bank_read("PW2", ecp)
        eex = S_(lambda e: e.activation(
            out=expG_tm[:, :, :], in_=G_tm[:, :, :], func=AFT.Exp),
            waits=[ecp], tag=f"Ge{it}")
        evba[("expG", 0)] = evba[("expG", 1)] = eex
        for ck in range(2):
            gfc = 352 + 160 * ck
            eF = T_(lambda e, ck=ck, gfc=gfc: e.matmul(
                PW2[0:NH, gfc:gfc + 128],
                g_tm[:, ck, :],
                lones[:, 0:128],
                start=True, stop=True),
                waits=[eg] + (bank_open("PW2") if ck == 0 else []),
                tag=f"Gf{it}_{ck}")
            egfc = V_(lambda e, ck=ck, gfc=gfc: e.tensor_copy(
                out=Gfm[:, :, ck, :],
                in_=PW2[0:NH, gfc:gfc + 128].rearrange("p (b t) -> p b t", b=B)),
                waits=[eF], tag=f"Gfc{it}_{ck}")
            bank_read("PW2", egfc)
            evba[("gfm", ck)] = egfc
            evba[("gfmfree", ck + 1)] = egfc
        Gl = [Gpos[0:16].rearrange("p ck b r -> p b ck r"), Gl1s[:],
              Gpos[32:48].rearrange("p ck b r -> p b ck r")]
        Gn = [Gneg[0:16].rearrange("p ck b r -> p b ck r"), Gn1s[:],
              Gneg[32:48].rearrange("p ck b r -> p b ck r")]
        gsrc = Gfm[:].rearrange("p b c t -> p (b c t)")
        gl = ev_wfree.get("Gs")
        l1 = V_(lambda e: e.tensor_copy(
            out=Gl[0], in_=Gfm[:]),
            waits=[evba[("gfm", 1)], gl], tag=f"lv1{it}")
        r1 = V_(lambda e: e.tensor_tensor(
            out=Glv1[:].rearrange("p b (ck r) -> p b ck r", ck=2), in0=Gfm[:],
            in1=Gl[0], op=OP.subtract),
            waits=[l1], tag=f"rs1{it}")
        l2 = V_(lambda e: e.tensor_copy(
            out=Gl[1].rearrange("p b t -> p (b t)"),
            in_=Glv1[:].rearrange("p b t -> p (b t)")), waits=[r1], tag=f"lv2{it}")
        r2 = V_(lambda e: e.tensor_tensor(
            out=Glv2[:].rearrange("p b t -> p (b t)"),
            in0=Glv1[:].rearrange("p b t -> p (b t)"),
            in1=Gl[1].rearrange("p b t -> p (b t)"), op=OP.subtract),
            waits=[l2, ev_wfree.get("Glv2")], tag=f"rs2{it}")
        l3 = V_(lambda e: e.tensor_copy(
            out=Gl[2],
            in_=Glv2[:].rearrange("p b (ck r) -> p b ck r", ck=2)), waits=[r2], tag=f"lv3{it}")
        evng = []
        for i in range(3):
            evng.append(V_(lambda e, i=i: e.tensor_scalar(
                out=(Gn[i] if i != 1 else Gn[i].rearrange("p b (ck r) -> p b ck r", ck=2)),
                in0=(Gl[i] if i != 1 else Gl[i].rearrange("p b (ck r) -> p b ck r", ck=2)),
                scalar1=-1.0, scalar2=None, op0=OP.mult),
                waits=[l3], tag=f"ngl{it}_{i}"))
        ng = evng[-1]
        ef1 = S_(lambda e: e.activation(
            out=expGfm[:].rearrange("p b t -> p (b t)"), in_=gsrc, func=AFT.Exp),
            waits=[evba[("gfm", 1)], ev_wfree.get("expGfm")], tag=f"egf{it}")
        ec1 = V_(lambda e: e.tensor_tensor(
            out=Glv1[:].rearrange("p b (c t) -> p b c t", c=2),
            in0=Gfm[:, :, :, 63:64].broadcast_to([NH, B, 2, 64]),
            in1=Gfm[:], op=OP.subtract), waits=[evba[("gfm", 1)], l3], tag=f"gcf{it}")
        ec2 = S_(lambda e: e.activation(
            out=eGceFM[:].rearrange("p b t -> p (b t)"),
            in_=Glv1[:].rearrange("p b t -> p (b t)"), func=AFT.Exp),
            waits=[ec1, ev_wfree.get("eGceFM")], tag=f"gce2{it}")
        evlb = []
        for ck in range(2):
            evlb.append(T_(lambda e, ck=ck: e.transpose(
                PW2[0:NH, ck * 128:ck * 128 + 128],
                lnb_tm[:, ck, :],
                identf[:]),
                waits=[evba[("lnb", 0)], evba[("lnb", 1)],
                       ev_wfree.get("pw2")] + (bank_open("PW2") if ck == 0 else []),
                tag=f"lbt{it}_{ck}"))
        elnb = V_(lambda e: e.tensor_copy(
            out=lnbfm[:].rearrange("p b (c r) -> p b c r", c=2),
            in_=PW2[0:NH, 0:256].rearrange("p (c b r) -> p b c r", c=2, b=B)),
            waits=[evlb[-1], ec2], tag=f"lbc{it}")
        bank_read("PW2", elnb)
        lb1 = V_(lambda e: e.tensor_copy(
            out=Lb[0][:].rearrange("p b t -> p (b t)"),
            in_=lnbfm[:].rearrange("p b t -> p (b t)")), waits=[elnb], tag=f"lb1{it}")
        lb2 = V_(lambda e: e.tensor_tensor(
            out=Lb[1][:].rearrange("p b t -> p (b t)"),
            in0=lnbfm[:].rearrange("p b t -> p (b t)"),
            in1=Lb[0][:].rearrange("p b t -> p (b t)"), op=OP.subtract),
            waits=[lb1], tag=f"lb2{it}")
        ev_wfree["Glv2"] = elnb
        ev_Gs = lb2
        ev_asm = []
        for bb_ in range(B):
            ev_asm.append(dma("gpsimd", "gdf", Gpos[16:32, :, bb_, :],
                              Gl1s[:, bb_, :], waits=[l3, ev_wfree.get("rhs")],
                              tag=f"asmP{it}{bb_}"))
            ev_asm.append(dma("gpsimd", "gdf", Gneg[16:32, :, bb_, :],
                              Gn1s[:, bb_, :], waits=[ng, ev_wfree.get("rhs")],
                              tag=f"asmN{it}{bb_}"))
            for i in range(2):
                ev_asm.append(dma("gpsimd", "gdf",
                                  Gneg[54 + 16 * i:70 + 16 * i, :, bb_, :],
                                  Lb[i][:, bb_, :], waits=[lb2, ev_wfree.get("rhs")],
                                  tag=f"asmL{it}_{i}{bb_}"))
        ev_flats = list(ev_asm)
        for ck in range(2):
            for b in range(B):
                base = 48 if b == 0 else 53
                ev_flats.append(dma(
                    "gpsimd", "gdf", rhsA[base:base + 3, ck, :],
                    Gneg[0:48, ck, b, :],
                    waits=[ev_asm[-1]], tag=f"fA{it}_{ck}_{b}"))
                ev_flats.append(dma(
                    "gpsimd", "gdf", rhsA[base + 3:base + 5, ck, :],
                    Gneg[54:86, ck, b, :],
                    waits=[ev_asm[-1]], tag=f"fAl{it}_{ck}_{b}"))
                b3 = 48 if b == 0 else 51
                ev_flats.append(dma(
                    "gpsimd", "gdf", rhs3[b3:b3 + 3, ck, :],
                    Gpos[0:48, ck, b, :],
                    waits=[ev_asm[-1]], tag=f"f3{it}_{ck}_{b}"))
        ev_flats_done = ev_flats[-1]

        evq = []
        for ic in range(NCC):
            half = (ic % 2) * 512
            evb = T_(lambda e, ic=ic, half=half: e.matmul(
                PW2[:, half:half + FW], hmaskT[:, ic, :],
                expGfm[:].rearrange("p b t -> p (b t)"),
                start=True, stop=True),
                waits=[ef1, ev_sc_done["q"]] + bank_open("PW2"),
                tag=f"bq{it}{ic}")
            evq.append(V_(lambda e, ic=ic, half=half: e.tensor_tensor(
                out=qtil[:, ic, :, :].rearrange("p b t -> p (b t)"),
                in0=qhat[:, ic, :, :].rearrange("p b t -> p (b t)"),
                in1=PW2[:, half:half + FW], op=OP.mult),
                waits=[evb, ev_wfree.get("qtil")], tag=f"qt{it}{ic}"))
            bank_read("PW2", evq[-1])
        ev_pre["qtil"] = evq[-1]
        k2prod = dvb[:].rearrange("p c w -> p (c w)").rearrange(
            "p (a b t) -> p a b t", a=NCC, b=B)
        evk = []
        for ic in range(NCC):
            half = (ic % 2) * 512
            evb = T_(lambda e, ic=ic, half=half: e.matmul(
                PW2[:, half:half + FW], hmaskT[:, ic, :],
                eGceFM[:].rearrange("p b t -> p (b t)"),
                start=True, stop=True),
                waits=[ec2, ev_sc_done["k"]] + bank_open("PW2"),
                tag=f"bk{it}{ic}")
            evk.append(V_(lambda e, ic=ic, half=half: e.tensor_tensor(
                out=k2prod[:, ic, :, :],
                in0=khat[:, ic, :, :],
                in1=PW2[:, half:half + FW].rearrange("p (b t) -> p b t", b=B),
                op=OP.mult),
                waits=[evb, ev_wfree.get("k2prod")], tag=f"k2{it}{ic}"))
            bank_read("PW2", evk[-1])
        evkt = []
        for b in range(B):
            for ck in range(2):
                for g2 in range(2):
                    evt = None
                    for q4 in range(4):
                        ic = g2 * 4 + q4
                        evt = T_(lambda e, b=b, ck=ck, ic=ic, q4=q4: e.transpose(
                            PW2b[0:64, q4 * 128:q4 * 128 + 128],
                            k2prod[:, ic, b, ck * 64:ck * 64 + 64],
                            ident[:]),
                            waits=[evk[-1]] + (bank_open("PW2") if q4 == 0 else []),
                            tag=f"k2t{it}_{b}{ck}{ic}")
                    ekt = S_(lambda e, b=b, ck=ck, g2=g2: e.activation(
                        out=k2tm[64 * b:64 * b + 64, ck, g2 * 512:g2 * 512 + 512],
                        in_=PW2b[0:64, 0:512], func=AFT.Copy),
                        waits=[evt, ev_wfree.get("k2tm")], tag=f"k2e{it}_{b}{ck}{g2}")
                    evkt.append(ekt)
                    bank_read("PW2", ekt)
        ev_pre["k2tm"] = evkt[-1]

        evD = {}
        ev_po_evac = {}
        prev_serial = []
        for ck in range(2):
            wA = [ev_Gs, ev_flats_done, ev_const] + bank_open("PW1")
            for hw_ in range(2):
                mA1 = T_(lambda e, ck=ck, hw_=hw_: e.matmul(
                    PW1[:, hw_ * 512:hw_ * 512 + 512],
                    Gpos[:, ck, :, :],
                    rhsA[:, ck, hw_ * 512:hw_ * 512 + 512], start=True, stop=False),
                    waits=wA, tag=f"dgA{it}_{ck}_{hw_}")
                mA2 = T_(lambda e, hw_=hw_: e.matmul(
                    PW1[:, hw_ * 512:hw_ * 512 + 512], ident[:],
                    maskA[:, 0:512],
                    start=False, stop=True), waits=[mA1], tag=f"dgAm{it}_{ck}_{hw_}")
            x1 = S_(lambda e: e.activation(out=Dexp1[:], in_=PW1[:, 0:WIDE],
                                           func=AFT.Exp),
                    waits=[mA2, evD.get(("d1f", ck))] + prev_serial,
                    tag=f"dxA{it}_{ck}")
            bank_read("PW1", x1)
            w2 = [x1] + bank_open("PW1")
            for hw_ in range(2):
                m21 = T_(lambda e, ck=ck, hw_=hw_: e.matmul(
                    PW1[:, hw_ * 512:hw_ * 512 + 512],
                    Gneg[0:54, ck, :, :],
                    rhs3[0:54, ck, hw_ * 512:hw_ * 512 + 512], start=True, stop=False),
                    waits=w2, tag=f"dg2{it}_{ck}_{hw_}")
                m22 = T_(lambda e, hw_=hw_: e.matmul(
                    PW1[:, hw_ * 512:hw_ * 512 + 512], ident[:],
                    mask2[:, 0:512],
                    start=False, stop=True), waits=[m21], tag=f"dg2m{it}_{ck}_{hw_}")
            x2 = S_(lambda e: e.activation(out=Dexp2[:], in_=PW1[:, 0:WIDE],
                                           func=AFT.Exp),
                    waits=[m22, evD.get(("d2f", ck))] + prev_serial,
                    tag=f"dx2{it}_{ck}")
            bank_read("PW1", x2)
            w3 = [x2] + bank_open("PW1")
            for hw_ in range(2):
                m31 = T_(lambda e, ck=ck, hw_=hw_: e.matmul(
                    PW1[:, hw_ * 512:hw_ * 512 + 512],
                    Gneg[:, ck, :, :],
                    rhs3[:, ck, hw_ * 512:hw_ * 512 + 512], start=True, stop=False),
                    waits=w3, tag=f"dg3{it}_{ck}_{hw_}")
                m32 = T_(lambda e, hw_=hw_: e.matmul(
                    PW1[:, hw_ * 512:hw_ * 512 + 512], ident[:],
                    mask3[:, 0:512],
                    start=False, stop=True), waits=[m31], tag=f"dg3m{it}_{ck}_{hw_}")
            x3 = S_(lambda e: e.activation(out=Dexp3[:], in_=PW1[:, 0:WIDE],
                                           func=AFT.Exp),
                    waits=[m32, evD.get(("d3f", ck))] + prev_serial,
                    tag=f"dx3{it}_{ck}")
            bank_read("PW1", x3)
            evkk = []
            for s in range(NH):
                par, j = s // 8, s % 8
                for b in range(B):
                    evkk.append(T_(lambda e, ck=ck, s=s, par=par, j=j, b=b: e.matmul(
                        PW2[64 * b:64 * b + 64, s * 64:s * 64 + 64],
                        khat[64 * par:64 * par + 64, j, b, ck * 64:ck * 64 + 64],
                        khat[64 * par:64 * par + 64, j, b, ck * 64:ck * 64 + 64],
                        start=True, stop=True, tile_position=(64 * par, 64 * b)),
                        waits=[ev_pre["khat"], ev_pre["vtm"]]
                        + (bank_open("PW2") if (s == 0 and b == 0) else []),
                        tag=f"kk{it}_{ck}_{s}_{b}"))
            eB = []
            eBT = []
            for hh in range(2):
                eB.append(V_(lambda e, hh=hh: e.scalar_tensor_tensor(
                    out=Bw[:, hh * 512:hh * 512 + 512],
                    in0=PW2[:, hh * 512:hh * 512 + 512], scalar=-1.0,
                    in1=Dexp1[:, hh * 512:hh * 512 + 512],
                    op0=OP.mult, op1=OP.mult),
                    waits=[evkk[15 + 16 * hh], x1, ev_wfree.get("Bw")],
                    tag=f"B{it}_{ck}_{hh}"))
                bank_read("PW2", eB[-1])
            for hh in range(2):
                eBT.append(V_(lambda e, hh=hh: e.scalar_tensor_tensor(
                    out=BTw[:, hh * 512:hh * 512 + 512],
                    in0=PW2[:, hh * 512:hh * 512 + 512], scalar=-1.0,
                    in1=Dexp3[:, hh * 512:hh * 512 + 512],
                    op0=OP.mult, op1=OP.mult),
                    waits=[evkk[15 + 16 * hh], x3, ev_wfree.get("BTw")],
                    tag=f"BT{it}_{ck}_{hh}"))
                bank_read("PW2", eBT[-1])
            evD[("d1f", ck + 1)] = eB
            evD[("d3f", ck + 1)] = eBT
            evqk = []
            for s in range(NH):
                par, j = s // 8, s % 8
                for b in range(B):
                    evqk.append(T_(lambda e, ck=ck, s=s, par=par, j=j, b=b: e.matmul(
                        PW1[64 * b:64 * b + 64, s * 64:s * 64 + 64],
                        khat[64 * par:64 * par + 64, j, b, ck * 64:ck * 64 + 64],
                        qhat[64 * par:64 * par + 64, j, b, ck * 64:ck * 64 + 64],
                        start=True, stop=True, tile_position=(64 * par, 64 * b)),
                        waits=[ev_pre["qhat"], x3]
                        + (bank_open("PW1") if (s == 0 and b == 0) else []),
                        tag=f"qk{it}_{ck}_{s}_{b}"))
            eM = []
            for hh in range(2):
                eM.append(V_(lambda e, hh=hh: e.tensor_tensor(
                    out=Mmw[:, hh * 512:hh * 512 + 512],
                    in0=PW1[:, hh * 512:hh * 512 + 512],
                    in1=Dexp2[:, hh * 512:hh * 512 + 512], op=OP.mult),
                    waits=[evqk[15 + 16 * hh], x2, ev_wfree.get("Mmw")],
                    tag=f"Mm{it}_{ck}_{hh}"))
                bank_read("PW1", eM[-1])
            evD[("d2f", ck + 1)] = eM
            evD[("B", ck)] = eB
            evD[("BT", ck)] = eBT
            evD[("Mm", ck)] = eM
            evD[("pw1f", ck + 1)] = eM
            evD[("pw2f", ck + 1)] = eBT

            evp = []
            for s in range(NH):
                par, j = s // 8, s % 8
                for b in range(B):
                    evp.append(T_(lambda e, s=s, par=par, j=j, b=b, ck=ck: e.matmul(
                        PW1[64 * b:64 * b + 64, s * 64:s * 64 + 64],
                        khat[64 * par:64 * par + 64, j, b, ck * 64:ck * 64 + 64],
                        Sb16[64 * par:64 * par + 64, b, j, :],
                        start=True, stop=True, tile_position=(64 * par, 64 * b)),
                        waits=[ev_state_Sb, eM[0 if s < 8 else 1], ev_pre["khat"]]
                        + (bank_open("PW1") if (s == 0 and b == 0) else []),
                        tag=f"pr{it}_{ck}_{s}_{b}"))
            ev_last_pr = evp[-1]
            eu1 = V_(lambda e, ck=ck: e.tensor_tensor(
                out=tscr[:].rearrange("p (h v) -> p h v", h=NH),
                in0=PW1[:, 0:WIDE].rearrange("p (h v) -> p h v", h=NH),
                in1=expG_tm[:, ck, :].unsqueeze(2).broadcast_to([128, NH, 64]),
                op=OP.mult), waits=[evp[-1], evba[("expG", ck)],
                                    ev_wfree.get("oxs")], tag=f"u1{it}_{ck}")
            bank_read("PW1", eu1)
            eu2 = V_(lambda e, ck=ck: e.tensor_tensor(
                out=uh[:, ck, :].rearrange("p (par j d) -> p par j d", par=2, j=8),
                in0=vtm[:, ck, :].rearrange("p (j par d) -> p par j d", j=8, par=2),
                in1=tscr[:].rearrange("p (par j d) -> p par j d", par=2, j=8),
                op=OP.subtract),
                waits=[eu1, ev_pre["vtm"], ev_uh_use], tag=f"u2{it}_{ck}")
            ev_uh_w = eu2

            Msrc = [Bw[:], McT[:], Mx[:], McT[:], Mx[:]]
            Nsrc = [BTw[:], Ncr[:], Nx[:], Ncr[:], Nx[:], Ncr[:]]
            evMw = {0: evD[("B", ck)]}
            evNw = {0: evD[("BT", ck)]}
            ycur, ev_y = uh[:, ck, :], [eu2, eu2]
            ev_pw1_reader = eu1
            ev_pw2_reader = eBT
            for l in range(6):
                evs = []
                for half in range(2):
                    evh = []
                    for b in range(B):
                        for h in range(half * 8, half * 8 + 8):
                            evh.append(T_(lambda e, b=b, h=h, l=l, N=Nsrc[l], yc=ycur: e.matmul(
                                PW1[64 * b:64 * b + 64, h * 64:h * 64 + 64],
                                N[64 * b:64 * b + 64, h * 64:h * 64 + 64],
                                yc[64 * b:64 * b + 64, h * 64:h * 64 + 64],
                                start=True, stop=True, tile_position=(64 * b, 64 * b)),
                                waits=[evNw[l][half], ev_y[half]]
                                + (bank_open("PW1") if (half == 0 and b == 0 and h == 0) else []),
                                tag=f"ap{it}_{ck}_{l}_{b}_{h}"))
                    evs.append(evh)
                ynext = (yv if l % 2 == 0 else yv2)
                ev_yh = []
                for half in range(2):
                    ev_yh.append(V_(lambda e, yn=ynext, yc=ycur, half=half: e.tensor_tensor(
                        out=yn[:, half * 512:half * 512 + 512],
                        in0=PW1[:, half * 512:half * 512 + 512],
                        in1=yc[:, half * 512:half * 512 + 512], op=OP.add),
                        waits=[evs[half][-1], ev_wfree.get("yv")],
                        tag=f"ye{it}_{ck}_{l}_{half}"))
                    bank_read("PW1", ev_yh[-1])
                ycur, ev_y = ynext[:], ev_yh
                if l < 5:
                    evsqN = []
                    for b in range(B):
                        for h in range(NH):
                            evsqN.append(T_(lambda e, b=b, h=h, M=Msrc[l], N=Nsrc[l]: e.matmul(
                                PW2[64 * b:64 * b + 64, h * 64:h * 64 + 64],
                                M[64 * b:64 * b + 64, h * 64:h * 64 + 64],
                                N[64 * b:64 * b + 64, h * 64:h * 64 + 64],
                                start=True, stop=True, tile_position=(64 * b, 64 * b)),
                                waits=[evMw[l][h // 8], evNw[l][h // 8]]
                                + (bank_open("PW2") if (b == 0 and h == 0) else []),
                                tag=f"sqN{it}_{ck}_{l}_{b}_{h}"))
                    eNev = []
                    for hh in range(2):
                        eNev.append(S_(lambda e, dstN=Nsrc[l + 1], hh=hh: e.activation(
                            out=dstN[:, hh * 512:hh * 512 + 512],
                            in_=PW2[:, hh * 512:hh * 512 + 512], func=AFT.Copy),
                            waits=[evsqN[23 if hh == 0 else 31], evNw.get(l - 1)],
                            tag=f"Ne{it}_{ck}_{l}_{hh}"))
                        bank_read("PW2", eNev[-1])
                    evNw[l + 1] = eNev
                    if l < 4:
                        evsqM = []
                        for b in range(B):
                            for h in range(NH):
                                evsqM.append(T_(lambda e, b=b, h=h, M=Msrc[l], N=Nsrc[l]: e.matmul(
                                    PO[64 * b:64 * b + 64, h * 64:h * 64 + 64],
                                    N[64 * b:64 * b + 64, h * 64:h * 64 + 64],
                                    M[64 * b:64 * b + 64, h * 64:h * 64 + 64],
                                    start=True, stop=True, tile_position=(64 * b, 64 * b)),
                                    waits=[evNw[l][h // 8], evMw[l][h // 8]]
                                    + (bank_open("PO") if (b == 0 and h == 0) else []),
                                    tag=f"sqM{it}_{ck}_{l}_{b}_{h}"))
                        eMev = []
                        for hh in range(2):
                            eMev.append(S_(lambda e, dstM=Msrc[l + 1], hh=hh: e.activation(
                                out=dstM[:, hh * 512:hh * 512 + 512],
                                in_=PO[:, hh * 512:hh * 512 + 512], func=AFT.Copy),
                                waits=[evsqM[23 if hh == 0 else 31], evMw.get(l - 1)],
                                tag=f"Me{it}_{ck}_{l}_{hh}"))
                            bank_read("PO", eMev[-1])
                        evMw[l + 1] = eMev
            ev_wfree["yv"] = None
            edv = V_(lambda e, ck=ck, yc=ycur: e.tensor_tensor(
                out=dvb[:, ck, :].rearrange("p (h v) -> p h v", h=NH),
                in0=yc.rearrange("p (h v) -> p h v", h=NH),
                in1=beta_tm[:, ck, :].unsqueeze(2).broadcast_to([128, NH, 64]),
                op=OP.mult), waits=[ev_y[0], ev_y[1], ev_pre["k2tm"],
                                    ev_wfree.get("dvb")],
                tag=f"dv{it}_{ck}")

            evo = []
            for s in range(NH):
                par, j = s // 8, s % 8
                for b in range(B):
                    evo.append(T_(lambda e, s=s, par=par, j=j, b=b, ck=ck: e.matmul(
                        PO[64 * par:64 * par + 64, b * 512 + j * 64:b * 512 + j * 64 + 64],
                        dvb[64 * b:64 * b + 64, ck, s * 64:s * 64 + 64],
                        Mmw[64 * b:64 * b + 64, s * 64:s * 64 + 64],
                        start=True, stop=True, tile_position=(64 * b, 64 * par)),
                        waits=[edv, eM[0 if s < 8 else 1]]
                        + (bank_open("PO") if (s == 0 and b == 0) else []),
                        tag=f"oi{it}_{ck}_{s}_{b}"))
            evox = []
            for s in range(NH):
                par, j = s // 8, s % 8
                for b in range(B):
                    evox.append(T_(lambda e, par=par, j=j, b=b, ck=ck: e.matmul(
                        PW1[64 * par:64 * par + 64, j * 128 + b * 64:j * 128 + b * 64 + 64],
                        Sb16[64 * par:64 * par + 64, b, j, :],
                        qtil[64 * par:64 * par + 64, j, b, ck * 64:ck * 64 + 64],
                        start=True, stop=True, tile_position=(64 * par, 64 * par)),
                        waits=[ev_state_Sb, ev_pre["qtil"]]
                        + (bank_open("PW1") if (s == 0 and b == 0) else []),
                        tag=f"ox{it}_{ck}_{s}_{b}"))
            oxs = oxsv
            ev_oxe = S_(lambda e, oxs=oxs: e.activation(
                out=oxs, in_=PW1[:, 0:1024], func=AFT.Copy),
                waits=[evox[-1], eu2, ev_wfree.get("oxs")],
                tag=f"oxe{it}_{ck}")
            bank_read("PW1", ev_oxe)
            ev_po_evac[ck + 1] = V_(lambda e, ck=ck, oxs=oxs: e.tensor_tensor(
                out=qhat[:, :, :, ck * 64:ck * 64 + 64],
                in0=PO[:, 0:1024].rearrange("p (b j t) -> p j b t", b=B, j=NCC),
                in1=oxs.rearrange("p (j b t) -> p j b t", j=NCC, b=B),
                op=OP.add),
                waits=[evo[-1], ev_oxe, ev_wfree.get("obuf")], tag=f"oe{it}_{ck}")
            bank_read("PO", ev_po_evac[ck + 1])
            ev_wfree["oxs"] = ev_po_evac[ck + 1]

            egr = []
            for b in range(B):
                egr.append(dma("gpsimd", "gdr", rowb[0:1, b, :],
                               G_tm[64 * b + 63:64 * b + 64, ck, :],
                               waits=[ecp, ev_wfree.get("rowbfree")],
                               tag=f"rb{it}_{ck}_{b}"))
            eeg = S_(lambda e: e.activation(
                out=rowbb[0:1].rearrange("p par (b j) -> p par b j", b=B),
                in_=rowb[0:1, :, :].rearrange("p b (par j) -> p par b j", par=2),
                func=AFT.Exp),
                waits=[egr[-1], ev_wfree.get("dwfree")], tag=f"egc{it}_{ck}")
            ev_wfree["rowbfree"] = eeg
            evba[("eGCp", ck)] = eeg
            edw = None
            for par in range(2):
                edw = T_(lambda e, par=par: e.matmul(
                    PW1[64 * par:64 * par + 64, 0:B * NCC],
                    onesrow[0:1, 0:64],
                    rowbb[0:1, par, :],
                    start=True, stop=True, tile_position=(0, 64 * par)),
                    waits=[evba[("eGCp", ck)]]
                    + (bank_open("PW1") if par == 0 else []),
                    tag=f"dw{it}_{ck}_{par}")
            ev_wfree["dwfree"] = edw
            es1 = V_(lambda e, ck=ck: e.tensor_tensor(
                out=S32b[:].rearrange("p (a b) -> p a b", a=B * NCC),
                in0=S32[:].rearrange("p a b c -> p (a b) c"),
                in1=PW1[:, 0:B * NCC].unsqueeze(2).broadcast_to([128, B * NCC, DH]),
                op=OP.mult), waits=[edw, ev_state_S, ev_po_evac[ck + 1],
                                    ev_wfree.get("oxs")], tag=f"sd{it}_{ck}")
            bank_read("PW1", es1)
            evst = []
            for s in range(NH):
                par, j = s // 8, s % 8
                hn = 2 * j + par
                for b in range(B):
                    evst.append(T_(lambda e, s=s, par=par, j=j, hn=hn, b=b, ck=ck: e.matmul(
                        PW2[64 * par:64 * par + 64, b * 512 + j * 64:b * 512 + j * 64 + 64],
                        k2tm[64 * b:64 * b + 64, ck, hn * 64:hn * 64 + 64],
                        dvb[64 * b:64 * b + 64, ck, s * 64:s * 64 + 64],
                        start=True, stop=True, tile_position=(64 * b, 64 * par)),
                        waits=[edv, ev_pre["k2tm"]]
                        + (bank_open("PW2") if (s == 0 and b == 0) else []),
                        tag=f"st{it}_{ck}_{s}_{b}"))
            es2b = V_(lambda e: e.tensor_tensor(
                out=S32[:].rearrange("p a b c -> p (a b c)"),
                in0=S32b[:, 0:1024],
                in1=PW2[:, 0:1024], op=OP.add),
                waits=[es1, evst[-1]], tag=f"sa{it}_{ck}")
            bank_read("PW2", es2b)
            ev_state_S = es2b
            ev_state_Sb = S_(lambda e: e.activation(
                out=Sb16[:].rearrange("p a b c -> p (a b c)"),
                in_=S32[:].rearrange("p a b c -> p (a b c)"), func=AFT.Copy),
                waits=[es2b], tag=f"sb{it}_{ck}")
            ev_pw2_reader = es2b
            evD[("pw2f", ck + 1)] = es2b
            evD[("pw1f", ck + 1)] = ev_pw1_reader
            prev_serial = [ev_y[0], ev_y[1], edv]
        ev_wfree["dvb"] = None
        ev_wfree["Bw"] = ev_y
        ev_wfree["BTw"] = list(ev_y)
        ev_wfree["Mmw"] = ev_po_evac[2]
        ev_wfree["vtm"] = ev_uh_w
        ev_wfree["k2tm"] = ev_state_Sb
        ev_obuf_done = ev_po_evac[2]

        eo2 = V_(lambda e: e.tensor_tensor(
            out=osq[:].rearrange("p a b t -> p (a b t)"),
            in0=qhat[:].rearrange("p a b t -> p (a b t)"),
            in1=qhat[:].rearrange("p a b t -> p (a b t)"), op=OP.mult),
            waits=[ev_obuf_done, ev_state_S], tag=f"o2{it}")
        evmm = None
        for ic in range(NCC):
            evmm = T_(lambda e, ic=ic: e.matmul(
                PW2[0:16, 0:FW], hmask[:, ic, 0:16],
                osq[:, ic, :, :].rearrange("p b t -> p (b t)"),
                start=(ic == 0), stop=(ic == NCC - 1)),
                waits=[eo2] + (bank_open("PW2") if ic == 0 else []),
                tag=f"os{it}{ic}",
                inc="tp" if ic == NCC - 1 else None)
        eol = S_(lambda e: e.activation(
            out=osc[0:16, :], in_=PW2[0:16, 0:FW], func=AFT.Ln,
            bias=epso[0:16, 0:1], scale=1.0 / 64.0), waits=[evmm], tag=f"ol{it}")
        bank_read("PW2", eol)
        eoe = S_(lambda e: e.activation(
            out=osc[0:16, :], in_=osc[0:16, :], func=AFT.Exp, scale=-0.5),
            waits=[eol], tag=f"oe2{it}")
        evon = []
        for ic in range(NCC):
            half = (ic % 2) * 512
            evb = T_(lambda e, ic=ic, half=half: e.matmul(
                PW1[:, half:half + FW], hmaskT[:, ic, :], osc[0:16, :],
                start=True, stop=True),
                waits=[eoe] + bank_open("PW1"),
                tag=f"ob{it}{ic}")
            ev1 = V_(lambda e, ic=ic, half=half: e.tensor_tensor(
                out=osq[:, ic, :, :].rearrange("p b t -> p (b t)"),
                in0=qhat[:, ic, :, :].rearrange("p b t -> p (b t)"),
                in1=PW1[:, half:half + FW], op=OP.mult),
                waits=[evb], tag=f"on{it}{ic}")
            bank_read("PW1", ev1)
            evon.append(V_(lambda e, ic=ic: e.tensor_tensor(
                out=qhat[:, ic, :, :].rearrange("p b t -> p (b t)"),
                in0=osq[:, ic, :, :].rearrange("p b t -> p (b t)"),
                in1=gatebuf[:, ic, :, :].rearrange("p b t -> p (b t)"), op=OP.mult),
                waits=[ev1, ev_gate_done], tag=f"og{it}{ic}"))
        ev_ofinal = evon[-1]

        for oc in range(NCC):
            pocol = ((oc // 2) % 2) * 512 + (oc % 2) * FW
            ex = dma("sync", "sdf0" if oc % 2 == 0 else "sdf1", xfm[:, oc % 2, :].rearrange("p (b t) -> p b t", b=B),
                     x_fm_d[:, oc * 128:(oc + 1) * 128, t0:t0 + TILE].rearrange("b p t -> p b t"),
                     waits=[ev_laststore.get((it - 1, oc)), ev_laststore.get((it, oc - 2))],
                     tag=f"xf{it}{oc}")
            evmm = None
            for kc in range(NCC):
                evmm = T_(lambda e, oc=oc, kc=kc, pocol=pocol: e.matmul(
                    PO[:, pocol:pocol + FW], W["o"][:, kc, oc * 128:(oc + 1) * 128],
                    qhat[:, kc, :, :].rearrange("p b t -> p (b t)"),
                    start=(kc == 0), stop=(kc == NCC - 1)),
                    waits=[ev_ofinal, ev_wcast[("o", NCC - 1)]]
                    + (bank_open("PO") if kc == 0 else []),
                    tag=f"wo{it}{oc}_{kc}", inc="tp" if kc == NCC - 1 else None)
            er1 = V_(lambda e, oc=oc, pocol=pocol: e.scalar_tensor_tensor(
                out=resid[:, oc % 2, :], in0=PO[:, pocol:pocol + FW], scalar=1.0,
                in1=ffng[:, oc, :, :].rearrange("p b t -> p (b t)"),
                op0=OP.mult, op1=OP.mult),
                waits=[evmm, ev_ffn_done, ev_state_Sb, ev_ofinal,
                       ev_laststore.get((it, oc - 2)) or ev_laststore.get((it - 1, oc + 6))],
                tag=f"r1{it}{oc}")
            bank_read("PO", er1)
            er2 = V_(lambda e, oc=oc: e.tensor_tensor(
                out=resid[:, oc % 2, :], in0=resid[:, oc % 2, :],
                in1=xfm[:, oc % 2, :], op=OP.add),
                waits=[er1, ex], tag=f"r2{it}{oc}")
            ev_laststore[(it, oc)] = dma(
                "sync", "sdo0" if oc % 2 == 0 else "sdo1", y_fm_d[:, oc * 128:(oc + 1) * 128, t0:t0 + TILE].rearrange("b p t -> p b t"),
                resid[:, oc % 2, :].rearrange("p (b t) -> p b t", b=B),
                waits=[er2], tag=f"st{it}{oc}")
        ev_wfree["gate"] = er1
        ev_wfree["ffn"] = er1
        ev_wfree["oxs"] = [ev_laststore[(it, NCC - 2)], ev_laststore[(it, NCC - 1)]]
        ev_wfree["k2prod"] = [evo[-1], evst[-1]]
        ev_wfree["qhat"] = evmm
        ev_wfree["obuf"] = evmm
        ev_wfree["qtil"] = ev_po_evac[2]
        ev_Tend = P.last_ev["tensor"]
        ev_wfree["Tend"] = ev_Tend
        ev_wfree["Gs"] = ev_Tend
        ev_wfree["rhs"] = ev_Tend
        ev_wfree["expGfm"] = ev_Tend
        ev_wfree["eGceFM"] = ev_Tend
        ev_wfree["qsc"] = [ev_sc_done["q"][1], ev_sc_done["k"][1]]
        ev_wfree["tmsmalls"] = ev_Tend
        ev_wfree["khat"] = ev_last_pr
        ev_wfree["PO"] = ev_po_evac[2]
        ev_wfree["pw2"] = ev_state_Sb

    def _finalwait(e):
        last = None
        for snm in DMA_SEMS:
            n = P.final_cnt.get(snm, 0)
            if n:
                last = e.wait_ge(sems[snm], n * 16)
        return last
    P.stream = ("F",)
    P.op("sync", _finalwait, tag="final_wait")

    emit_groups = [[("P",), ("L", 0)]]
    for t in range(NT):
        g = [("S", t)]
        if t + 1 < NT:
            g.append(("L", t + 1))
        emit_groups.append(g)
    emit_groups.append([("F",)])

    blk = es.enter_context(nc.Block())
    P.emit(blk, sems, groups=emit_groups)
    es.close()
    return nc



_PROG_CACHE = {}


def _get_prog(T):
    if T not in _PROG_CACHE:
        cfg = CFG(T)
        _PROG_CACHE[T] = (cfg, build_program(cfg))
    return _PROG_CACHE[T]


def make_in_maps(cfg, inputs):
    x = np.ascontiguousarray(inputs["L0_attn_hidden"], dtype=np.float32)
    NBAT = x.shape[0]
    T = x.shape[1]
    consts = host_constants(cfg)

    def t(a):
        return np.ascontiguousarray(np.asarray(a, dtype=np.float32).T)

    shared = {
        "w_qt": t(inputs["L0_attn_Wq"]), "w_kt": t(inputs["L0_attn_Wk"]),
        "w_vt": t(inputs["L0_attn_Wv"]), "w_gt": t(inputs["L0_attn_Wg"]),
        "w_ot": t(inputs["L0_attn_Wo"]), "w_ft": t(inputs["L0_ffn_weight"]),
        "wbat": np.ascontiguousarray(
            np.concatenate([t(inputs["L0_attn_Wb"])[:, SIG],
                            t(inputs["L0_attn_Wa"])[:, SIG]], axis=1)),
        "alog": np.asarray(inputs["L0_attn_A_log"], np.float32)[SIG].reshape(1, -1),
        "dtrep": np.tile(np.asarray(inputs["L0_attn_dt_bias"], np.float32)[SIG][None, :],
                         (128, 1)),
        "normw2": np.tile(np.asarray(inputs["L0_attn_norm_w"], np.float32), 2)[:, None].copy(),
    }
    cw = np.stack([np.asarray(inputs["L0_attn_conv_q"], np.float32),
                   np.asarray(inputs["L0_attn_conv_k"], np.float32),
                   np.asarray(inputs["L0_attn_conv_v"], np.float32)])
    shared["convw"] = np.ascontiguousarray(
        cw.reshape(3, cfg.NCC, 128, 4).transpose(2, 0, 1, 3))
    for nm, a in consts.items():
        shared["c_" + nm] = np.ascontiguousarray(a)

    in_maps = []
    for ci in range(NCORES):
        m = dict(shared)
        xs = x[2 * ci:2 * ci + 2]
        m["x_tm"] = np.ascontiguousarray(xs)
        m["x_fm"] = np.ascontiguousarray(xs.transpose(0, 2, 1))
        in_maps.append(m)
    return in_maps


def kernel(**inputs):
    x = inputs["L0_attn_hidden"]
    NBAT, T, H = x.shape
    cfg, nc = _get_prog(T)
    in_maps = make_in_maps(cfg, inputs)
    res = run_bass_kernel_spmd(nc, in_maps, core_ids=list(range(NCORES)))
    outs = []
    for ci in range(NCORES):
        y = np.asarray(res.results[ci]["y_fm"])
        outs.append(y.transpose(0, 2, 1))
    return np.ascontiguousarray(np.concatenate(outs, axis=0), dtype=np.float32)


# version 21
# speedup vs baseline: 1.1164x; 1.0356x over previous
import numpy as np
import ml_dtypes
import contextlib

import concourse.bass as bass
import concourse.mybir as mybir
from concourse.bass_utils import run_bass_kernel_spmd
from concourse.alu_op_type import AluOpType as OP

F32 = mybir.dt.float32
BF16 = mybir.dt.bfloat16
AFT = mybir.ActivationFunctionType
BF = ml_dtypes.bfloat16

NCORES = 8
ENGINES = ("sync", "gpsimd", "vector", "scalar", "tensor")
DMA_SEMS = {"sdc", "sdw", "sdw1", "sdx", "sdx1", "sdf0", "sdf1", "sdo0", "sdo1", "gdr", "gdf"}
SIG = np.array([(s % 8) * 2 + s // 8 for s in range(16)])


class CFG:
    def __init__(self, T=2048):
        self.T = T
        self.H = 1024
        self.NH = 16
        self.DH = 64
        self.B = 2
        self.TILE = 128
        self.NT = T // self.TILE
        self.NCC = 8
        self.FW = self.B * self.TILE
        self.WIDE = self.NH * self.DH


class Ev:
    __slots__ = ("sem", "val", "eng")
    def __init__(self, sem, val, eng):
        self.sem, self.val, self.eng = sem, val, eng


class Prog:
    def __init__(self):
        self.ops = {e: [] for e in ENGINES}
        self.seq = 0
        self.stream = ("P",)
        self.last_ev = {}
        self._lastfam = None
        self.final_cnt = {}

    def unit(self, sem):
        return 16 if sem in DMA_SEMS else 1

    def op(self, eng, fn, waits=(), inc=None, tag=""):
        ev = None
        if inc is not None:
            ev = Ev(inc, None, eng)
            self.last_ev[eng] = ev
        flat = []
        def _fl(w):
            if w is None:
                return
            if isinstance(w, (list, tuple)):
                for x in w:
                    _fl(x)
            else:
                flat.append(w)
        for w in waits:
            _fl(w)
        self.ops[eng].append((fn, tuple(flat), inc, ev, tag, self.seq, self.stream))
        self.seq += 1
        return ev

    _COST = (
        ("pj", 160), ("wo", 160), ("vt", 130), ("xnt", 130), ("k2t", 130),
        ("kk", 105), ("qk", 105), ("pr", 105), ("ap", 105), ("sqN", 105),
        ("sqM", 105), ("oi", 105), ("ox", 105), ("st", 105),
        ("dg", 380), ("G", 100), ("Gf", 120), ("ba", 120),
        ("bc", 120), ("bq", 120), ("bk", 120), ("ob", 120),
        ("ss", 150), ("os", 150), ("dw", 60),
        ("cv", 520), ("ye", 1150), ("B", 1150), ("BT", 1150), ("Mm", 1150),
        ("u1", 1200), ("u2", 1200), ("dv", 1150), ("oe", 1200),
        ("sd", 1150), ("sa", 1150), ("sc", 560), ("qt", 560), ("k2", 560),
        ("on", 560), ("og", 560), ("gn", 2300), ("lns", 450), ("lnq", 450),
        ("lna", 450), ("r1", 600), ("r2", 600),
        ("pe", 1250), ("sv", 1250), ("sl", 1800), ("sq", 1800), ("dx", 1250),
        ("k2e", 650), ("vte", 650), ("xne", 650), ("oxe", 1150),
        ("sb", 650), ("Ne", 1250), ("Me", 1250),
        ("xtm", 2500), ("xf", 2500), ("st", 2500), ("c_", 2000),
        ("wd", 2500), ("fA", 1200), ("f3", 1200), ("asm", 800), ("rb", 800),
    )

    _FAM = (
        ("sig", 1), ("sl", 1), ("sv", 1), ("lnq", 2),
        ("lnl", 0), ("lne", 0), ("re", 0), ("spe", 0), ("spl", 0),
        ("Ge", 0), ("egc", 0), ("gce2", 0), ("egf", 0), ("ol", 0),
        ("oe2", 0),
    )

    def _family(self, tag):
        for pfx, f in self._FAM:
            if tag.startswith(pfx):
                return f
        return None

    def _cost(self, tag, eng):
        if eng == "gpsimd" and tag[:2] in ("sq", "o2", "gn"):
            return 3500
        if eng in ("sync", "gpsimd"):
            return 900
        for pfx, c in self._COST:
            if tag.startswith(pfx):
                return c
        return 300

    def _final_order(self, groups):
        if groups is None:
            return {e: list(self.ops[e]) for e in ENGINES}
        sub = {e: {} for e in ENGINES}
        for e in ENGINES:
            for op in self.ops[e]:
                sub[e].setdefault(op[6], []).append(op)
        seen = set()
        for g in groups:
            seen.update(g)
        for e in ENGINES:
            for k in sub[e]:
                assert k in seen, f"stream {k} not in emission groups"
        final = {e: [] for e in ENGINES}
        evt = {}
        clock = {e: 0.0 for e in ENGINES}
        for g in groups:
            lists = {}
            for e in ENGINES:
                for k in g:
                    lst = sub[e].get(k)
                    if lst:
                        lists[(e, k)] = lst
            idxs = {key: 0 for key in lists}
            remaining = sum(len(v) for v in lists.values())
            while remaining:
                best, beststart = None, None
                for key, lst in lists.items():
                    i = idxs[key]
                    if i >= len(lst):
                        continue
                    op = lst[i]
                    ok = True
                    start = clock[key[0]]
                    for w in op[1]:
                        t = evt.get(id(w))
                        if t is None:
                            ok = False
                            break
                        if t > start:
                            start = t
                    if not ok:
                        continue
                    if key[0] == "scalar":
                        fam = self._family(op[4])
                        if fam is not None and fam != self._lastfam:
                            start += 1300
                    if beststart is None or start < beststart:
                        best, beststart = key, start
                assert best is not None, (
                    f"stream merge stalled in group {g}")
                e, k = best
                op = lists[best][idxs[best]]
                idxs[best] += 1
                remaining -= 1
                final[e].append(op)
                dur = self._cost(op[4], e)
                if e == "scalar":
                    fam = self._family(op[4])
                    if fam is not None:
                        self._lastfam = fam
                end = beststart + dur
                clock[e] = end
                if op[3] is not None:
                    evt[id(op[3])] = end
        return final

    def emit(self, block, sems, groups=None):
        import os as _os
        import re as _re
        stubk = int(_os.environ.get("STUBK", "-1"))
        stubtag = _os.environ.get("STUBTAG", "")
        stubwin = _os.environ.get("STUBWIN", "")
        wlo, whi = (int(v) for v in stubwin.split(":")) if stubwin else (0, 0)
        final = self._final_order(groups)
        cnt = {}
        for e in ENGINES:
            for op in final[e]:
                if op[2] is not None:
                    cnt[op[2]] = cnt.get(op[2], 0) + 1
                    op[3].val = cnt[op[2]]
        self.final_cnt = cnt
        def mk(ename):
            oplist = final[ename]
            def body(eng):
                waited = {}
                for fn, waits, inc, ev, tag, seq, stream in oplist:
                    stub = (stubk >= 0 and seq >= stubk) or \
                           (wlo <= seq < whi) or \
                           (stubtag and _re.match(stubtag, tag))
                    for w in waits:
                        assert w.val is not None, f"unnumbered wait in {tag}"
                        tgt = w.val * self.unit(w.sem)
                        if waited.get(w.sem, -1) >= tgt:
                            continue
                        eng.wait_ge(sems[w.sem], tgt)
                        waited[w.sem] = tgt
                    if stub:
                        if inc is not None:
                            eng.sem_inc(sems[inc], self.unit(inc))
                        continue
                    instr = fn(eng)
                    if inc is not None:
                        assert instr is not None, f"inc-op returned None: {tag}"
                        instr.then_inc(sems[inc], self.unit(inc))
            return body
        for e in ENGINES:
            getattr(block, e)(mk(e))


def host_constants(cfg):
    NH = cfg.NH
    c = {}
    c["ident128"] = np.eye(128, dtype=BF)
    L = np.zeros((128, 128), np.float32)
    for blk in range(2):
        for s in range(64):
            L[blk * 64 + s, blk * 64 + s:blk * 64 + 64] = 1.0
    c["lones"] = L
    bo = np.zeros((3 * NH, NH * 64), np.float32)
    for l in range(3):
        for h in range(NH):
            bo[l * NH + h, h * 64:h * 64 + 64] = 1.0
    c["bones48"] = bo.astype(BF)
    c["bones32"] = bo[:2 * NH].astype(BF)
    r = np.arange(64)
    mA = np.where(r[:, None] > r[None, :], 0.0, -30000.0)
    m2 = np.where(r[None, :] >= r[:, None], 0.0, -30000.0)
    m3 = np.where(r[None, :] > r[:, None], 0.0, -30000.0)
    c["maskA"] = np.tile(np.concatenate([mA, mA], 0), (1, 8)).astype(BF)
    c["mask2"] = np.tile(np.concatenate([m2, m2], 0), (1, 8)).astype(BF)
    c["mask3"] = np.tile(np.concatenate([m3, m3], 0), (1, 8)).astype(BF)
    hm = np.zeros((cfg.NCC, 128, NH), np.float32)
    hmT = np.zeros((cfg.NCC, NH, 128), np.float32)
    for cc in range(cfg.NCC):
        hm[cc, 0:64, cc] = 1.0
        hm[cc, 64:128, 8 + cc] = 1.0
        hmT[cc, cc, 0:64] = 1.0
        hmT[cc, 8 + cc, 64:128] = 1.0
    c["hmask"] = hm.astype(BF)
    c["hmaskT"] = hmT.astype(BF)
    ho = np.zeros((2, 2 * 2 * 64), np.float32)
    ho[0, 0:64] = 1.0; ho[0, 128:192] = 1.0
    ho[1, 64:128] = 1.0; ho[1, 192:256] = 1.0
    c["halfones"] = ho.astype(BF)
    c["onesrow"] = np.ones((1, 64), dtype=BF)
    c["identf32"] = np.eye(128, dtype=np.float32)
    return c


def build_program(cfg):
    nc = bass.Bass("TRN2", target_bir_lowering=False)
    H, NH, DH, B = cfg.H, cfg.NH, cfg.DH, cfg.B
    TILE, NT, NCC, FW, WIDE = cfg.TILE, cfg.NT, cfg.NCC, cfg.FW, cfg.WIDE
    T = cfg.T
    consts = host_constants(cfg)

    def din(name, shape, dt=F32):
        return nc.dram_tensor(name, shape, dt, kind="ExternalInput").ap()

    x_tm_d = din("x_tm", [B, T, H])
    x_fm_d = din("x_fm", [B, H, T])
    w_d = {w: din(f"w_{w}t", [H, H]) for w in ("q", "k", "v", "g", "o", "f")}
    wbat_d = din("wbat", [H, 2 * NH])
    convw_d = din("convw", [128, 3, NCC, 4])
    alog_d = din("alog", [1, NH])
    dtrep_d = din("dtrep", [128, NH])
    normw2_d = din("normw2", [128, 1])
    cst_d = {nm: din("c_" + nm, list(a.shape), BF16 if a.dtype == BF else F32)
             for nm, a in consts.items()}
    y_fm_d = nc.dram_tensor("y_fm", [B, H, T], F32, kind="ExternalOutput").ap()

    es = contextlib.ExitStack()
    sb = lambda nm, sh, dt=F32: es.enter_context(nc.sbuf_tensor(nm, sh, dt))
    ps = lambda nm, sh, dt=F32: es.enter_context(nc.psum_tensor(nm, sh, dt))

    W = {w: sb(f"W{w}", [128, NCC, H], BF16) for w in ("q", "k", "v", "g", "o", "f")}
    Wba = sb("Wba", [128, NCC, 2 * NH], BF16)
    wstage = sb("wstage", [128, 1024])
    convw = sb("convw_t", [128, 3, NCC, 4])
    ident = sb("identt", [128, 128], BF16)
    identf = sb("identft", [128, 128])
    lones = sb("lonest", [128, 128])
    maskA = sb("maskAt", [128, 512], BF16)
    mask2 = sb("mask2t", [128, 512], BF16)
    mask3 = sb("mask3t", [128, 512], BF16)
    hmask = sb("hmaskt", [128, NCC, NH], BF16)
    hmaskT = sb("hmaskTt", [NH, NCC, 128], BF16)
    onesrow = sb("onesrowt", [1, 64], BF16)
    dtrep = sb("dtrept", [128, NH])
    normw2 = sb("normw2t", [128, 1])
    alog = sb("alogt", [1, NH])
    negexpA = sb("negexpAt", [128, NH])
    epsln = sb("epsln", [128, 1])
    epsl2 = sb("epsl2", [128, 1])
    epso = sb("epso", [128, 1])
    lnq8 = sb("lnq8", [128, 1])
    lnsq = sb("lnsq", [128, 1])
    lnsq2 = sb("lnsq2", [128, 1])
    rhsA = sb("rhsA", [58, 2, WIDE], BF16)
    rhs3 = sb("rhs3", [86, 2, WIDE], BF16)
    Gpos = sb("Gpos", [58, 2, B, 64], BF16)
    Gneg = sb("Gneg", [86, 2, B, 64], BF16)
    xtm = sb("xtm", [128, H])
    xfm = sb("xfm", [128, 2, FW])
    xnf = sb("xnf", [128, NCC, B, TILE], BF16)
    lnsc = sb("lnsc", [128, B, 2])
    prebuf = sb("prebuf", [128, NCC, B, TILE + 3], BF16)
    xnbv = prebuf[:].rearrange("p a b t -> p (a b t)")[:, 0:2 * H].rearrange(
        "p (b h) -> p b h", b=2)
    halo = sb("halo", [128, 3, NCC, B, 3], BF16)
    gatebuf = sb("gatebuf", [128, NCC, B, TILE], BF16)
    ffng = sb("ffng", [128, NCC, B, TILE], BF16)
    convacc = sb("convacc", [128, NCC, B, TILE], BF16)
    silbuf = sb("silbuf", [128, NCC, B, TILE], BF16)
    xtm2 = silbuf[:].rearrange("p a b t -> p (a b t)").bitcast(F32)[:, 0:H]
    khat = sb("khat", [128, NCC, B, TILE], BF16)
    qtil = sb("qtil", [128, NCC, B, TILE], BF16)
    qhat = sb("qhat", [128, NCC, B, TILE], BF16)
    vtm = sb("vtm", [128, 2, WIDE], BF16)
    k2tm = sb("k2tm", [128, 2, WIDE], BF16)
    uh = sb("uh", [128, 2, WIDE], BF16)
    dvb = sb("dvb", [128, 2, WIDE], BF16)
    yv = sb("yv", [128, WIDE], BF16)
    yv2 = sb("yv2", [128, WIDE], BF16)
    Dexp1 = yv
    Dexp3 = yv2
    Mc = sb("Mcc2", [128, WIDE], BF16) if False else None
    tscr = wstage
    S32b = wstage
    Dexp2 = sb("Dexp2", [128, WIDE], BF16)
    McT = Dexp2
    Bw = sb("Bww", [128, WIDE], BF16)
    BTw = sb("BTww", [128, WIDE], BF16)
    Mmw = sb("Mmww", [128, WIDE], BF16)

    Ncr = sb("Ncc", [128, WIDE], BF16)
    Mx = sb("Mxx", [128, WIDE], BF16)
    Nx = sb("Nxx", [128, WIDE], BF16)
    S32 = sb("S32", [128, B, NCC, DH])
    Sb16 = sb("Sb16", [128, B, NCC, DH], BF16)
    qscq = sb("qscq", [16, FW], BF16)
    qsck = sb("qsck", [16, FW], BF16)
    osc = sb("osc", [16, FW], BF16)
    beta_tm = sb("beta_tm", [128, 2, NH])
    lnb_tm = sb("lnb_tm", [128, 2, NH])
    g_tm = sb("g_tm", [128, 2, NH])
    G_tm = sb("G_tm", [128, 2, NH])
    expG_tm = sb("expG_tm", [128, 2, NH])
    tmsc1 = sb("tmsc1", [128, 2, NH])
    Gfm = sb("Gfm", [NH, B, 2, 64])
    Gl1s = sb("Gl1s", [NH, B, TILE], BF16)
    Gn1s = sb("Gn1s", [NH, B, TILE], BF16)
    Lb = [sb(f"Lb{i}", [NH, B, TILE], BF16) for i in range(2)]
    Glv1 = sb("Glv1", [NH, B, TILE])
    Glv2 = sb("Glv2", [NH, B, TILE])
    lnbfm = Glv2
    eGceFM = sb("eGceFM", [NH, B, TILE], BF16)
    expGfm = sb("expGfm", [NH, B, TILE], BF16)
    rowb = sb("rowb", [1, B, NH])
    rowbb = sb("rowbb", [1, 2, B * NCC], BF16)
    resid = wstage[:].rearrange("p (a b) -> p a b", a=2)[:, :, 0:FW]
    osq = wstage[:].bitcast(BF16).rearrange("p (a b t) -> p a b t", a=NCC, b=B)
    oxsv = wstage[:].bitcast(BF16)[:, 0:1024]

    PA = ps("PA", [128, 512])
    PB = ps("PBb", [128, 512])
    PW1 = ps("PW1", [128, 1024])
    PW2 = ps("PW2", [128, 1024])
    PO = ps("PO", [128, 1024])
    PAb = PA[:].bitcast(BF16)
    PW2b = PW2[:].bitcast(BF16)

    sems = {nm: es.enter_context(nc.semaphore(nm))
            for nm in ("sdc", "sdw", "sdw1", "sdx", "sdx1", "sdf0", "sdf1",
                       "sdo0", "sdo1", "gdr", "gdf", "tp", "vp", "sp", "gp")}
    P = Prog()

    def dma(q, sem, out, in_, waits=(), tag="", slow=False):
        return P.op(q, lambda e, o=out, i=in_, s=slow: e.dma_start(
            out=o, in_=i, allow_slow_non_contiguous=s),
            waits=waits, inc=sem, tag=tag)

    def T_(fn, waits=(), tag="", inc="tp"):
        return P.op("tensor", fn, waits=waits, inc=inc, tag=tag)

    def V_(fn, waits=(), tag="", inc="vp"):
        return P.op("vector", fn, waits=waits, inc=inc, tag=tag)

    def S_(fn, waits=(), tag="", inc="sp"):
        return P.op("scalar", fn, waits=waits, inc=inc, tag=tag)

    def G_(fn, waits=(), tag="", inc="gp"):
        return P.op("gpsimd", fn, waits=waits, inc=inc, tag=tag)

    bkr = {k: [] for k in ("PA", "PB", "PW1", "PW2", "PO")}

    def bank_open(bank):
        ws = bkr[bank]
        bkr[bank] = []
        return ws

    def bank_read(bank, ev):
        bkr[bank].append(ev)
        return ev

    evc = []
    for dst, nm in [(ident[:], "ident128"), (identf[:], "identf32"),
                    (lones[:], "lones"),
                    (maskA[:], "maskA"), (mask2[:], "mask2"), (mask3[:], "mask3"),
                    (onesrow[:], "onesrow")]:
        evc.append(dma("sync", "sdc", dst, cst_d[nm], tag=f"c_{nm}"))
    evc.append(dma("sync", "sdc", hmask[:], cst_d["hmask"].rearrange("a p b -> p a b"),
                   tag="c_hm"))
    evc.append(dma("sync", "sdc", hmaskT[:], cst_d["hmaskT"].rearrange("a p b -> p a b"),
                   tag="c_hmT"))
    evc.append(dma("sync", "sdc", convw[:].rearrange("p a b c -> p (a b c)"),
                   convw_d.rearrange("p a b c -> p (a b c)"), tag="c_cw"))
    evc.append(dma("sync", "sdc", dtrep[:], dtrep_d, tag="c_dt"))
    evc.append(dma("sync", "sdc", normw2[:], normw2_d, tag="c_nw"))
    evc.append(dma("sync", "sdc", alog[:], alog_d, tag="c_al"))
    for ck in range(2):
        evc.append(dma("sync", "sdc", rhsA[0:48, ck, :], cst_d["bones48"], tag="c_bA"))
        evc.append(dma("sync", "sdc", rhs3[0:48, ck, :], cst_d["bones48"], tag="c_b3a"))
        evc.append(dma("sync", "sdc", rhs3[54:86, ck, :], cst_d["bones32"], tag="c_b3b"))
    sel0 = cst_d["halfones"][0:1, :]
    sel1 = cst_d["halfones"][1:2, :]
    row = lambda ap: ap.rearrange("p ck b r -> p (ck b r)")
    for i in range(5):
        evc.append(dma("sync", "sdc", row(Gpos[48 + i:49 + i]), sel0, tag="hoA"))
        evc.append(dma("sync", "sdc", row(Gpos[53 + i:54 + i]), sel1, tag="hoB"))
    for i in range(3):
        evc.append(dma("sync", "sdc", row(Gneg[48 + i:49 + i]), sel0, tag="ho3A"))
        evc.append(dma("sync", "sdc", row(Gneg[51 + i:52 + i]), sel1, tag="ho3B"))
    ev_const = evc[-1]

    evm = []
    evm.append(V_(lambda e: e.memset(epsln[:], 1e-5), tag="m1"))
    evm.append(V_(lambda e: e.memset(epsl2[:], 1e-6), tag="m2"))
    evm.append(V_(lambda e: e.memset(epso[:], 1e-5), tag="m3"))
    evm.append(V_(lambda e: e.memset(lnq8[:], float(np.log(1.0 / 8.0))), tag="m4"))
    ev_S_w = V_(lambda e: e.memset(S32[:].rearrange("p a b c -> p (a b c)"), 0.0), tag="m5")
    ev_Sb_w = V_(lambda e: e.memset(Sb16[:].rearrange("p a b c -> p (a b c)"), 0.0), tag="m6")
    ev_halo0 = V_(lambda e: e.memset(halo[:].rearrange("p a b c d -> p (a b c d)"), 0.0), tag="m7")
    ev_minit = ev_halo0

    ev_e1 = S_(lambda e: e.activation(out=alog[:], in_=alog[:], func=AFT.Exp),
               waits=[ev_const], tag="expA")
    wA0 = [ev_e1] + bank_open("PB")
    ev_e2a = T_(lambda e: e.matmul(PB[0:64, 496:496 + NH], lones[0:1, 0:64], alog[:],
                                   start=True, stop=True, tile_position=(0, 0)),
                waits=wA0, tag="bcA0")
    ev_e2 = T_(lambda e: e.matmul(PB[64:128, 496:496 + NH], lones[0:1, 0:64], alog[:],
                                  start=True, stop=True, tile_position=(0, 64)),
               waits=[ev_e2a], tag="bcA1")
    ev_negA = V_(lambda e: e.tensor_scalar(out=negexpA[:], in0=PB[:, 496:496 + NH],
                                           scalar1=-1.0, scalar2=None, op0=OP.mult),
                 waits=[ev_e2], tag="negA")
    bank_read("PB", ev_negA)

    ev_wcast = {}
    half_cast = [None, None]
    nslot = 0
    for ic in range(NCC):
        hh = nslot % 2
        nslot += 1
        evd = dma("sync" if hh == 0 else "scalar", "sdw" if hh == 0 else "sdw1",
                  wstage[:, hh * 512:hh * 512 + 2 * NH],
                  wbat_d[ic * 128:(ic + 1) * 128, :],
                  waits=[half_cast[hh]], tag=f"wdba{ic}")
        half_cast[hh] = V_(lambda e, ic=ic, hh=hh: e.tensor_scalar(
            out=Wba[:, ic, :], in0=wstage[:, hh * 512:hh * 512 + 2 * NH],
            scalar1=1.0, scalar2=None, op0=OP.mult),
            waits=[evd], tag=f"wcba{ic}")
        ev_wcast[("ba", ic)] = half_cast[hh]
    for wname in ("q", "k", "v", "g", "o", "f"):
        for ic in range(NCC):
            last = None
            for ch in range(2):
                hh = nslot % 2
                nslot += 1
                evd = dma("sync" if hh == 0 else "scalar",
                          "sdw" if hh == 0 else "sdw1",
                          wstage[:, hh * 512:hh * 512 + 512],
                          w_d[wname][ic * 128:(ic + 1) * 128, ch * 512:ch * 512 + 512],
                          waits=[half_cast[hh]], tag=f"wd{wname}{ic}{ch}")
                if wname == "f":
                    last = S_(lambda e, ic=ic, hh=hh, ch=ch: e.sign(
                        out=W["f"][:, ic, ch * 512:ch * 512 + 512],
                        in_=wstage[:, hh * 512:hh * 512 + 512]),
                        waits=[evd], tag=f"wsg{ic}{ch}")
                else:
                    last = V_(lambda e, wn=wname, ic=ic, hh=hh, ch=ch: e.tensor_scalar(
                        out=W[wn][:, ic, ch * 512:ch * 512 + 512],
                        in0=wstage[:, hh * 512:hh * 512 + 512],
                        scalar1=1.0, scalar2=None, op0=OP.mult),
                        waits=[evd], tag=f"wc{wname}{ic}{ch}")
                half_cast[hh] = last
            ev_wcast[(wname, ic)] = last


    ev_xtm_use = ev_minit
    ev_xtm2_use = ev_minit
    ev_lnsq = None
    ev_uh_use = None
    ev_uh_w = None
    ev_state_S = ev_S_w
    ev_state_Sb = [ev_Sb_w, ev_Sb_w]
    ev_halo = [ev_halo0, ev_halo0, ev_halo0]
    ev_prebuf_free = [None, None, None]
    bankfree = {0: None, 1: None}
    ev_wfree = {}
    ev_laststore = {}

    for it in range(NT):
        P.stream = ("L", it)
        t0 = it * TILE
        ev_lnap = []
        for b in range(B):
            xb = xtm if b == 0 else xtm2
            ev_xt = dma("sync", "sdx" if b == 0 else "sdx1", xb[:, :],
                        x_tm_d[b, t0:t0 + TILE, :],
                        waits=[ev_xtm_use if b == 0 else
                               [ev_xtm2_use, ev_wfree.get("silbuf")]],
                        tag=f"xtm{it}_{b}")
            e1 = V_(lambda e, b=b, xb=xb: e.tensor_scalar(
                out=xb[:, :], in0=xb[:, :], scalar1=1.0, scalar2=0.0,
                op0=OP.mult, op1=OP.add, accum_out=lnsc[:, b, 0:1]),
                waits=[ev_xt], tag=f"lns{it}_{b}")
            e2w = bank_open("PA")
            e2a = S_(lambda e, b=b, xb=xb: e.activation(
                out=PA[:, 0:512], in_=xb[:, 0:512], func=AFT.Square,
                accum_out=lnsc[:, b, 1:2]), waits=[ev_xt, e1, ev_lnsq] + e2w,
                tag=f"lnq{it}_{b}")
            e2b = S_(lambda e, b=b, xb=xb: e.activation(
                out=PA[:, 0:512], in_=xb[:, 512:1024], func=AFT.Square,
                accum_out=lnsq[:, 0:1] if b == 0 else lnsq2[:, 0:1]),
                waits=[e2a], tag=f"lnq2{it}_{b}")
            bank_read("PA", e2b)
            e2 = V_(lambda e, b=b: e.tensor_tensor(
                out=lnsc[:, b, 1:2], in0=lnsc[:, b, 1:2],
                in1=lnsq[:, 0:1] if b == 0 else lnsq2[:, 0:1],
                op=OP.add), waits=[e2b], tag=f"lnq3{it}_{b}")
            ev_lnsq = e2
            e3 = V_(lambda e, b=b: e.tensor_scalar(
                out=lnsc[:, b, :], in0=lnsc[:, b, :], scalar1=1.0 / H,
                scalar2=None, op0=OP.mult), waits=[e1, e2], tag=f"lnm{it}_{b}")
            e5 = V_(lambda e, b=b: e.scalar_tensor_tensor(
                out=lnsc[:, b, 1:2], in0=lnsc[:, b, 0:1], scalar=lnsc[:, b, 0:1],
                in1=lnsc[:, b, 1:2], op0=OP.mult, op1=OP.subtract),
                waits=[e3], tag=f"lnv{it}_{b}")
            e6 = V_(lambda e, b=b: e.tensor_scalar(
                out=lnsc[:, b, 1:2], in0=lnsc[:, b, 1:2], scalar1=-1.0,
                scalar2=None, op0=OP.mult), waits=[e5], tag=f"lnn{it}_{b}")
            e7 = S_(lambda e, b=b: e.activation(
                out=lnsc[:, b, 1:2], in_=lnsc[:, b, 1:2], func=AFT.Ln,
                bias=epsln[:, 0:1]), waits=[e6], tag=f"lnl{it}_{b}")
            e8 = S_(lambda e, b=b: e.activation(
                out=lnsc[:, b, 1:2], in_=lnsc[:, b, 1:2], func=AFT.Exp,
                scale=-0.5), waits=[e7], tag=f"lne{it}_{b}")
            e9 = V_(lambda e, b=b, xb=xb: e.tensor_scalar(
                out=xnbv[:, b, :], in0=xb[:, :],
                scalar1=lnsc[:, b, 0:1], scalar2=lnsc[:, b, 1:2],
                op0=OP.subtract, op1=OP.mult),
                waits=[e8, ev_prebuf_free[0], ev_prebuf_free[1],
                       ev_prebuf_free[2], ev_halo[0], ev_halo[1], ev_halo[2]],
                tag=f"lna{it}_{b}")
            ev_lnap.append(e9)
            if b == 0:
                ev_xtm_use = [e9, e1, e2b]
            else:
                ev_xtm2_use = [e9, e1, e2b]
                ev_wfree["silbuf"] = [e9, e2b, ev_wfree.get("silbuf")]

        ev_xnf = []
        for b in range(B):
            for g4 in range(2):
                evt = None
                for q4 in range(4):
                    ic = g4 * 4 + q4
                    evt = T_(lambda e, b=b, ic=ic, q4=q4: e.transpose(
                        PAb[:, q4 * 128:q4 * 128 + 128],
                        xnbv[:, b, ic * 128:ic * 128 + 128],
                        ident[:]),
                        waits=[ev_lnap[b]] + (bank_open("PA") if q4 == 0 else []),
                        tag=f"xnt{it}_{b}_{ic}")
                enf = S_(lambda e, b=b, g4=g4: e.activation(
                    out=xnf[:, g4 * 4:g4 * 4 + 4, b, :],
                    in_=PAb[:, 0:512].rearrange("p (a t) -> p a t", a=4), func=AFT.Copy),
                    waits=[evt, ev_wfree.get("xnf")], tag=f"xne{it}_{b}_{g4}")
                ev_xnf.append(enf)
                bank_read("PA", enf)
        ev_xnf_done = ev_xnf[-1]
        ev_uh_use = ev_xnf_done

        ev_pre = {}
        ev_sc_done = {}
        banks = [PA, PB]
        for wname in ("v", "k", "q"):
            wi = {"q": 0, "k": 1, "v": 2}[wname]
            ehc = V_(lambda e, wi=wi: e.tensor_copy(
                out=prebuf[:, :, :, 0:3],
                in_=halo[:, wi, :, :, :]),
                waits=[ev_halo[wi], ev_prebuf_free[wi], ev_xnf_done],
                tag=f"hi{it}_{wi}")
            evac = {}
            for oc in range(NCC):
                bank = banks[(oc // 2) % 2]
                bnm = "PA" if (oc // 2) % 2 == 0 else "PB"
                half = (oc % 2) * FW
                evmm = None
                for kc in range(NCC):
                    evmm = T_(lambda e, wn=wname, oc=oc, kc=kc, bank=bank, half=half: e.matmul(
                        bank[:, half:half + FW], W[wn][:, kc, oc * 128:(oc + 1) * 128],
                        xnf[:, kc, :, :].rearrange("p b t -> p (b t)"),
                        start=(kc == 0), stop=(kc == NCC - 1)),
                        waits=[ev_xnf_done, ev_wcast[(wname, NCC - 1)]]
                        + (bank_open(bnm) if kc == 0 else []),
                        tag=f"pj{it}{wname}{oc}_{kc}",
                        inc="tp" if kc == NCC - 1 else None)
                if oc % 2 == 1:
                    eev = S_(lambda e, oc=oc, bank=bank: e.activation(
                        out=prebuf[:, oc - 1:oc + 1, :, 3:3 + TILE],
                        in_=bank[:, 0:2 * FW].rearrange("p (o b t) -> p o b t", o=2, b=B),
                        func=AFT.Copy),
                        waits=[evmm, ehc], tag=f"pe{it}{wname}{oc}")
                    evac[oc - 1] = evac[oc] = eev
                    bank_read(bnm, eev)
            ev_pre[wname] = evac[NCC - 1]
            ev_halo[wi] = V_(lambda e, wi=wi: e.tensor_copy(
                out=halo[:, wi, :, :, :],
                in_=prebuf[:, :, :, TILE:TILE + 3]),
                waits=[ev_pre[wname]], tag=f"ho{it}_{wi}")

            if wname == "v":
                cwb = lambda j: convw[:, 2, :, j].unsqueeze(2).unsqueeze(3).broadcast_to(
                    [128, NCC, B, TILE])
                ecv = V_(lambda e, cw0=cwb(0): e.tensor_tensor(
                    out=convacc[:, :, :, :], in0=prebuf[:, :, :, 0:TILE],
                    in1=cw0, op=OP.mult),
                    waits=[evac[NCC - 1], ev_const, ev_wfree.get("convacc")],
                    tag=f"cv{it}v0")
                for j in range(1, 4):
                    ez2 = V_(lambda e, j=j, cwj=cwb(j): e.tensor_tensor(
                        out=silbuf[:, :, :, :], in0=prebuf[:, :, :, j:j + TILE],
                        in1=cwj, op=OP.mult),
                        waits=[ecv,
                               ev_wfree.get("silbuf") if j == 1 else None],
                        tag=f"cw{it}v{j}")
                    ecv = V_(lambda e: e.tensor_tensor(
                        out=convacc[:, :, :, :], in0=convacc[:, :, :, :],
                        in1=silbuf[:, :, :, :], op=OP.add),
                        waits=[ez2], tag=f"cv{it}v{j}")
                ev_prebuf_free[wi] = ecv
                ev_wfree["convacc"] = None
                evsil = S_(lambda e: e.activation(
                    out=silbuf[:].rearrange("p a b t -> p (a b t)"),
                    in_=convacc[:].rearrange("p a b t -> p (a b t)"), func=AFT.Silu),
                    waits=[ecv, ev_wfree.get("silbuf")], tag=f"sl{it}v")
                evvt = []
                for b in range(B):
                    for ck in range(2):
                        for g2 in range(2):
                            evt = None
                            for q4 in range(4):
                                ic = g2 * 4 + q4
                                evt = T_(lambda e, b=b, ck=ck, ic=ic, q4=q4: e.transpose(
                                    PAb[0:64, q4 * 128:q4 * 128 + 128],
                                    silbuf[:, ic, b, ck * 64:ck * 64 + 64],
                                    ident[:]),
                                    waits=[evsil] + (bank_open("PA") if q4 == 0 else []),
                                    tag=f"vt{it}_{b}{ck}{ic}")
                            evte = S_(lambda e, b=b, ck=ck, g2=g2: e.activation(
                                out=vtm[64 * b:64 * b + 64, ck, g2 * 512:g2 * 512 + 512],
                                in_=PAb[0:64, 0:512], func=AFT.Copy),
                                waits=[evt, ev_wfree.get("vtm")], tag=f"vte{it}_{b}{ck}{g2}")
                            evvt.append(evte)
                            bank_read("PA", evte)
                ev_pre["vtm"] = evvt[-1]
                ev_wfree["silbuf"] = evvt[-1]
                ev_wfree["convacc"] = evsil
                continue

            cwb = lambda j: convw[:, wi, :, j].unsqueeze(2).unsqueeze(3).broadcast_to(
                [128, NCC, B, TILE])
            ecv = V_(lambda e, cw0=cwb(0): e.tensor_tensor(
                out=convacc[:, :, :, :], in0=prebuf[:, :, :, 0:TILE],
                in1=cw0, op=OP.mult),
                waits=[evac[NCC - 1], ev_const, ev_wfree.get("convacc")],
                tag=f"cv{it}{wname}0")
            for j in range(1, 4):
                ez2 = V_(lambda e, j=j, cwj=cwb(j): e.tensor_tensor(
                    out=silbuf[:, :, :, :], in0=prebuf[:, :, :, j:j + TILE],
                    in1=cwj, op=OP.mult),
                    waits=[ecv,
                           ev_wfree.get("silbuf") if j == 1 else None],
                    tag=f"cw{it}{wname}{j}")
                ecv = V_(lambda e: e.tensor_tensor(
                    out=convacc[:, :, :, :], in0=convacc[:, :, :, :],
                    in1=silbuf[:, :, :, :], op=OP.add),
                    waits=[ez2], tag=f"cv{it}{wname}{j}")
            ev_prebuf_free[wi] = ecv
            esil = S_(lambda e: e.activation(
                out=silbuf[:].rearrange("p a b t -> p (a b t)"),
                in_=convacc[:].rearrange("p a b t -> p (a b t)"), func=AFT.Silu),
                waits=[ecv, ev_wfree.get("silbuf")], tag=f"sl{it}{wname}")
            esq = V_(lambda e: e.tensor_tensor(
                out=convacc[:].rearrange("p a b t -> p (a b t)"),
                in0=silbuf[:].rearrange("p a b t -> p (a b t)"),
                in1=silbuf[:].rearrange("p a b t -> p (a b t)"), op=OP.mult),
                waits=[esil], tag=f"sq{it}{wname}")
            ev_wfree["convacc"] = esq
            evmm = None
            for ic in range(NCC):
                evmm = T_(lambda e, ic=ic: e.matmul(
                    PB[0:16, 0:FW], hmask[:, ic, 0:16],
                    convacc[:, ic, :, :].rearrange("p b t -> p (b t)"),
                    start=(ic == 0), stop=(ic == NCC - 1)),
                    waits=[esq] + (bank_open("PB") if ic == 0 else []),
                    tag=f"ss{it}{wname}{ic}",
                    inc="tp" if ic == NCC - 1 else None)
            qst = qscq if wname == "q" else qsck
            el1 = S_(lambda e, qst=qst: e.activation(
                out=qst[:], in_=PB[0:16, 0:FW], func=AFT.Ln,
                bias=epsl2[0:16, 0:1]), waits=[evmm, ev_wfree.get("qsc")],
                tag=f"rl{it}{wname}")
            bank_read("PB", el1)
            if wname == "q":
                el2 = S_(lambda e, qst=qst: e.activation(
                    out=qst[:], in_=qst[:],
                    func=AFT.Exp, scale=-0.5, bias=lnq8[0:16, 0:1]),
                    waits=[el1], tag=f"re{it}{wname}")
            else:
                el2 = S_(lambda e, qst=qst: e.activation(
                    out=qst[:], in_=qst[:],
                    func=AFT.Exp, scale=-0.5), waits=[el1], tag=f"re{it}{wname}")
            dst = qhat if wname == "q" else khat
            evsc = []
            for ic in range(NCC):
                bbank = banks[ic % 2]
                bbnm = "PA" if ic % 2 == 0 else "PB"
                evb = T_(lambda e, ic=ic, qst=qst, bbank=bbank: e.matmul(
                    bbank[:, 0:FW], hmaskT[:, ic, :], qst[:],
                    start=True, stop=True),
                    waits=[el2] + bank_open(bbnm),
                    tag=f"bc{it}{wname}{ic}")
                evsc.append(V_(lambda e, ic=ic, dst=dst, bbank=bbank: e.tensor_tensor(
                    out=dst[:, ic, :, :].rearrange("p b t -> p (b t)"),
                    in0=silbuf[:, ic, :, :].rearrange("p b t -> p (b t)"),
                    in1=bbank[:, 0:FW], op=OP.mult),
                    waits=[evb, ev_wfree.get("qhat") if wname == "q"
                           else ev_wfree.get("khat")],
                    tag=f"sc{it}{wname}{ic}"))
                bank_read(bbnm, evsc[-1])
            ev_pre[wname + "hat"] = evsc[-1]
            ev_sc_done[wname] = [evsc[-1], evb]
            ev_wfree["silbuf"] = evsc[-1]


        for wname, dstb in (("g", gatebuf), ("f", ffng)):
            evacs = []
            for oc in range(NCC):
                bank = banks[(oc // 2) % 2]
                bnm = "PA" if (oc // 2) % 2 == 0 else "PB"
                half = (oc % 2) * FW
                evmm = None
                for kc in range(NCC):
                    evmm = T_(lambda e, wn=wname, oc=oc, kc=kc, bank=bank, half=half: e.matmul(
                        bank[:, half:half + FW], W[wn][:, kc, oc * 128:(oc + 1) * 128],
                        xnf[:, kc, :, :].rearrange("p b t -> p (b t)"),
                        start=(kc == 0), stop=(kc == NCC - 1)),
                        waits=[ev_xnf_done, ev_wcast[(wname, NCC - 1)]]
                        + (bank_open(bnm) if kc == 0 else []),
                        tag=f"pj{it}{wname}{oc}_{kc}",
                        inc="tp" if kc == NCC - 1 else None)
                if oc % 2 == 1:
                    eev = S_(lambda e, oc=oc, bank=bank, dstb=dstb: e.activation(
                        out=dstb[:, oc - 1:oc + 1, :, :],
                        in_=bank[:, 0:2 * FW].rearrange("p (o b t) -> p o b t", o=2, b=B),
                        func=AFT.Silu),
                        waits=[evmm, ev_wfree.get("gate" if dstb is gatebuf else "ffn")],
                        tag=f"sv{it}{wname}{oc}")
                    evacs.append(eev)
                    bank_read(bnm, eev)
            if wname == "g":
                ev_gate_raw = evacs[-1]
            else:
                ev_ffn_done = evacs[-1]
                ev_wfree["xnf"] = evmm
        eg1 = V_(lambda e: e.tensor_copy(
            out=convacc[:].rearrange("p a b t -> p (a b t)"),
            in_=gatebuf[:].rearrange("p a b t -> p (a b t)")),
            waits=[ev_gate_raw, ev_wfree.get("convacc")], tag=f"gn1{it}")
        ev_gate_done = V_(lambda e: e.tensor_scalar(
            out=gatebuf[:].rearrange("p a b t -> p (a b t)"),
            in0=convacc[:].rearrange("p a b t -> p (a b t)"),
            scalar1=normw2[:, 0:1], scalar2=None, op0=OP.mult),
            waits=[eg1], tag=f"gn2{it}")
        ev_wfree["convacc"] = ev_gate_done

        P.stream = ("S", it)

        evba = {}
        evmm = None
        for ck in range(2):
            for b in range(B):
                for kc in range(NCC):
                    evmm = T_(lambda e, ck=ck, b=b, kc=kc: e.matmul(
                        PW2[64 * b:64 * b + 64, 256 + 32 * ck:288 + 32 * ck],
                        xnf[:, kc, b, ck * 64:ck * 64 + 64],
                        Wba[:, kc, :], start=(kc == 0), stop=(kc == NCC - 1),
                        tile_position=(0, 64 * b)),
                        waits=[ev_xnf_done, ev_wcast[("ba", NCC - 1)]]
                        + (bank_open("PW2") if (ck == 0 and b == 0 and kc == 0) else []),
                        tag=f"ba{it}_{ck}_{b}_{kc}",
                        inc="tp" if kc == NCC - 1 else None)
        bav = PW2[:, 256:320].rearrange("p (c w) -> p c w", c=2)
        eb = S_(lambda e: e.activation(
            out=beta_tm[:, :, :], in_=bav[:, :, 0:NH], func=AFT.Sigmoid),
            waits=[evmm, ev_wfree.get("tmsmalls")], tag=f"sig{it}")
        bank_read("PW2", eb)
        elb = S_(lambda e: e.activation(
            out=lnb_tm[:, :, :], in_=beta_tm[:, :, :], func=AFT.Ln),
            waits=[eb], tag=f"lnb{it}")
        evba[("lnb", 0)] = evba[("lnb", 1)] = elb
        ez = V_(lambda e: e.tensor_tensor(
            out=g_tm[:, :, :], in0=bav[:, :, NH:2 * NH],
            in1=dtrep[:].unsqueeze(1).broadcast_to([128, 2, NH]),
            op=OP.add), waits=[evmm, ev_const], tag=f"z{it}")
        bank_read("PW2", ez)
        ea = V_(lambda e: e.scalar_tensor_tensor(
            out=tmsc1[:, :, :], in0=g_tm[:, :, :], scalar=-1.0,
            op0=OP.mult, in1=g_tm[:, :, :], op1=OP.max),
            waits=[ez], tag=f"abs{it}")
        ee = S_(lambda e: e.activation(
            out=tmsc1[:, :, :], in_=tmsc1[:, :, :], func=AFT.Exp,
            scale=-1.0), waits=[ea, elb], tag=f"spe{it}")
        el = S_(lambda e: e.activation(
            out=tmsc1[:, :, :], in_=tmsc1[:, :, :], func=AFT.Ln,
            bias=1.0), waits=[ee], tag=f"spl{it}")
        er = V_(lambda e: e.tensor_scalar(
            out=g_tm[:, :, :], in0=g_tm[:, :, :], scalar1=0.0,
            scalar2=None, op0=OP.max), waits=[el], tag=f"spr{it}")
        es2 = V_(lambda e: e.tensor_tensor(
            out=g_tm[:, :, :], in0=g_tm[:, :, :], in1=tmsc1[:, :, :],
            op=OP.add), waits=[er], tag=f"spa{it}")
        eg = V_(lambda e: e.tensor_tensor(
            out=g_tm[:, :, :], in0=g_tm[:, :, :],
            in1=negexpA[:].unsqueeze(1).broadcast_to([128, 2, NH]),
            op=OP.mult), waits=[es2, ev_negA], tag=f"gg{it}")
        evg = []
        for ck in range(2):
            for b in range(B):
                evg.append(T_(lambda e, ck=ck, b=b: e.matmul(
                    PW2[64 * b:64 * b + 64, 320 + 16 * ck:336 + 16 * ck],
                    lones[64 * b:64 * b + 64, 64 * b:64 * b + 64],
                    g_tm[64 * b:64 * b + 64, ck, :], start=True, stop=True,
                    tile_position=(64 * b, 64 * b)),
                    waits=[eg] + (bank_open("PW2") if (ck == 0 and b == 0) else []),
                    tag=f"G{it}_{ck}_{b}"))
        ecp = V_(lambda e: e.tensor_copy(
            out=G_tm[:, :, :],
            in_=PW2[:, 320:352].rearrange("p (c h) -> p c h", c=2)),
            waits=[evg[-1]], tag=f"Gc{it}")
        bank_read("PW2", ecp)
        eex = S_(lambda e: e.activation(
            out=expG_tm[:, :, :], in_=G_tm[:, :, :], func=AFT.Exp),
            waits=[ecp], tag=f"Ge{it}")
        evba[("expG", 0)] = evba[("expG", 1)] = eex
        for ck in range(2):
            gfc = 352 + 160 * ck
            eF = T_(lambda e, ck=ck, gfc=gfc: e.matmul(
                PW2[0:NH, gfc:gfc + 128],
                g_tm[:, ck, :],
                lones[:, 0:128],
                start=True, stop=True),
                waits=[eg] + (bank_open("PW2") if ck == 0 else []),
                tag=f"Gf{it}_{ck}")
            egfc = V_(lambda e, ck=ck, gfc=gfc: e.tensor_copy(
                out=Gfm[:, :, ck, :],
                in_=PW2[0:NH, gfc:gfc + 128].rearrange("p (b t) -> p b t", b=B)),
                waits=[eF], tag=f"Gfc{it}_{ck}")
            bank_read("PW2", egfc)
            evba[("gfm", ck)] = egfc
            evba[("gfmfree", ck + 1)] = egfc
        Gl = [Gpos[0:16].rearrange("p ck b r -> p b ck r"), Gl1s[:],
              Gpos[32:48].rearrange("p ck b r -> p b ck r")]
        Gn = [Gneg[0:16].rearrange("p ck b r -> p b ck r"), Gn1s[:],
              Gneg[32:48].rearrange("p ck b r -> p b ck r")]
        gsrc = Gfm[:].rearrange("p b c t -> p (b c t)")
        gl = ev_wfree.get("Gs")
        l1 = V_(lambda e: e.tensor_copy(
            out=Gl[0], in_=Gfm[:]),
            waits=[evba[("gfm", 1)], gl], tag=f"lv1{it}")
        r1 = V_(lambda e: e.tensor_tensor(
            out=Glv1[:].rearrange("p b (ck r) -> p b ck r", ck=2), in0=Gfm[:],
            in1=Gl[0], op=OP.subtract),
            waits=[l1], tag=f"rs1{it}")
        l2 = V_(lambda e: e.tensor_copy(
            out=Gl[1].rearrange("p b t -> p (b t)"),
            in_=Glv1[:].rearrange("p b t -> p (b t)")), waits=[r1], tag=f"lv2{it}")
        r2 = V_(lambda e: e.tensor_tensor(
            out=Glv2[:].rearrange("p b t -> p (b t)"),
            in0=Glv1[:].rearrange("p b t -> p (b t)"),
            in1=Gl[1].rearrange("p b t -> p (b t)"), op=OP.subtract),
            waits=[l2, ev_wfree.get("Glv2")], tag=f"rs2{it}")
        l3 = V_(lambda e: e.tensor_copy(
            out=Gl[2],
            in_=Glv2[:].rearrange("p b (ck r) -> p b ck r", ck=2)), waits=[r2], tag=f"lv3{it}")
        evng = []
        for i in range(3):
            evng.append(V_(lambda e, i=i: e.tensor_scalar(
                out=(Gn[i] if i != 1 else Gn[i].rearrange("p b (ck r) -> p b ck r", ck=2)),
                in0=(Gl[i] if i != 1 else Gl[i].rearrange("p b (ck r) -> p b ck r", ck=2)),
                scalar1=-1.0, scalar2=None, op0=OP.mult),
                waits=[l3], tag=f"ngl{it}_{i}"))
        ng = evng[-1]
        ef1 = S_(lambda e: e.activation(
            out=expGfm[:].rearrange("p b t -> p (b t)"), in_=gsrc, func=AFT.Exp),
            waits=[evba[("gfm", 1)], ev_wfree.get("expGfm")], tag=f"egf{it}")
        ec1 = V_(lambda e: e.tensor_tensor(
            out=Glv1[:].rearrange("p b (c t) -> p b c t", c=2),
            in0=Gfm[:, :, :, 63:64].broadcast_to([NH, B, 2, 64]),
            in1=Gfm[:], op=OP.subtract), waits=[evba[("gfm", 1)], l3], tag=f"gcf{it}")
        ec2 = S_(lambda e: e.activation(
            out=eGceFM[:].rearrange("p b t -> p (b t)"),
            in_=Glv1[:].rearrange("p b t -> p (b t)"), func=AFT.Exp),
            waits=[ec1, ev_wfree.get("eGceFM")], tag=f"gce2{it}")
        evlb = []
        for ck in range(2):
            evlb.append(T_(lambda e, ck=ck: e.transpose(
                PW2[0:NH, ck * 128:ck * 128 + 128],
                lnb_tm[:, ck, :],
                identf[:]),
                waits=[evba[("lnb", 0)], evba[("lnb", 1)],
                       ev_wfree.get("pw2")] + (bank_open("PW2") if ck == 0 else []),
                tag=f"lbt{it}_{ck}"))
        elnb = V_(lambda e: e.tensor_copy(
            out=lnbfm[:].rearrange("p b (c r) -> p b c r", c=2),
            in_=PW2[0:NH, 0:256].rearrange("p (c b r) -> p b c r", c=2, b=B)),
            waits=[evlb[-1], ec2], tag=f"lbc{it}")
        bank_read("PW2", elnb)
        lb1 = V_(lambda e: e.tensor_copy(
            out=Lb[0][:].rearrange("p b t -> p (b t)"),
            in_=lnbfm[:].rearrange("p b t -> p (b t)")), waits=[elnb], tag=f"lb1{it}")
        lb2 = V_(lambda e: e.tensor_tensor(
            out=Lb[1][:].rearrange("p b t -> p (b t)"),
            in0=lnbfm[:].rearrange("p b t -> p (b t)"),
            in1=Lb[0][:].rearrange("p b t -> p (b t)"), op=OP.subtract),
            waits=[lb1], tag=f"lb2{it}")
        ev_wfree["Glv2"] = elnb
        ev_Gs = lb2
        ev_asm = []
        for bb_ in range(B):
            ev_asm.append(dma("gpsimd", "gdf", Gpos[16:32, :, bb_, :],
                              Gl1s[:, bb_, :], waits=[l3, ev_wfree.get("rhs")],
                              tag=f"asmP{it}{bb_}"))
            ev_asm.append(dma("gpsimd", "gdf", Gneg[16:32, :, bb_, :],
                              Gn1s[:, bb_, :], waits=[ng, ev_wfree.get("rhs")],
                              tag=f"asmN{it}{bb_}"))
            for i in range(2):
                ev_asm.append(dma("gpsimd", "gdf",
                                  Gneg[54 + 16 * i:70 + 16 * i, :, bb_, :],
                                  Lb[i][:, bb_, :], waits=[lb2, ev_wfree.get("rhs")],
                                  tag=f"asmL{it}_{i}{bb_}"))
        ev_flats = list(ev_asm)
        for ck in range(2):
            for b in range(B):
                base = 48 if b == 0 else 53
                ev_flats.append(dma(
                    "gpsimd", "gdf", rhsA[base:base + 3, ck, :],
                    Gneg[0:48, ck, b, :],
                    waits=[ev_asm[-1]], tag=f"fA{it}_{ck}_{b}"))
                ev_flats.append(dma(
                    "gpsimd", "gdf", rhsA[base + 3:base + 5, ck, :],
                    Gneg[54:86, ck, b, :],
                    waits=[ev_asm[-1]], tag=f"fAl{it}_{ck}_{b}"))
                b3 = 48 if b == 0 else 51
                ev_flats.append(dma(
                    "gpsimd", "gdf", rhs3[b3:b3 + 3, ck, :],
                    Gpos[0:48, ck, b, :],
                    waits=[ev_asm[-1]], tag=f"f3{it}_{ck}_{b}"))
        ev_flats_done = ev_flats[-1]

        evq = []
        for ic in range(NCC):
            half = (ic % 2) * 512
            evb = T_(lambda e, ic=ic, half=half: e.matmul(
                PW2[:, half:half + FW], hmaskT[:, ic, :],
                expGfm[:].rearrange("p b t -> p (b t)"),
                start=True, stop=True),
                waits=[ef1, ev_sc_done["q"]] + bank_open("PW2"),
                tag=f"bq{it}{ic}")
            evq.append(V_(lambda e, ic=ic, half=half: e.tensor_tensor(
                out=qtil[:, ic, :, :].rearrange("p b t -> p (b t)"),
                in0=qhat[:, ic, :, :].rearrange("p b t -> p (b t)"),
                in1=PW2[:, half:half + FW], op=OP.mult),
                waits=[evb, ev_wfree.get("qtil")], tag=f"qt{it}{ic}"))
            bank_read("PW2", evq[-1])
        ev_pre["qtil"] = evq[-1]
        k2prod = dvb[:].rearrange("p c w -> p (c w)").rearrange(
            "p (a b t) -> p a b t", a=NCC, b=B)
        evk = []
        for ic in range(NCC):
            half = (ic % 2) * 512
            evb = T_(lambda e, ic=ic, half=half: e.matmul(
                PW2[:, half:half + FW], hmaskT[:, ic, :],
                eGceFM[:].rearrange("p b t -> p (b t)"),
                start=True, stop=True),
                waits=[ec2, ev_sc_done["k"]] + bank_open("PW2"),
                tag=f"bk{it}{ic}")
            evk.append(V_(lambda e, ic=ic, half=half: e.tensor_tensor(
                out=k2prod[:, ic, :, :],
                in0=khat[:, ic, :, :],
                in1=PW2[:, half:half + FW].rearrange("p (b t) -> p b t", b=B),
                op=OP.mult),
                waits=[evb, ev_wfree.get("k2prod")], tag=f"k2{it}{ic}"))
            bank_read("PW2", evk[-1])
        evkt = []
        for b in range(B):
            for ck in range(2):
                for g2 in range(2):
                    evt = None
                    for q4 in range(4):
                        ic = g2 * 4 + q4
                        evt = T_(lambda e, b=b, ck=ck, ic=ic, q4=q4: e.transpose(
                            PW2b[0:64, q4 * 128:q4 * 128 + 128],
                            k2prod[:, ic, b, ck * 64:ck * 64 + 64],
                            ident[:]),
                            waits=[evk[-1]] + (bank_open("PW2") if q4 == 0 else []),
                            tag=f"k2t{it}_{b}{ck}{ic}")
                    ekt = S_(lambda e, b=b, ck=ck, g2=g2: e.activation(
                        out=k2tm[64 * b:64 * b + 64, ck, g2 * 512:g2 * 512 + 512],
                        in_=PW2b[0:64, 0:512], func=AFT.Copy),
                        waits=[evt, ev_wfree.get("k2tm")], tag=f"k2e{it}_{b}{ck}{g2}")
                    evkt.append(ekt)
                    bank_read("PW2", ekt)
        ev_pre["k2tm"] = evkt[-1]

        evD = {}
        ev_po_evac = {}
        prev_serial = []
        for ck in range(2):
            wA = [ev_Gs, ev_flats_done, ev_const] + bank_open("PW1")
            for hw_ in range(2):
                mA1 = T_(lambda e, ck=ck, hw_=hw_: e.matmul(
                    PW1[:, hw_ * 512:hw_ * 512 + 512],
                    Gpos[:, ck, :, :],
                    rhsA[:, ck, hw_ * 512:hw_ * 512 + 512], start=True, stop=False),
                    waits=wA, tag=f"dgA{it}_{ck}_{hw_}")
                mA2 = T_(lambda e, hw_=hw_: e.matmul(
                    PW1[:, hw_ * 512:hw_ * 512 + 512], ident[:],
                    maskA[:, 0:512],
                    start=False, stop=True), waits=[mA1], tag=f"dgAm{it}_{ck}_{hw_}")
                wA.append(mA2)
            x1 = []
            for hh in range(2):
                x1.append(S_(lambda e, hh=hh: e.activation(
                    out=Dexp1[:, hh * 512:hh * 512 + 512],
                    in_=PW1[:, hh * 512:hh * 512 + 512], func=AFT.Exp),
                    waits=[wA[-2 + hh], evD.get(("d1f", ck))] + prev_serial,
                    tag=f"dxA{it}_{ck}_{hh}"))
                bank_read("PW1", x1[-1])
            w2 = [x1] + bank_open("PW1")
            for hw_ in range(2):
                m21 = T_(lambda e, ck=ck, hw_=hw_: e.matmul(
                    PW1[:, hw_ * 512:hw_ * 512 + 512],
                    Gneg[0:54, ck, :, :],
                    rhs3[0:54, ck, hw_ * 512:hw_ * 512 + 512], start=True, stop=False),
                    waits=w2, tag=f"dg2{it}_{ck}_{hw_}")
                m22 = T_(lambda e, hw_=hw_: e.matmul(
                    PW1[:, hw_ * 512:hw_ * 512 + 512], ident[:],
                    mask2[:, 0:512],
                    start=False, stop=True), waits=[m21], tag=f"dg2m{it}_{ck}_{hw_}")
                w2.append(m22)
            x2 = []
            for hh in range(2):
                x2.append(S_(lambda e, hh=hh: e.activation(
                    out=Dexp2[:, hh * 512:hh * 512 + 512],
                    in_=PW1[:, hh * 512:hh * 512 + 512], func=AFT.Exp),
                    waits=[w2[-2 + hh], evD.get(("d2f", ck))] + prev_serial,
                    tag=f"dx2{it}_{ck}_{hh}"))
                bank_read("PW1", x2[-1])
            w3 = [x2] + bank_open("PW1")
            for hw_ in range(2):
                m31 = T_(lambda e, ck=ck, hw_=hw_: e.matmul(
                    PW1[:, hw_ * 512:hw_ * 512 + 512],
                    Gneg[:, ck, :, :],
                    rhs3[:, ck, hw_ * 512:hw_ * 512 + 512], start=True, stop=False),
                    waits=w3, tag=f"dg3{it}_{ck}_{hw_}")
                m32 = T_(lambda e, hw_=hw_: e.matmul(
                    PW1[:, hw_ * 512:hw_ * 512 + 512], ident[:],
                    mask3[:, 0:512],
                    start=False, stop=True), waits=[m31], tag=f"dg3m{it}_{ck}_{hw_}")
                w3.append(m32)
            x3 = []
            for hh in range(2):
                x3.append(S_(lambda e, hh=hh: e.activation(
                    out=Dexp3[:, hh * 512:hh * 512 + 512],
                    in_=PW1[:, hh * 512:hh * 512 + 512], func=AFT.Exp),
                    waits=[w3[-2 + hh], evD.get(("d3f", ck))] + prev_serial,
                    tag=f"dx3{it}_{ck}_{hh}"))
                bank_read("PW1", x3[-1])
            evkk = []
            for s in range(NH):
                par, j = s // 8, s % 8
                for b in range(B):
                    evkk.append(T_(lambda e, ck=ck, s=s, par=par, j=j, b=b: e.matmul(
                        PW2[64 * b:64 * b + 64, s * 64:s * 64 + 64],
                        khat[64 * par:64 * par + 64, j, b, ck * 64:ck * 64 + 64],
                        khat[64 * par:64 * par + 64, j, b, ck * 64:ck * 64 + 64],
                        start=True, stop=True, tile_position=(64 * par, 64 * b)),
                        waits=[ev_pre["khat"], ev_pre["vtm"]]
                        + (bank_open("PW2") if (s == 0 and b == 0) else []),
                        tag=f"kk{it}_{ck}_{s}_{b}"))
            eB = []
            eBT = []
            for hh in range(2):
                eB.append(V_(lambda e, hh=hh: e.scalar_tensor_tensor(
                    out=Bw[:, hh * 512:hh * 512 + 512],
                    in0=PW2[:, hh * 512:hh * 512 + 512], scalar=-1.0,
                    in1=Dexp1[:, hh * 512:hh * 512 + 512],
                    op0=OP.mult, op1=OP.mult),
                    waits=[evkk[15 + 16 * hh], x1[hh], ev_wfree.get("Bw")],
                    tag=f"B{it}_{ck}_{hh}"))
                bank_read("PW2", eB[-1])
            for hh in range(2):
                eBT.append(V_(lambda e, hh=hh: e.scalar_tensor_tensor(
                    out=BTw[:, hh * 512:hh * 512 + 512],
                    in0=PW2[:, hh * 512:hh * 512 + 512], scalar=-1.0,
                    in1=Dexp3[:, hh * 512:hh * 512 + 512],
                    op0=OP.mult, op1=OP.mult),
                    waits=[evkk[15 + 16 * hh], x3[hh], ev_wfree.get("BTw")],
                    tag=f"BT{it}_{ck}_{hh}"))
                bank_read("PW2", eBT[-1])
            evD[("d1f", ck + 1)] = eB
            evD[("d3f", ck + 1)] = eBT
            evqk = []
            for s in range(NH):
                par, j = s // 8, s % 8
                for b in range(B):
                    evqk.append(T_(lambda e, ck=ck, s=s, par=par, j=j, b=b: e.matmul(
                        PW1[64 * b:64 * b + 64, s * 64:s * 64 + 64],
                        khat[64 * par:64 * par + 64, j, b, ck * 64:ck * 64 + 64],
                        qhat[64 * par:64 * par + 64, j, b, ck * 64:ck * 64 + 64],
                        start=True, stop=True, tile_position=(64 * par, 64 * b)),
                        waits=[ev_pre["qhat"], x3]
                        + (bank_open("PW1") if (s == 0 and b == 0) else []),
                        tag=f"qk{it}_{ck}_{s}_{b}"))
            eM = []
            for hh in range(2):
                eM.append(V_(lambda e, hh=hh: e.tensor_tensor(
                    out=Mmw[:, hh * 512:hh * 512 + 512],
                    in0=PW1[:, hh * 512:hh * 512 + 512],
                    in1=Dexp2[:, hh * 512:hh * 512 + 512], op=OP.mult),
                    waits=[evqk[15 + 16 * hh], x2[hh], ev_wfree.get("Mmw")],
                    tag=f"Mm{it}_{ck}_{hh}"))
                bank_read("PW1", eM[-1])
            evD[("d2f", ck + 1)] = eM
            evD[("B", ck)] = eB
            evD[("BT", ck)] = eBT
            evD[("Mm", ck)] = eM
            evD[("pw1f", ck + 1)] = eM
            evD[("pw2f", ck + 1)] = eBT

            evp = []
            for s in range(NH):
                par, j = s // 8, s % 8
                for b in range(B):
                    evp.append(T_(lambda e, s=s, par=par, j=j, b=b, ck=ck: e.matmul(
                        PW1[64 * b:64 * b + 64, s * 64:s * 64 + 64],
                        khat[64 * par:64 * par + 64, j, b, ck * 64:ck * 64 + 64],
                        Sb16[64 * par:64 * par + 64, b, j, :],
                        start=True, stop=True, tile_position=(64 * par, 64 * b)),
                        waits=[ev_state_Sb[b], eM[0 if s < 8 else 1], ev_pre["khat"]]
                        + (bank_open("PW1") if (s == 0 and b == 0) else []),
                        tag=f"pr{it}_{ck}_{s}_{b}"))
            ev_last_pr = evp[-1]
            eu1 = V_(lambda e, ck=ck: e.tensor_tensor(
                out=tscr[:].rearrange("p (h v) -> p h v", h=NH),
                in0=PW1[:, 0:WIDE].rearrange("p (h v) -> p h v", h=NH),
                in1=expG_tm[:, ck, :].unsqueeze(2).broadcast_to([128, NH, 64]),
                op=OP.mult), waits=[evp[-1], evba[("expG", ck)],
                                    ev_wfree.get("oxs")], tag=f"u1{it}_{ck}")
            bank_read("PW1", eu1)
            eu2 = V_(lambda e, ck=ck: e.tensor_tensor(
                out=uh[:, ck, :].rearrange("p (par j d) -> p par j d", par=2, j=8),
                in0=vtm[:, ck, :].rearrange("p (j par d) -> p par j d", j=8, par=2),
                in1=tscr[:].rearrange("p (par j d) -> p par j d", par=2, j=8),
                op=OP.subtract),
                waits=[eu1, ev_pre["vtm"], ev_uh_use], tag=f"u2{it}_{ck}")
            ev_uh_w = eu2

            Msrc = [Bw[:], McT[:], Mx[:], McT[:], Mx[:]]
            Nsrc = [BTw[:], Ncr[:], Nx[:], Ncr[:], Nx[:], Ncr[:]]
            evMw = {0: evD[("B", ck)]}
            evNw = {0: evD[("BT", ck)]}
            ycur, ev_y = uh[:, ck, :], [eu2, eu2]
            ev_pw1_reader = eu1
            ev_pw2_reader = eBT
            for l in range(6):
                evs = []
                for half in range(2):
                    evh = []
                    for b in range(B):
                        for h in range(half * 8, half * 8 + 8):
                            evh.append(T_(lambda e, b=b, h=h, l=l, N=Nsrc[l], yc=ycur: e.matmul(
                                PW1[64 * b:64 * b + 64, h * 64:h * 64 + 64],
                                N[64 * b:64 * b + 64, h * 64:h * 64 + 64],
                                yc[64 * b:64 * b + 64, h * 64:h * 64 + 64],
                                start=True, stop=True, tile_position=(64 * b, 64 * b)),
                                waits=[evNw[l][half], ev_y[half]]
                                + (bank_open("PW1") if (half == 0 and b == 0 and h == 0) else []),
                                tag=f"ap{it}_{ck}_{l}_{b}_{h}"))
                    evs.append(evh)
                ynext = (yv if l % 2 == 0 else yv2)
                ev_yh = []
                for half in range(2):
                    ev_yh.append(V_(lambda e, yn=ynext, yc=ycur, half=half: e.tensor_tensor(
                        out=yn[:, half * 512:half * 512 + 512],
                        in0=PW1[:, half * 512:half * 512 + 512],
                        in1=yc[:, half * 512:half * 512 + 512], op=OP.add),
                        waits=[evs[half][-1], ev_wfree.get("yv")],
                        tag=f"ye{it}_{ck}_{l}_{half}"))
                    bank_read("PW1", ev_yh[-1])
                ycur, ev_y = ynext[:], ev_yh
                if l < 5:
                    evsqN = []
                    for b in range(B):
                        for h in range(NH):
                            evsqN.append(T_(lambda e, b=b, h=h, M=Msrc[l], N=Nsrc[l]: e.matmul(
                                PW2[64 * b:64 * b + 64, h * 64:h * 64 + 64],
                                M[64 * b:64 * b + 64, h * 64:h * 64 + 64],
                                N[64 * b:64 * b + 64, h * 64:h * 64 + 64],
                                start=True, stop=True, tile_position=(64 * b, 64 * b)),
                                waits=[evMw[l][h // 8], evNw[l][h // 8]]
                                + (bank_open("PW2") if (b == 0 and h == 0) else []),
                                tag=f"sqN{it}_{ck}_{l}_{b}_{h}"))
                    eNev = []
                    for hh in range(2):
                        eNev.append(S_(lambda e, dstN=Nsrc[l + 1], hh=hh: e.activation(
                            out=dstN[:, hh * 512:hh * 512 + 512],
                            in_=PW2[:, hh * 512:hh * 512 + 512], func=AFT.Copy),
                            waits=[evsqN[23 if hh == 0 else 31], evNw.get(l - 1)],
                            tag=f"Ne{it}_{ck}_{l}_{hh}"))
                        bank_read("PW2", eNev[-1])
                    evNw[l + 1] = eNev
                    if l < 4:
                        evsqM = []
                        for b in range(B):
                            for h in range(NH):
                                evsqM.append(T_(lambda e, b=b, h=h, M=Msrc[l], N=Nsrc[l]: e.matmul(
                                    PO[64 * b:64 * b + 64, h * 64:h * 64 + 64],
                                    N[64 * b:64 * b + 64, h * 64:h * 64 + 64],
                                    M[64 * b:64 * b + 64, h * 64:h * 64 + 64],
                                    start=True, stop=True, tile_position=(64 * b, 64 * b)),
                                    waits=[evNw[l][h // 8], evMw[l][h // 8]]
                                    + (bank_open("PO") if (b == 0 and h == 0) else []),
                                    tag=f"sqM{it}_{ck}_{l}_{b}_{h}"))
                        eMev = []
                        for hh in range(2):
                            eMev.append(S_(lambda e, dstM=Msrc[l + 1], hh=hh: e.activation(
                                out=dstM[:, hh * 512:hh * 512 + 512],
                                in_=PO[:, hh * 512:hh * 512 + 512], func=AFT.Copy),
                                waits=[evsqM[23 if hh == 0 else 31], evMw.get(l - 1)],
                                tag=f"Me{it}_{ck}_{l}_{hh}"))
                            bank_read("PO", eMev[-1])
                        evMw[l + 1] = eMev
            ev_wfree["yv"] = None
            edv = V_(lambda e, ck=ck, yc=ycur: e.tensor_tensor(
                out=dvb[:, ck, :].rearrange("p (h v) -> p h v", h=NH),
                in0=yc.rearrange("p (h v) -> p h v", h=NH),
                in1=beta_tm[:, ck, :].unsqueeze(2).broadcast_to([128, NH, 64]),
                op=OP.mult), waits=[ev_y[0], ev_y[1], ev_pre["k2tm"],
                                    ev_wfree.get("dvb")],
                tag=f"dv{it}_{ck}")

            evo = []
            for s in range(NH):
                par, j = s // 8, s % 8
                for b in range(B):
                    evo.append(T_(lambda e, s=s, par=par, j=j, b=b, ck=ck: e.matmul(
                        PO[64 * par:64 * par + 64, b * 512 + j * 64:b * 512 + j * 64 + 64],
                        dvb[64 * b:64 * b + 64, ck, s * 64:s * 64 + 64],
                        Mmw[64 * b:64 * b + 64, s * 64:s * 64 + 64],
                        start=True, stop=True, tile_position=(64 * b, 64 * par)),
                        waits=[edv, eM[0 if s < 8 else 1]]
                        + (bank_open("PO") if (s == 0 and b == 0) else []),
                        tag=f"oi{it}_{ck}_{s}_{b}"))
            evox = []
            for s in range(NH):
                par, j = s // 8, s % 8
                for b in range(B):
                    evox.append(T_(lambda e, par=par, j=j, b=b, ck=ck: e.matmul(
                        PW1[64 * par:64 * par + 64, j * 128 + b * 64:j * 128 + b * 64 + 64],
                        Sb16[64 * par:64 * par + 64, b, j, :],
                        qtil[64 * par:64 * par + 64, j, b, ck * 64:ck * 64 + 64],
                        start=True, stop=True, tile_position=(64 * par, 64 * par)),
                        waits=[ev_state_Sb[b], ev_pre["qtil"]]
                        + (bank_open("PW1") if (s == 0 and b == 0) else []),
                        tag=f"ox{it}_{ck}_{s}_{b}"))
            oxs = oxsv
            ev_oxe = S_(lambda e, oxs=oxs: e.activation(
                out=oxs, in_=PW1[:, 0:1024], func=AFT.Copy),
                waits=[evox[-1], eu2, ev_wfree.get("oxs")],
                tag=f"oxe{it}_{ck}")
            bank_read("PW1", ev_oxe)
            ev_po_evac[ck + 1] = V_(lambda e, ck=ck, oxs=oxs: e.tensor_tensor(
                out=qhat[:, :, :, ck * 64:ck * 64 + 64],
                in0=PO[:, 0:1024].rearrange("p (b j t) -> p j b t", b=B, j=NCC),
                in1=oxs.rearrange("p (j b t) -> p j b t", j=NCC, b=B),
                op=OP.add),
                waits=[evo[-1], ev_oxe, ev_wfree.get("obuf")], tag=f"oe{it}_{ck}")
            bank_read("PO", ev_po_evac[ck + 1])
            ev_wfree["oxs"] = ev_po_evac[ck + 1]

            egr = []
            for b in range(B):
                egr.append(dma("gpsimd", "gdr", rowb[0:1, b, :],
                               G_tm[64 * b + 63:64 * b + 64, ck, :],
                               waits=[ecp, ev_wfree.get("rowbfree")],
                               tag=f"rb{it}_{ck}_{b}"))
            eeg = S_(lambda e: e.activation(
                out=rowbb[0:1].rearrange("p par (b j) -> p par b j", b=B),
                in_=rowb[0:1, :, :].rearrange("p b (par j) -> p par b j", par=2),
                func=AFT.Exp),
                waits=[egr[-1], ev_wfree.get("dwfree")], tag=f"egc{it}_{ck}")
            ev_wfree["rowbfree"] = eeg
            evba[("eGCp", ck)] = eeg
            edw = None
            for par in range(2):
                edw = T_(lambda e, par=par: e.matmul(
                    PW1[64 * par:64 * par + 64, 0:B * NCC],
                    onesrow[0:1, 0:64],
                    rowbb[0:1, par, :],
                    start=True, stop=True, tile_position=(0, 64 * par)),
                    waits=[evba[("eGCp", ck)]]
                    + (bank_open("PW1") if par == 0 else []),
                    tag=f"dw{it}_{ck}_{par}")
            ev_wfree["dwfree"] = edw
            es1 = []
            for bb in range(B):
                es1.append(V_(lambda e, bb=bb: e.tensor_tensor(
                    out=S32b[:, bb * 512:bb * 512 + 512].rearrange(
                        "p (a c) -> p a c", a=NCC),
                    in0=S32[:, bb, :, :],
                    in1=PW1[:, bb * NCC:bb * NCC + NCC].unsqueeze(2).broadcast_to(
                        [128, NCC, DH]),
                    op=OP.mult), waits=[edw, ev_state_S, ev_po_evac[ck + 1],
                                        ev_wfree.get("oxs")],
                    tag=f"sd{it}_{ck}_{bb}"))
                bank_read("PW1", es1[-1])
            evst = []
            for s in range(NH):
                par, j = s // 8, s % 8
                hn = 2 * j + par
                for b in range(B):
                    evst.append(T_(lambda e, s=s, par=par, j=j, hn=hn, b=b, ck=ck: e.matmul(
                        PW2[64 * par:64 * par + 64, b * 512 + j * 64:b * 512 + j * 64 + 64],
                        k2tm[64 * b:64 * b + 64, ck, hn * 64:hn * 64 + 64],
                        dvb[64 * b:64 * b + 64, ck, s * 64:s * 64 + 64],
                        start=True, stop=True, tile_position=(64 * b, 64 * par)),
                        waits=[edv, ev_pre["k2tm"]]
                        + (bank_open("PW2") if (s == 0 and b == 0) else []),
                        tag=f"st{it}_{ck}_{s}_{b}"))
            es2b = []
            ev_state_Sb = []
            for bb in range(B):
                es2b.append(V_(lambda e, bb=bb: e.tensor_tensor(
                    out=S32[:, bb, :, :].rearrange("p a c -> p (a c)"),
                    in0=S32b[:, bb * 512:bb * 512 + 512],
                    in1=PW2[:, bb * 512:bb * 512 + 512], op=OP.add),
                    waits=[es1[bb], evst[30 + bb]], tag=f"sa{it}_{ck}_{bb}"))
                bank_read("PW2", es2b[-1])
                ev_state_Sb.append(S_(lambda e, bb=bb: e.activation(
                    out=Sb16[:, bb, :, :].rearrange("p a c -> p (a c)"),
                    in_=S32[:, bb, :, :].rearrange("p a c -> p (a c)"),
                    func=AFT.Copy),
                    waits=[es2b[-1]], tag=f"sb{it}_{ck}_{bb}"))
            ev_state_S = es2b
            ev_pw2_reader = es2b
            evD[("pw2f", ck + 1)] = es2b
            evD[("pw1f", ck + 1)] = ev_pw1_reader
            prev_serial = [ev_y[0], ev_y[1], edv]
        ev_wfree["dvb"] = None
        ev_wfree["Bw"] = ev_y
        ev_wfree["BTw"] = list(ev_y)
        ev_wfree["Mmw"] = ev_po_evac[2]
        ev_wfree["vtm"] = ev_uh_w
        ev_wfree["k2tm"] = ev_state_Sb
        ev_obuf_done = ev_po_evac[2]

        eo2 = V_(lambda e: e.tensor_tensor(
            out=osq[:].rearrange("p a b t -> p (a b t)"),
            in0=qhat[:].rearrange("p a b t -> p (a b t)"),
            in1=qhat[:].rearrange("p a b t -> p (a b t)"), op=OP.mult),
            waits=[ev_obuf_done, ev_state_S], tag=f"o2{it}")
        evmm = None
        for ic in range(NCC):
            evmm = T_(lambda e, ic=ic: e.matmul(
                PW2[0:16, 0:FW], hmask[:, ic, 0:16],
                osq[:, ic, :, :].rearrange("p b t -> p (b t)"),
                start=(ic == 0), stop=(ic == NCC - 1)),
                waits=[eo2] + (bank_open("PW2") if ic == 0 else []),
                tag=f"os{it}{ic}",
                inc="tp" if ic == NCC - 1 else None)
        eol = S_(lambda e: e.activation(
            out=osc[0:16, :], in_=PW2[0:16, 0:FW], func=AFT.Ln,
            bias=epso[0:16, 0:1], scale=1.0 / 64.0), waits=[evmm], tag=f"ol{it}")
        bank_read("PW2", eol)
        eoe = S_(lambda e: e.activation(
            out=osc[0:16, :], in_=osc[0:16, :], func=AFT.Exp, scale=-0.5),
            waits=[eol], tag=f"oe2{it}")
        evon = []
        for ic in range(NCC):
            half = (ic % 2) * 512
            evb = T_(lambda e, ic=ic, half=half: e.matmul(
                PW1[:, half:half + FW], hmaskT[:, ic, :], osc[0:16, :],
                start=True, stop=True),
                waits=[eoe] + bank_open("PW1"),
                tag=f"ob{it}{ic}")
            ev1 = V_(lambda e, ic=ic, half=half: e.tensor_tensor(
                out=osq[:, ic, :, :].rearrange("p b t -> p (b t)"),
                in0=qhat[:, ic, :, :].rearrange("p b t -> p (b t)"),
                in1=PW1[:, half:half + FW], op=OP.mult),
                waits=[evb], tag=f"on{it}{ic}")
            bank_read("PW1", ev1)
            evon.append(V_(lambda e, ic=ic: e.tensor_tensor(
                out=qhat[:, ic, :, :].rearrange("p b t -> p (b t)"),
                in0=osq[:, ic, :, :].rearrange("p b t -> p (b t)"),
                in1=gatebuf[:, ic, :, :].rearrange("p b t -> p (b t)"), op=OP.mult),
                waits=[ev1, ev_gate_done], tag=f"og{it}{ic}"))
        ev_ofinal = evon[-1]

        for oc in range(NCC):
            pocol = ((oc // 2) % 2) * 512 + (oc % 2) * FW
            ex = dma("sync", "sdf0" if oc % 2 == 0 else "sdf1", xfm[:, oc % 2, :].rearrange("p (b t) -> p b t", b=B),
                     x_fm_d[:, oc * 128:(oc + 1) * 128, t0:t0 + TILE].rearrange("b p t -> p b t"),
                     waits=[ev_laststore.get((it - 1, oc)), ev_laststore.get((it, oc - 2))],
                     tag=f"xf{it}{oc}")
            evmm = None
            for kc in range(NCC):
                evmm = T_(lambda e, oc=oc, kc=kc, pocol=pocol: e.matmul(
                    PO[:, pocol:pocol + FW], W["o"][:, kc, oc * 128:(oc + 1) * 128],
                    qhat[:, kc, :, :].rearrange("p b t -> p (b t)"),
                    start=(kc == 0), stop=(kc == NCC - 1)),
                    waits=[ev_ofinal, ev_wcast[("o", NCC - 1)]]
                    + (bank_open("PO") if kc == 0 else []),
                    tag=f"wo{it}{oc}_{kc}", inc="tp" if kc == NCC - 1 else None)
            er1 = V_(lambda e, oc=oc, pocol=pocol: e.scalar_tensor_tensor(
                out=resid[:, oc % 2, :], in0=PO[:, pocol:pocol + FW], scalar=1.0,
                in1=ffng[:, oc, :, :].rearrange("p b t -> p (b t)"),
                op0=OP.mult, op1=OP.mult),
                waits=[evmm, ev_ffn_done, ev_state_Sb, ev_ofinal,
                       ev_laststore.get((it, oc - 2)) or ev_laststore.get((it - 1, oc + 6))],
                tag=f"r1{it}{oc}")
            bank_read("PO", er1)
            er2 = V_(lambda e, oc=oc: e.tensor_tensor(
                out=resid[:, oc % 2, :], in0=resid[:, oc % 2, :],
                in1=xfm[:, oc % 2, :], op=OP.add),
                waits=[er1, ex], tag=f"r2{it}{oc}")
            ev_laststore[(it, oc)] = dma(
                "sync", "sdo0" if oc % 2 == 0 else "sdo1", y_fm_d[:, oc * 128:(oc + 1) * 128, t0:t0 + TILE].rearrange("b p t -> p b t"),
                resid[:, oc % 2, :].rearrange("p (b t) -> p b t", b=B),
                waits=[er2], tag=f"st{it}{oc}")
        ev_wfree["gate"] = er1
        ev_wfree["ffn"] = er1
        ev_wfree["oxs"] = [ev_laststore[(it, NCC - 2)], ev_laststore[(it, NCC - 1)]]
        ev_wfree["k2prod"] = [evo[-1], evst[-1]]
        ev_wfree["qhat"] = evmm
        ev_wfree["obuf"] = evmm
        ev_wfree["qtil"] = ev_po_evac[2]
        ev_Tend = P.last_ev["tensor"]
        ev_wfree["Tend"] = ev_Tend
        ev_wfree["Gs"] = ev_Tend
        ev_wfree["rhs"] = ev_Tend
        ev_wfree["expGfm"] = ev_Tend
        ev_wfree["eGceFM"] = ev_Tend
        ev_wfree["qsc"] = [ev_sc_done["q"][1], ev_sc_done["k"][1]]
        ev_wfree["tmsmalls"] = ev_Tend
        ev_wfree["khat"] = ev_last_pr
        ev_wfree["PO"] = ev_po_evac[2]
        ev_wfree["pw2"] = ev_state_Sb

    def _finalwait(e):
        last = None
        for snm in DMA_SEMS:
            n = P.final_cnt.get(snm, 0)
            if n:
                last = e.wait_ge(sems[snm], n * 16)
        return last
    P.stream = ("F",)
    P.op("sync", _finalwait, tag="final_wait")

    emit_groups = [[("P",), ("L", 0)]]
    for t in range(NT):
        g = [("S", t)]
        if t + 1 < NT:
            g.append(("L", t + 1))
        emit_groups.append(g)
    emit_groups.append([("F",)])

    blk = es.enter_context(nc.Block())
    P.emit(blk, sems, groups=emit_groups)
    es.close()
    return nc



_PROG_CACHE = {}


def _get_prog(T):
    if T not in _PROG_CACHE:
        cfg = CFG(T)
        _PROG_CACHE[T] = (cfg, build_program(cfg))
    return _PROG_CACHE[T]


def make_in_maps(cfg, inputs):
    x = np.ascontiguousarray(inputs["L0_attn_hidden"], dtype=np.float32)
    NBAT = x.shape[0]
    T = x.shape[1]
    consts = host_constants(cfg)

    def t(a):
        return np.ascontiguousarray(np.asarray(a, dtype=np.float32).T)

    shared = {
        "w_qt": t(inputs["L0_attn_Wq"]), "w_kt": t(inputs["L0_attn_Wk"]),
        "w_vt": t(inputs["L0_attn_Wv"]), "w_gt": t(inputs["L0_attn_Wg"]),
        "w_ot": t(inputs["L0_attn_Wo"]), "w_ft": t(inputs["L0_ffn_weight"]),
        "wbat": np.ascontiguousarray(
            np.concatenate([t(inputs["L0_attn_Wb"])[:, SIG],
                            t(inputs["L0_attn_Wa"])[:, SIG]], axis=1)),
        "alog": np.asarray(inputs["L0_attn_A_log"], np.float32)[SIG].reshape(1, -1),
        "dtrep": np.tile(np.asarray(inputs["L0_attn_dt_bias"], np.float32)[SIG][None, :],
                         (128, 1)),
        "normw2": np.tile(np.asarray(inputs["L0_attn_norm_w"], np.float32), 2)[:, None].copy(),
    }
    cw = np.stack([np.asarray(inputs["L0_attn_conv_q"], np.float32),
                   np.asarray(inputs["L0_attn_conv_k"], np.float32),
                   np.asarray(inputs["L0_attn_conv_v"], np.float32)])
    shared["convw"] = np.ascontiguousarray(
        cw.reshape(3, cfg.NCC, 128, 4).transpose(2, 0, 1, 3))
    for nm, a in consts.items():
        shared["c_" + nm] = np.ascontiguousarray(a)

    in_maps = []
    for ci in range(NCORES):
        m = dict(shared)
        xs = x[2 * ci:2 * ci + 2]
        m["x_tm"] = np.ascontiguousarray(xs)
        m["x_fm"] = np.ascontiguousarray(xs.transpose(0, 2, 1))
        in_maps.append(m)
    return in_maps


def kernel(**inputs):
    x = inputs["L0_attn_hidden"]
    NBAT, T, H = x.shape
    cfg, nc = _get_prog(T)
    in_maps = make_in_maps(cfg, inputs)
    res = run_bass_kernel_spmd(nc, in_maps, core_ids=list(range(NCORES)))
    outs = []
    for ci in range(NCORES):
        y = np.asarray(res.results[ci]["y_fm"])
        outs.append(y.transpose(0, 2, 1))
    return np.ascontiguousarray(np.concatenate(outs, axis=0), dtype=np.float32)
